# version 26
# baseline (speedup 1.0000x reference)
import math
import numpy as np
import ml_dtypes
import concourse.bass as bass
import concourse.mybir as mybir
from concourse.bass_utils import run_bass_kernel_spmd

F32 = mybir.dt.float32
BF16 = mybir.dt.bfloat16
AF = mybir.ActivationFunctionType
OP = mybir.AluOpType
AX = mybir.AxisListType

NCORES = 8
N = 4096
D = 1024
DC = 8
HID = 2048
HC = 16
QKD = 128
GSZ = 256
NG = N // GSZ
TA = 512
NT = N // TA
TE = 1024
NTE = N // TE
SUB = 512
EPS = 1e-5
NEXP = 8

PARANOID = False
QPOOL = {"QS": 10, "QA": 8, "QG": 6}
COMPUTE_ENGS = ("T", "V", "S", "P")
QUEUES = {"QS": "sync", "QA": "scalar", "QG": "gpsimd"}
BODY_OF = {"T": "tensor", "V": "vector", "S": "scalar", "P": "gpsimd",
           "QS": "sync", "QA": "scalar", "QG": "gpsimd"}


def bucket1d_np():
    d = np.arange(-255, 256)
    retv = (d < 0).astype(np.int32) * 16
    a = np.abs(d)
    lg = np.log(np.maximum(a, 1).astype(np.float32) / np.float32(8.0))
    vl = 8 + (lg / np.float32(math.log(16.0)) * np.float32(8.0)).astype(np.int32)
    vl = np.minimum(vl, 15)
    return retv + np.where(a < 8, a, vl)


class Emitter:

    def __init__(self):
        self.ops = []

    def op(self, eng, fn, reads=(), writes=(), tag="", serial=False):
        self.ops.append(dict(e=eng, fn=fn, r=tuple(reads), w=tuple(writes),
                             tag=tag, barrier=False, inc=False, serial=serial,
                             waits=None, dep=set(), drain_dep=0, myval=0))

    def barrier(self, tag="bar"):
        self.ops.append(dict(e=None, fn=None, r=(), w=(), tag=tag, barrier=True,
                             inc=False, serial=False, waits=None, dep=set(),
                             drain_dep=0, myval=0))

    def finalize(self, nc, sems):
        ops = self.ops
        last_w = {}
        readers = {}
        last_bar = -1
        last_op_of = {}
        for i, o in enumerate(ops):
            if o["barrier"]:
                for e, j in last_op_of.items():
                    if j > last_bar:
                        if ops[j]["e"] in COMPUTE_ENGS:
                            ops[j]["inc"] = True
                last_bar = i
                continue
            deps = set()
            for k in o["r"]:
                if k in last_w:
                    deps.add(last_w[k])
            for k in o["w"]:
                if k in last_w:
                    deps.add(last_w[k])
                for j in readers.get(k, ()):
                    deps.add(j)
            e = o["e"]
            for d in deps:
                if d <= last_bar:
                    continue
                pe = ops[d]["e"]
                if pe == e:
                    if e == "T":
                        continue
                    if e in QUEUES:
                        ops[d]["inc"] = True
                        o["dep"].add(d)
                    elif PARANOID:
                        ops[d]["inc"] = True
                        o["dep"].add(d)
                        o["drain_dep"] = max(o["drain_dep"], d + 1)
                    else:
                        o["drain_dep"] = max(o["drain_dep"], d + 1)
                elif pe in QUEUES and e in QUEUES and BODY_OF[pe] == BODY_OF[e]:
                    ops[d]["inc"] = True
                    o["dep"].add(d)
                else:
                    ops[d]["inc"] = True
                    o["dep"].add(d)
            for k in o["w"]:
                last_w[k] = i
                readers[k] = []
            for k in o["r"]:
                readers.setdefault(k, []).append(i)
            last_op_of[e] = i
        counters = {s: 0 for s in sems}
        semval = [0] * len(ops)
        qrot = {q: 0 for q in QUEUES}
        for i, o in enumerate(ops):
            if o["barrier"]:
                continue
            e = o["e"]
            if e in QUEUES:
                j = qrot[e] % QPOOL[e]
                qrot[e] += 1
                name = f"{e}.{j}"
                counters[name] += 16
                semval[i] = counters[name]
                o["inc"] = True
                o["myval"] = counters[name]
                o["sem"] = name
                o["guard"] = counters[name] - 16
            elif o["inc"]:
                counters[e] += 1
                semval[i] = counters[e]
                o["sem"] = e
        bodies = sorted(set(BODY_OF.values()))
        wm = {b: {s: 0 for s in sems} for b in bodies}
        totals_hist = []
        cnt2 = {s: 0 for s in sems}
        last_drain_pos = {b: 0 for b in bodies}
        for i, o in enumerate(ops):
            if o["barrier"]:
                o["totals"] = dict(cnt2)
                for b in bodies:
                    for s in sems:
                        wm[b][s] = max(wm[b][s], cnt2[s])
                continue
            e = o["e"]
            b = BODY_OF[e]
            waits = {}
            for d in o["dep"]:
                ps = ops[d].get("sem") or ops[d]["e"]
                v = semval[d]
                if v > wm[b].get(ps, 0):
                    waits[ps] = max(waits.get(ps, 0), v)
            if e in QUEUES and o.get("guard", 0) > 0:
                s = o["sem"]
                if o["guard"] > wm[b].get(s, 0):
                    waits[s] = max(waits.get(s, 0), o["guard"])
            for s, v in waits.items():
                wm[b][s] = v
            o["waits"] = sorted(waits.items())
            o["need_drain"] = (o["drain_dep"] > last_drain_pos[b])
            if o["need_drain"]:
                last_drain_pos[b] = i
            if o["inc"]:
                cnt2[o["sem"]] += 16 if o["e"] in QUEUES else 1
        self.final_totals = dict(cnt2)
        return ops


def build_nc(run_phases=("A", "B", "C", "E"), debug_y=False, debug_a=False):
    nc = bass.Bass("TRN2", target_bir_lowering=False)
    em = Emitter()

    xT = nc.dram_tensor("xT", [D, N], F32, kind="ExternalInput").ap()
    Wh = nc.dram_tensor("Wh", [D, 2 * HID], F32, kind="ExternalInput").ap()
    Wqk = nc.dram_tensor("Wqk", [D, QKD], F32, kind="ExternalInput").ap()
    Wo = nc.dram_tensor("Wo", [HID, D], F32, kind="ExternalInput").ap()
    bh = nc.dram_tensor("bh", [2 * HID], F32, kind="ExternalInput").ap()
    bhp = nc.dram_tensor("bhp", [128, 32], F32, kind="ExternalInput").ap()
    bqk = nc.dram_tensor("bqk", [QKD], F32, kind="ExternalInput").ap()
    ngv = nc.dram_tensor("ngv", [D], F32, kind="ExternalInput").ap()
    nbv = nc.dram_tensor("nbv", [D], F32, kind="ExternalInput").ap()
    bo = nc.dram_tensor("bo", [D], F32, kind="ExternalInput").ap()
    qgT = nc.dram_tensor("qgT", [128, 4], F32, kind="ExternalInput").ap()
    qbT = nc.dram_tensor("qbT", [128, 4], F32, kind="ExternalInput").ap()
    qg3 = nc.dram_tensor("qg3", [1, 128], F32, kind="ExternalInput").ap()
    qb3 = nc.dram_tensor("qb3", [1, 128], F32, kind="ExternalInput").ap()
    relE = nc.dram_tensor("relE", [32, 1], F32, kind="ExternalInput").ap()
    ohC = nc.dram_tensor("ohC", [32, 512], F32, kind="ExternalInput").ap()
    antiI = nc.dram_tensor("antiI", [128, 128], F32, kind="ExternalInput").ap()
    perm8 = nc.dram_tensor("perm8", [8, 24], F32, kind="ExternalInput").ap()
    nbp = nc.dram_tensor("nbp", [128, 8], F32, kind="ExternalInput").ap()
    bop = nc.dram_tensor("bop", [128, 8], F32, kind="ExternalInput").ap()
    gwT = nc.dram_tensor("gwT", [D, 8], F32, kind="ExternalInput").ap()
    w1 = nc.dram_tensor("w1", [NEXP, D, D], F32, kind="ExternalInput").ap()
    w2 = nc.dram_tensor("w2", [NEXP, D, D], F32, kind="ExternalInput").ap()
    w3 = nc.dram_tensor("w3", [NEXP, D, D], F32, kind="ExternalInput").ap()
    w1s = nc.dram_tensor("w1s", [D, D], F32, kind="ExternalInput").ap()
    w2s = nc.dram_tensor("w2s", [D, D], F32, kind="ExternalInput").ap()
    w3s = nc.dram_tensor("w3s", [D, D], F32, kind="ExternalInput").ap()

    oh8 = nc.dram_tensor("oh8", [8, 1024], F32, kind="ExternalInput").ap()
    outT = nc.dram_tensor("outT", [D, N], F32, kind="ExternalOutput").ap()
    dbg_a = nc.dram_tensor("dbg_a", [8, N], F32, kind="ExternalOutput").ap() if debug_a else None
    dbg_y = nc.dram_tensor("dbg_y", [DC, 128, N], F32, kind="ExternalOutput").ap() if debug_a else None

    vtm_d = nc.dram_tensor("vtm_d", [NT, 128, 4, HID], BF16, kind="Internal").ap()
    gate_d = nc.dram_tensor("gate_d", [NT, HC, 128, TA], BF16, kind="Internal").ap()
    og_d = nc.dram_tensor("og_d", [HC, 128, N], BF16, kind="Internal").ap()
    y_d = nc.dram_tensor("y_d", [DC, 128, N], F32, kind="Internal").ap()
    z_d = nc.dram_tensor("z_d", [DC, 128, N], BF16, kind="Internal").ap()
    a_d = nc.dram_tensor("a_d", [8, N], F32, kind="Internal").ap()
    grow_d = nc.dram_tensor("grow_d", [512], F32, kind="Internal").ap()

    xTr = xT.rearrange("(dc p) n -> p dc n", p=128)
    outTr = outT.rearrange("(dc p) n -> p dc n", p=128)

    from contextlib import ExitStack
    ctx = ExitStack()
    E = ctx.enter_context

    sems = {}
    for s in ("T", "V", "S", "P"):
        sems[s] = E(nc.semaphore("sem_" + s))
    for q, np_ in QPOOL.items():
        for j in range(np_):
            sems[f"{q}.{j}"] = E(nc.semaphore(f"sem_{q}_{j}"))

    PS = [E(nc.psum_tensor(f"PS{i}", [128, 512], F32)) for i in range(8)]
    arF = E(nc.sbuf_tensor("arF", [1, 5120], F32))
    arB = E(nc.sbuf_tensor("arB", [1, 4096], BF16))
    zcol = E(nc.sbuf_tensor("zcol", [128, 1], F32))
    epsr = E(nc.sbuf_tensor("epsr", [1, 1], F32))
    zrow = E(nc.sbuf_tensor("zrow", [1, 1], F32))
    ones_cf = E(nc.sbuf_tensor("ones_cf", [128, 1], F32))
    ones_cb = E(nc.sbuf_tensor("ones_cb", [128, 1], BF16))
    ones_rb = E(nc.sbuf_tensor("ones_rb", [1, 128], BF16))

    ctxABC = ExitStack()
    EABC = ctxABC.enter_context
    ng_row = EABC(nc.sbuf_tensor("ng_row", [1, D], BF16))
    bqk_rowb = EABC(nc.sbuf_tensor("bqk_rowb", [1, 128], BF16))
    bqk_col = EABC(nc.sbuf_tensor("bqk_col", [128, 1], F32))
    ones8f = EABC(nc.sbuf_tensor("ones8f", [1, 8], F32))
    o8col = EABC(nc.sbuf_tensor("o8col", [8, 1], F32))
    c4096 = EABC(nc.sbuf_tensor("c4096", [1, 128], F32))
    qg_pt = EABC(nc.sbuf_tensor("qg_pt", [128, 4], F32))
    qb_pt = EABC(nc.sbuf_tensor("qb_pt", [128, 4], F32))
    qgs = EABC(nc.sbuf_tensor("qgs", [128, 4], F32))
    qbs = EABC(nc.sbuf_tensor("qbs", [128, 4], F32))
    g3bc = EABC(nc.sbuf_tensor("g3bc", [128, 128], BF16))
    b3bc = EABC(nc.sbuf_tensor("b3bc", [128, 128], BF16))
    g3row = EABC(nc.sbuf_tensor("g3row", [1, 128], F32))
    b3row = EABC(nc.sbuf_tensor("b3row", [1, 128], F32))
    antiI_sb = EABC(nc.sbuf_tensor("antiI_sb", [128, 128], F32))
    H0 = EABC(nc.sbuf_tensor("H0", [128, 256], F32))
    H1 = EABC(nc.sbuf_tensor("H1", [128, 256], F32))
    gwT_sb = EABC(nc.sbuf_tensor("gwT_sb", [128, 8, 8], F32))
    srow = EABC(nc.sbuf_tensor("srow", [1, 8], F32))
    bhp_sb = EABC(nc.sbuf_tensor("bhp_sb", [128, 32], F32))
    perm8_sb = EABC(nc.sbuf_tensor("perm8_sb", [8, 24], F32))
    nbp_sb = EABC(nc.sbuf_tensor("nbp_sb", [128, 8], F32))
    bop_sb = EABC(nc.sbuf_tensor("bop_sb", [128, 8], F32))

    _ROW_SLOT = {
        "m1": 0, "vv": 1, "var": 2, "r1": 3, "v2": 4, "r2": 5, "Rr": 6, "Pp": 7, "tmp": 8,
        "m3": 0, "v3": 1, "t3r": 2, "r3": 3, "q3": 4, "mf": 5, "vf": 6, "tfr": 7,
        "rf": 8, "qf": 9, "nmf": 2, "sumr": 0, "rsum": 1, "mx1": 4, "mx2": 7,
        "m4": 0, "v4": 1, "t4r": 2, "r4": 3, "q4": 4,
    }
    def rowF(name, n=512):
        s = _ROW_SLOT[name]
        return arF[0:1, s * 512:s * 512 + n]
    _BROW_SLOT = {"r3b": 0, "q3b": 1, "rfb": 2, "qfb": 3, "r4b": 0, "q4b": 1,
                  "Rbs0": 4, "Pbs0": 5, "Rbs1": 6, "Pbs1": 7}
    def rowB(name, n=512):
        s = _BROW_SLOT[name]
        return arB[0:1, s * 512:s * 512 + n]

    o = em.op

    def dma(q, out, in_, reads, writes, tag="", serial=False, **kw):
        o(q, (lambda eng, out=out, in_=in_, kw=kw: eng.dma_start(out=out, in_=in_, **kw)),
          reads, writes, tag, serial=serial)

    o("V", lambda e: e.memset(zcol[:], 0.0), (), [("zcol",)])
    o("V", lambda e: e.memset(epsr[:], EPS), (), [("epsr",)])
    o("V", lambda e: e.memset(zrow[:], 0.0), (), [("zrow",)])
    o("V", lambda e: e.memset(ones_cf[:], 1.0), (), [("ones_cf",)])
    o("V", lambda e: e.memset(ones_cb[:], 1.0), (), [("ones_cb",)])
    o("V", lambda e: e.memset(ones_rb[:], 1.0), (), [("ones_rb",)])
    o("V", lambda e: e.memset(ones8f[:], 1.0), (), [("ones8f",)])
    o("V", lambda e: e.memset(o8col[:], 1.0), (), [("o8col",)])
    o("V", lambda e: e.memset(c4096[:], 1.0 / N), (), [("c4096",)])
    em.barrier("init")

    dma("QS", bhp_sb[:], bhp, [("d_in",)], [("bhp_sb",)])
    dma("QS", perm8_sb[:], perm8, [("d_in",)], [("perm8",)])
    dma("QS", nbp_sb[:], nbp, [("d_in",)], [("nbp_sb",)])
    dma("QS", bop_sb[:], bop, [("d_in",)], [("bop_sb",)])
    dma("QS", qg_pt[:], qgT, [("d_in",)], [("qg_pt",)])
    dma("QS", qb_pt[:], qbT, [("d_in",)], [("qb_pt",)])
    dma("QS", g3row[:], qg3, [("d_in",)], [("g3row",)])
    dma("QS", b3row[:], qb3, [("d_in",)], [("b3row",)])
    dma("QS", antiI_sb[:], antiI, [("d_in",)], [("antiI_sb",)])
    dma("QS", gwT_sb[:], gwT.rearrange("(kc p) e -> p kc e", p=128), [("d_in",)], [("gwT_sb",)])
    o("QS", lambda e: e.dma_start(out=bqk_col[:], in_=bqk.rearrange("(p u) -> p u", u=1),
                                  allow_slow_non_contiguous=True),
      [("d_in",)], [("bqk_col",)])
    dma("QG", ng_row[:], ngv.rearrange("(a n) -> a n", a=1), [("d_in",)], [("ng_row",)])
    dma("QG", bqk_rowb[:], bqk.rearrange("(a n) -> a n", a=1), [("d_in",)], [("bqk_rowb",)])

    for h, sc in ((0, 1.0 / GSZ), (1, 1.0), (2, 1.0)):
        o("V", lambda e, h=h, sc=sc: e.tensor_scalar_mul(qgs[:, h:h + 1], qg_pt[:, h:h + 1], sc),
          [("qg_pt",)], [("qgs",)])
        o("V", lambda e, h=h, sc=sc: e.tensor_scalar_mul(qbs[:, h:h + 1], qb_pt[:, h:h + 1], sc),
          [("qb_pt",)], [("qbs",)])
    o("T", lambda e: e.matmul(PS[0][:, 0:128], c4096[:], g3row[:], start=True, stop=True),
      [("c4096",), ("g3row",)], [("PS", 0)])
    o("V", lambda e: e.tensor_copy(g3bc[:], PS[0][:, 0:128]), [("PS", 0)], [("g3bc",)])
    o("T", lambda e: e.matmul(PS[1][:, 0:128], c4096[:], b3row[:], start=True, stop=True),
      [("c4096",), ("b3row",)], [("PS", 1)])
    o("V", lambda e: e.tensor_copy(b3bc[:], PS[1][:, 0:128]), [("PS", 1)], [("b3bc",)])
    for kc in range(8):
        o("T", lambda e, kc=kc: e.matmul(PS[3][0:1, 0:8], ones_cf[:], gwT_sb[:, kc, :],
                                         start=(kc == 0), stop=(kc == 7)),
          [("ones_cf",), ("gwT_sb",)], [("PS", 3)])
    o("V", lambda e: e.tensor_copy(srow[:], PS[3][0:1, 0:8]), [("PS", 3)], [("srow",)])

    sqfm = EABC(nc.sbuf_tensor("sqfm", [128, N], BF16))
    linkv = EABC(nc.sbuf_tensor("linkv", [128, HID], F32))
    linkvb = EABC(nc.sbuf_tensor("linkvb", [128, HID], BF16))
    r1b = EABC(nc.sbuf_tensor("r1b", [1, N], BF16))
    p1b = EABC(nc.sbuf_tensor("p1b", [1, N], BF16))

    ctxA = ExitStack()
    EA = ctxA.enter_context
    Wh_sb = EA(nc.sbuf_tensor("Wh_sb", [128, 8, 2 * HID], BF16))
    bh_row = EA(nc.sbuf_tensor("bh_row", [1, HID], BF16))
    ohC_sb = EA(nc.sbuf_tensor("ohC_sb", [32, 512], F32))
    rel_sb = EA(nc.sbuf_tensor("rel_sb", [32, 1], F32))
    grow_sb = EA(nc.sbuf_tensor("grow_sb", [1, 512], F32))
    Wqk_sb = EA(nc.sbuf_tensor("Wqk_sb", [128, 8, QKD], BF16))
    xt2 = [EA(nc.sbuf_tensor(f"xt2_{i}", [128, DC, TA], BF16)) for i in range(2)]
    normed2 = [EA(nc.sbuf_tensor(f"normed2_{i}", [128, DC, TA], BF16)) for i in range(2)]
    tnorm = [EA(nc.sbuf_tensor(f"tnorm{i}", [128, TA], F32)) for i in range(2)]
    sqtm = EA(nc.sbuf_tensor("sqtm", [128, 4, QKD], BF16))
    lktm = EA(nc.sbuf_tensor("lktm", [128, 4, QKD], BF16))
    lkt2 = EA(nc.sbuf_tensor("lkt2", [128, QKD], BF16))
    vtmb = [EA(nc.sbuf_tensor(f"vtmb{i}", [128, HID], BF16)) for i in range(2)]
    gateb = [EA(nc.sbuf_tensor(f"gateb{i}", [128, 2, TA], BF16)) for i in range(2)]

    dma("QG", bh_row[:], bh[0:HID].rearrange("(a n) -> a n", a=1), [("d_in",)], [("bh_row",)])
    dma("QS", ohC_sb[:], ohC, [("d_in",)], [("ohC_sb",)])
    o("QS", lambda e: e.dma_start(out=rel_sb[:], in_=relE, allow_slow_non_contiguous=True),
      [("d_in",)], [("rel_sb",)])
    o("T", lambda e: e.matmul(PS[2][0:1, :], rel_sb[:], ohC_sb[:], start=True, stop=True),
      [("rel_sb",), ("ohC_sb",)], [("PS", 2)])
    o("V", lambda e: e.tensor_scalar_mul(grow_sb[:], PS[2][0:1, :], math.sqrt(128.0)),
      [("PS", 2)], [("grow_sb",)])
    dma("QS", grow_d[0:512], grow_sb[:], [("grow_sb",)], [("grow_d",)])
    gsk0 = bass.AP(tensor=grow_d.tensor, offset=128, ap=[[1, 128], [1, 256]])
    gsk1 = bass.AP(tensor=grow_d.tensor, offset=0, ap=[[1, 128], [1, 256]])
    dma("QS", H0[:], gsk0, [("grow_d",)], [("H0",)])
    dma("QS", H1[:], gsk1, [("grow_d",)], [("H1",)])

    dma("QG", Wh_sb[:].rearrange("p kc (h m) -> p kc h m", m=2048),
        Wh.rearrange("(kc p) (h m) -> p kc h m", p=128, m=2048),
        [("d_in",)], [("Wh_sb",)])
    dma("QG", Wqk_sb[:], Wqk.rearrange("(kc p) m -> p kc m", p=128),
        [("d_in",)], [("Wqk_sb",)])

    def a_stats(t):
        par = t % 2
        xa = xt2[par]
        tok = t * TA
        dma("QG", xa[:], xTr[:, :, tok:tok + TA], [("d_in",)], [("xt", par)], tag=f"Ax{t}")
        xsq = normed2[par][:, 0, :]
        for dc in range(DC):
            o("S", lambda e, dc=dc, xa=xa, xsq=xsq: e.activation(out=xsq, in_=xa[:, dc, :], func=AF.Square, bias=zcol[:]),
              [("xt", par)], [("nsq", par)], f"A0sq{t}.{dc}")
            o("T", lambda e, dc=dc, xa=xa: e.matmul(PS[6][0:1, :], ones_cb[:], xa[:, dc, :],
                                                    start=(dc == 0), stop=(dc == DC - 1)),
              [("ones_cb",), ("xt", par)], [("PS", 6)], f"A0sx{t}.{dc}")
            o("T", lambda e, dc=dc, xsq=xsq: e.matmul(PS[7][0:1, :], ones_cb[:], xsq,
                                             start=(dc == 0), stop=(dc == DC - 1)),
              [("ones_cb",), ("nsq", par)], [("PS", 7)], f"A0sxx{t}.{dc}")

    def a_rows(t):
        par = t % 2
        tok = t * TA
        m1 = rowF("m1"); vv = rowF("vv"); var = rowF("var"); r1 = rowF("r1")
        v2 = rowF("v2"); r2 = rowF("r2"); Rr = rowF("Rr"); Pp = rowF("Pp")
        tmp = rowF("tmp")
        o("V", lambda e: e.tensor_scalar_mul(m1, PS[6][0:1, :], 1.0 / D), [("PS", 6)], [("R0",)])
        o("V", lambda e: e.tensor_scalar_mul(vv, PS[7][0:1, :], 1.0 / D), [("PS", 7)], [("R1",)])
        o("V", lambda e: e.tensor_tensor(out=tmp, in0=m1, in1=m1, op=OP.mult),
          [("R0",)], [("R8",)])
        o("V", lambda e: e.tensor_tensor(out=var, in0=vv, in1=tmp, op=OP.subtract),
          [("R1",), ("R8",)], [("R2",)])
        o("S", lambda e: e.activation(out=tmp, in_=var, func=AF.Ln, bias=epsr[:]),
          [("R2",)], [("R8",)])
        o("S", lambda e: e.activation(out=r1, in_=tmp, func=AF.Exp, scale=-0.5, bias=zrow[:]),
          [("R8",)], [("R3",)])
        o("V", lambda e: e.tensor_tensor(out=v2, in0=var, in1=r1, op=OP.mult),
          [("R2",), ("R3",)], [("R4",)])
        o("V", lambda e: e.tensor_tensor(out=v2, in0=v2, in1=r1, op=OP.mult),
          [("R4",), ("R3",)], [("R4",)])
        o("S", lambda e: e.activation(out=tmp, in_=v2, func=AF.Ln, bias=epsr[:]),
          [("R4",)], [("R8",)])
        o("S", lambda e: e.activation(out=r2, in_=tmp, func=AF.Exp, scale=-0.5, bias=zrow[:]),
          [("R8",)], [("R5",)])
        o("V", lambda e: e.tensor_tensor(out=Rr, in0=r1, in1=r2, op=OP.mult),
          [("R3",), ("R5",)], [("R6",)])
        o("V", lambda e: e.tensor_tensor(out=Pp, in0=m1, in1=Rr, op=OP.mult),
          [("R0",), ("R6",)], [("R7",)])
        Rbs = rowB(f"Rbs{par}"); Pbs = rowB(f"Pbs{par}")
        o("V", lambda e, Rbs=Rbs: e.tensor_copy(Rbs, Rr), [("R6",)], [("RBR", par)])
        o("V", lambda e, Pbs=Pbs: e.tensor_copy(Pbs, Pp), [("R7",)], [("RBP", par)])
        o("V", lambda e, tok=tok: e.tensor_copy(r1b[0:1, tok:tok + TA], r1), [("R3",)], [("r1b", t)])
        o("V", lambda e: e.tensor_tensor(out=tmp, in0=m1, in1=r1, op=OP.mult),
          [("R0",), ("R3",)], [("R8",)])
        o("V", lambda e, tok=tok: e.tensor_copy(p1b[0:1, tok:tok + TA], tmp), [("R8",)], [("p1b", t)])

    def a_normed(t):
        par = t % 2
        xa = xt2[par]
        normed = normed2[par]
        Rbs = rowB(f"Rbs{par}"); Pbs = rowB(f"Pbs{par}")
        for dc in range(DC):
            pa, pb = (0, 1) if dc % 2 == 0 else (2, 3)
            gsl = ng_row[0:1, dc * 128:(dc + 1) * 128]
            o("T", lambda e, pa=pa, gsl=gsl, Rbs=Rbs: e.matmul(
                PS[pa][:], gsl, Rbs, start=True, stop=True),
              [("ng_row",), ("RBR", par)], [("PS", pa)], f"Aba{t}.{dc}")
            o("T", lambda e, pb=pb, gsl=gsl, Pbs=Pbs: e.matmul(
                PS[pb][:], gsl, Pbs, start=True, stop=True),
              [("ng_row",), ("RBP", par)], [("PS", pb)], f"Abb1{t}.{dc}")
            sc = tnorm[dc % 2]
            o("V", lambda e, dc=dc, pa=pa, sc=sc, xa=xa: e.tensor_tensor(
                out=sc[:], in0=xa[:, dc, :], in1=PS[pa][:], op=OP.mult),
              [("xt", par), ("PS", pa)], [("tn", dc % 2)], f"An1{t}.{dc}")
            o("V", lambda e, dc=dc, pb=pb, sc=sc, normed=normed: e.scalar_tensor_tensor(
                out=normed[:, dc, :], in0=sc[:], scalar=nbp_sb[:, dc:dc + 1],
                in1=PS[pb][:], op0=OP.add, op1=OP.subtract),
              [("tn", dc % 2), ("PS", pb), ("nbp_sb",)],
              [("normed", par, dc), ("nsq", par)], f"An2{t}.{dc}")

    def a_proj(t):
        par = t % 2
        normed = normed2[par]
        tok = t * TA
        vg_ps = [0]
        def nxt_ps():
            vg_ps[0] = (vg_ps[0] + 1) % 4
            return 4 + vg_ps[0]
        for kc in range(DC):
            o("T", lambda e, kc=kc: e.matmul(PS[4][:], Wqk_sb[:, kc, :],
                                             normed[:, kc, :], start=(kc == 0), stop=(kc == DC - 1)),
              [("Wqk_sb",), ("normed", par, kc)], [("PS", 4)], f"Aqk{t}.{kc}")
        o("S", lambda e, tok=tok: e.activation(out=sqfm[:, tok:tok + TA], in_=PS[4][:],
                                               func=AF.Silu, bias=bqk_col[:]),
          [("PS", 4), ("bqk_col",)], [("sqfm", t)], f"Aqke{t}")
        for tc in range(4):
            tsl = slice(tc * 128, (tc + 1) * 128)
            for kc in range(DC):
                o("T", lambda e, kc=kc, tsl=tsl: e.matmul(
                    PS[5][:, 0:QKD], normed[:, kc, tsl], Wqk_sb[:, kc, :],
                    start=(kc == 0), stop=False),
                  [("normed", par, kc), ("Wqk_sb",)], [("PS", 5)], f"Aqt{t}.{tc}.{kc}")
            o("T", lambda e: e.matmul(PS[5][:, 0:QKD], ones_rb[:], bqk_rowb[:],
                                      start=False, stop=True),
              [("ones_rb",), ("bqk_rowb",)], [("PS", 5)], f"Aqtb{t}.{tc}")
            o("S", lambda e, tc=tc: e.activation(out=sqtm[:, tc, :], in_=PS[5][:, 0:QKD],
                                                 func=AF.Silu, bias=zcol[:]),
              [("PS", 5)], [("sqtm", tc)], f"Aqte{t}.{tc}")
            o("V", lambda e, tc=tc: e.tensor_tensor(out=lkt2[:], in0=sqtm[:, tc, :],
                                                    in1=g3bc[:], op=OP.mult),
              [("sqtm", tc), ("g3bc",)], [("lkt2",)], f"Alk1{t}.{tc}")
            o("V", lambda e, tc=tc: e.tensor_tensor(out=lktm[:, tc, :], in0=lkt2[:],
                                                    in1=b3bc[:], op=OP.add),
              [("lkt2",), ("b3bc",)], [("lktm", tc)], f"Alk2{t}.{tc}")
            par2 = (t * 4 + tc) % 2
            vb = vtmb[par2]
            for Ncc in range(4):
                ps = nxt_ps()
                for kc in range(DC):
                    o("T", lambda e, kc=kc, tsl=tsl, Ncc=Ncc, ps=ps: e.matmul(
                        PS[ps][:], normed[:, kc, tsl], Wh_sb[:, kc, Ncc * 512:(Ncc + 1) * 512],
                        start=(kc == 0), stop=False),
                      [("normed", par, kc), ("Wh_sb",)], [("PS", ps)], f"Av{t}.{tc}.{Ncc}.{kc}")
                o("T", lambda e, Ncc=Ncc, ps=ps: e.matmul(
                    PS[ps][:], ones_rb[:], bh_row[0:1, Ncc * 512:(Ncc + 1) * 512],
                    start=False, stop=True),
                  [("ones_rb",), ("bh_row",)], [("PS", ps)], f"Avb{t}.{tc}.{Ncc}")
                o("S", lambda e, Ncc=Ncc, ps=ps, vb=vb: e.activation(
                    out=vb[:, Ncc * 512:(Ncc + 1) * 512], in_=PS[ps][:], func=AF.Silu, bias=zcol[:]),
                  [("PS", ps)], [("vtmb", par2, Ncc)], f"Ave{t}.{tc}.{Ncc}")
            for Ncc in range(4):
                o("T", lambda e, tc=tc, Ncc=Ncc, vb=vb: e.matmul(
                    PS[Ncc][:], lktm[:, tc, :], vb[:, Ncc * 512:(Ncc + 1) * 512],
                    start=(tc == 0), stop=(tc == 3)),
                  [("lktm", tc), ("vtmb", par2, Ncc)], [("PS", Ncc)], f"Alkv{t}.{tc}.{Ncc}")
            dma("QS", vtm_d[t, :, tc, :], vb[:], [("vtmb", par2, 0), ("vtmb", par2, 1),
                ("vtmb", par2, 2), ("vtmb", par2, 3)], [("vtm_d", t, tc)], tag=f"Avs{t}.{tc}")
        for Ncc in range(4):
            if t == 0:
                o("V", lambda e, Ncc=Ncc: e.tensor_copy(linkv[:, Ncc * 512:(Ncc + 1) * 512], PS[Ncc][:]),
                  [("PS", Ncc)], [("linkv", Ncc)], f"Alkc{t}.{Ncc}")
            else:
                o("V", lambda e, Ncc=Ncc: e.tensor_tensor(
                    out=linkv[:, Ncc * 512:(Ncc + 1) * 512],
                    in0=linkv[:, Ncc * 512:(Ncc + 1) * 512], in1=PS[Ncc][:], op=OP.add),
                  [("PS", Ncc), ("linkv", Ncc)], [("linkv", Ncc)], f"Alka{t}.{Ncc}")
        def gates():
          for hc in range(HC):
            ps = nxt_ps()
            gb = gateb[(hc // 2) % 2]
            for kc in range(DC):
                o("T", lambda e, kc=kc, hc=hc, ps=ps: e.matmul(
                    PS[ps][:], Wh_sb[:, kc, HID + hc * 128:HID + (hc + 1) * 128],
                    normed[:, kc, :], start=(kc == 0), stop=(kc == DC - 1)),
                  [("Wh_sb",), ("normed", par, kc)], [("PS", ps)], f"Ag{t}.{hc}.{kc}")
            o("S", lambda e, hc=hc, ps=ps, gb=gb: e.activation(
                out=gb[:, hc % 2, :], in_=PS[ps][:], func=AF.Silu,
                bias=bhp_sb[:, 16 + hc:17 + hc]),
              [("PS", ps), ("bhp_sb",)], [("gateb", (hc // 2) % 2, hc % 2)], f"Age{t}.{hc}")
            if hc % 2 == 1:
                h2 = hc // 2
                gb2 = gateb[h2 % 2]
                dma("QS", gate_d[t].rearrange("hc p n -> p hc n")[:, h2 * 2:(h2 + 1) * 2, :],
                    gb2[:], [("gateb", h2 % 2, 0), ("gateb", h2 % 2, 1)],
                    [("gate_d", t, h2)], tag=f"Ags{t}.{h2}")
        return gates

    if "A" in run_phases:
        a_stats(0)
        a_rows(0)
        a_normed(0)
        for t in range(NT):
            if t + 1 < NT:
                a_stats(t + 1)
                a_rows(t + 1)
            gates = a_proj(t)
            if t + 1 < NT:
                a_normed(t + 1)
            gates()
        for Ncc in range(4):
            o("V", lambda e, Ncc=Ncc: e.tensor_copy(linkvb[:, Ncc * 512:(Ncc + 1) * 512],
                                                    linkv[:, Ncc * 512:(Ncc + 1) * 512]),
              [("linkv", Ncc)], [("linkvb",)], f"Alkb{Ncc}")

    em.barrier("A|B")
    ctxA.close()

    ctxB = ExitStack()
    EB = ctxB.enter_context
    vtmg = [EB(nc.sbuf_tensor(f"vtmg{i}", [128, 2, HID], BF16)) for i in range(2)]
    gateg = [EB(nc.sbuf_tensor(f"gateg{i}", [128, HC, 256], BF16)) for i in range(2)]
    hq = [EB(nc.sbuf_tensor(f"hq{i}", [128, 256], BF16)) for i in range(2)]
    hk = [EB(nc.sbuf_tensor(f"hk{i}", [128, 256], BF16)) for i in range(2)]
    hlq = [EB(nc.sbuf_tensor(f"hlq{i}", [128, 256], BF16)) for i in range(2)]
    rsc = [EB(nc.sbuf_tensor(f"rsc{i}", [128, 256], F32)) for i in range(2)]
    attnT = [EB(nc.sbuf_tensor(f"attnT{i}", [128, 256], BF16)) for i in range(4)]
    ogb = [EB(nc.sbuf_tensor(f"ogb{i}", [128, HC, 256], BF16)) for i in range(2)]

    def b_pre(g):
            t, half = g // 2, g % 2
            tok = g * GSZ
            par = g % 2
            vg = vtmg[par]; gg = gateg[par]
            dma("QA", vg[:], vtm_d[t, :, 2 * half:2 * half + 2, :],
                [("vtm_d", t, 2 * half), ("vtm_d", t, 2 * half + 1)], [("vtmg", par)], tag=f"Bv{g}")
            dma("QA", gg[:], gate_d[t].rearrange("hc p n -> p hc n")[:, :, half * 256:half * 256 + 256],
                [("gate_d", t, h8) for h8 in range(8)],
                [("gateg", par)], tag=f"Bg{g}")
            sq = sqfm[:, tok:tok + 256]
            o("V", lambda e, sq=sq, par=par: e.tensor_scalar(
                out=hq[par][:], in0=sq, scalar1=qgs[:, 0:1], scalar2=qbs[:, 0:1],
                op0=OP.mult, op1=OP.add),
              [("sqfm", t), ("qgs",), ("qbs",)], [("hq", par)], f"Bhq{g}")
            o("V", lambda e, sq=sq, par=par: e.tensor_scalar(
                out=hlq[par][:], in0=sq, scalar1=qgs[:, 1:2], scalar2=qbs[:, 1:2],
                op0=OP.mult, op1=OP.add),
              [("sqfm", t), ("qgs",), ("qbs",)], [("hlq", par)], f"Bhlq{g}")
            o("V", lambda e, sq=sq, par=par: e.tensor_scalar(
                out=hk[par][:], in0=sq, scalar1=qgs[:, 2:3], scalar2=qbs[:, 2:3],
                op0=OP.mult, op1=OP.add),
              [("sqfm", t), ("qgs",), ("qbs",)], [("hk", par)], f"Bhk{g}")
            for jc in range(2):
                ps = jc
                Hj = H0 if jc == 0 else H1
                o("T", lambda e, ps=ps, par=par, jc=jc: e.matmul(
                    PS[ps][:, 0:256], hk[par][:, jc * 128:(jc + 1) * 128], hq[par][:],
                    start=True, stop=False),
                  [("hk", par), ("hq", par)], [("PS", ps)], f"Bsim{g}.{jc}")
                o("T", lambda e, ps=ps, Hj=Hj: e.matmul(
                    PS[ps][:, 0:256], antiI_sb[:], Hj[:], start=False, stop=True),
                  [("antiI_sb",), ("H0",), ("H1",)], [("PS", ps)], f"Bbias{g}.{jc}")
                o("S", lambda e, ps=ps, jc=jc: e.activation(out=rsc[jc][:], in_=PS[ps][:, 0:256],
                                                            func=AF.Relu, bias=zcol[:]),
                  [("PS", ps)], [("rsc", jc)], f"Brelu{g}.{jc}")
                at = attnT[par * 2 + jc]
                o("V", lambda e, jc=jc, at=at: e.tensor_tensor(out=at[:], in0=rsc[jc][:],
                                                               in1=rsc[jc][:], op=OP.mult),
                  [("rsc", jc)], [("attnT", par * 2 + jc)], f"Bsq{g}.{jc}")
    def b_qolo(g):
            t, half = g // 2, g % 2
            tok = g * GSZ
            par = g % 2
            ob = ogb[par]
            for ec in range(HC):
                ps = 2 + ec % 4
                o("T", lambda e, ec=ec, ps=ps, par=par: e.matmul(
                    PS[ps][:, 0:256], vtmg[par][:, 0, ec * 128:(ec + 1) * 128], attnT[par * 2][:],
                    start=True, stop=False),
                  [("vtmg", par), ("attnT", par * 2)], [("PS", ps)], f"Bqo1{g}.{ec}")
                o("T", lambda e, ec=ec, ps=ps, par=par: e.matmul(
                    PS[ps][:, 0:256], vtmg[par][:, 1, ec * 128:(ec + 1) * 128], attnT[par * 2 + 1][:],
                    start=False, stop=False),
                  [("vtmg", par), ("attnT", par * 2 + 1)], [("PS", ps)], f"Bqo2{g}.{ec}")
                o("T", lambda e, ec=ec, ps=ps, par=par: e.matmul(
                    PS[ps][:, 0:256], linkvb[:, ec * 128:(ec + 1) * 128], hlq[par][:],
                    start=False, stop=True),
                  [("linkvb",), ("hlq", par)], [("PS", ps)], f"Blo{g}.{ec}")
                o("V", lambda e, ec=ec, ps=ps, par=par, ob=ob: e.tensor_tensor(
                    out=ob[:, ec, :], in0=gateg[par][:, ec, :], in1=PS[ps][:, 0:256], op=OP.mult),
                  [("gateg", par), ("PS", ps)], [("ogb", par, ec)], f"Bog{g}.{ec}")
            dma("QS", og_d.rearrange("ec p n -> p ec n")[:, :, tok:tok + 256], ob[:],
                [("ogb", par, ec2) for ec2 in range(HC)], [("og_d", g)], tag=f"Bogs{g}")

    if "B" in run_phases:
        b_pre(0)
        for g in range(NG):
            if g + 1 < NG:
                b_pre(g + 1)
            b_qolo(g)

    em.barrier("B|C")
    ctxB.close()

    ctxC = ExitStack()
    EC = ctxC.enter_context
    Wo_sb = EC(nc.sbuf_tensor("Wo_sb", [128, HC, D], BF16))
    ogt = EC(nc.sbuf_tensor("ogt", [128, HC, TA], BF16))
    xtc2 = [EC(nc.sbuf_tensor(f"xtc2_{i}", [128, DC, TA], BF16)) for i in range(2)]
    ao2 = [EC(nc.sbuf_tensor(f"ao2_{i}", [128, DC, TA], BF16)) for i in range(2)]
    yt = EC(nc.sbuf_tensor("yt", [128, DC, TA], F32))
    zt2 = [EC(nc.sbuf_tensor(f"zt2_{i}", [128, 2 * TA], BF16)) for i in range(2)]
    a1sb = EC(nc.sbuf_tensor("a1sb", [128, TA], BF16))
    pb1sb = EC(nc.sbuf_tensor("pb1sb", [128, TA], BF16))
    a3sb = EC(nc.sbuf_tensor("a3sb", [128, TA], BF16))
    b3sb = EC(nc.sbuf_tensor("b3sb", [128, TA], BF16))
    afsb = EC(nc.sbuf_tensor("afsb", [128, TA], BF16))
    zsc = EC(nc.sbuf_tensor("zsc", [128, 2 * TA], F32))
    bfsb = EC(nc.sbuf_tensor("bfsb", [128, TA], BF16))
    tc1b = EC(nc.sbuf_tensor("tc1b", [128, 2 * TA], F32))
    tc2b = EC(nc.sbuf_tensor("tc2b", [128, 2 * TA], F32))
    casb = EC(nc.sbuf_tensor("casb", [128, 2 * TA], BF16))
    cpl = EC(nc.sbuf_tensor("cpl", [128, 2 * TA], BF16))
    lgsb = EC(nc.sbuf_tensor("lgsb", [8, TA], F32))
    esb = EC(nc.sbuf_tensor("esb", [8, TA], F32))
    scsb = EC(nc.sbuf_tensor("scsb", [8, TA], F32))
    mk1 = EC(nc.sbuf_tensor("mk1", [8, TA], BF16))
    afm = EC(nc.sbuf_tensor("afm", [8, TA], F32))

    dma("QG", Wo_sb[:], Wo.rearrange("(ec p) m -> p ec m", p=128),
        [("d_in",)], [("Wo_sb",)])


    def rep2(tens, n=TA):
        return bass.AP(tensor=tens.ap().tensor, offset=0, ap=[[n, 128], [0, 2], [1, n]])

    def c_xload(t):
        par = t % 2
        tok = t * TA
        dma("QG", xtc2[par][:], xTr[:, :, tok:tok + TA], [("d_in",)], [("xtc", par)], tag=f"Cx{t}")

    def c_head(t):
        tok = t * TA
        dma("QS", ogt[:], og_d.rearrange("ec p n -> p ec n")[:, :, tok:tok + TA],
            [("og_d", 2 * t), ("og_d", 2 * t + 1)], [("ogt",)], tag=f"Cog{t}")
        o("T", lambda e, tok=tok: e.matmul(PS[2][:], ones_rb[:], r1b[0:1, tok:tok + TA],
                                           start=True, stop=True),
          [("ones_rb",), ("r1b", t)], [("PS", 2)], f"Ca1{t}")
        o("S", lambda e: e.activation(out=a1sb[:], in_=PS[2][:], func=AF.Copy),
          [("PS", 2)], [("a1sb",)], f"Ca1c{t}")
        o("T", lambda e, tok=tok: e.matmul(PS[2][:], ones_rb[:], p1b[0:1, tok:tok + TA],
                                           start=True, stop=True),
          [("ones_rb",), ("p1b", t)], [("PS", 2)], f"Cb1{t}")
        o("S", lambda e: e.activation(out=pb1sb[:], in_=PS[2][:], func=AF.Copy),
          [("PS", 2)], [("pb1sb",)], f"Cb1c{t}")

    def c_wo_parts(t):
        par = t % 2
        xc = xtc2[par]
        aot = ao2[par]
        ps_s, ps_q = (3, 4) if par == 0 else (6, 7)
        parts = []

        def mk(dc):
            def th():
                for d2 in (dc, dc + 1):
                    pwo = d2 % 2
                    for ec in range(HC):
                        o("T", lambda e, d2=d2, ec=ec, pwo=pwo: e.matmul(
                            PS[pwo][:], Wo_sb[:, ec, d2 * 128:(d2 + 1) * 128], ogt[:, ec, :],
                            start=(ec == 0), stop=(ec == HC - 1)),
                          [("Wo_sb",), ("ogt",)], [("PS", pwo)], f"Cwo{t}.{d2}.{ec}")
                for d2 in (dc, dc + 1):
                    pwo = d2 % 2
                    o("V", lambda e, d2=d2, dc=dc, pwo=pwo: e.tensor_scalar(
                        out=tc2b[:, (d2 - dc) * TA:(d2 - dc + 1) * TA],
                        in0=PS[pwo][:], scalar1=bop_sb[:, d2:d2 + 1], scalar2=None,
                        op0=OP.add),
                      [("PS", pwo), ("bop_sb",)], [("tc2b",)], f"Cwv{t}.{d2}")
                o("V", lambda e, dc=dc, xc=xc: e.tensor_tensor(
                    out=tc1b[:].rearrange("p (a n) -> p a n", a=2), in0=xc[:, dc:dc + 2, :],
                    in1=rep2(a1sb), op=OP.mult),
                  [("xtc", par), ("a1sb",)], [("tc1b",)], f"Cu1{t}.{dc}")
                o("V", lambda e: e.tensor_tensor(
                    out=tc1b[:], in0=tc1b[:], in1=tc2b[:], op=OP.add),
                  [("tc1b",), ("tc2b",)], [("tc1b",)], f"Cu2{t}.{dc}")
                o("V", lambda e, dc=dc, aot=aot: e.tensor_tensor(
                    out=aot[:, dc:dc + 2, :], in0=tc1b[:].rearrange("p (a n) -> p a n", a=2),
                    in1=rep2(pb1sb), op=OP.subtract),
                  [("tc1b",), ("pb1sb",)], [("ao", par, dc), ("ao", par, dc + 1)], f"Cao{t}.{dc}")
                o("S", lambda e, dc=dc, aot=aot: e.activation(
                    out=casb[:], in_=aot[:, dc:dc + 2, :].rearrange("p a n -> p (a n)"),
                    func=AF.Square, bias=zcol[:]),
                  [("ao", par, dc), ("ao", par, dc + 1)], [("casb",)], f"Csq{t}.{dc}")
                for d2 in (dc, dc + 1):
                    o("T", lambda e, d2=d2, dc=dc, ps_q=ps_q: e.matmul(
                        PS[ps_q][0:1, :], ones_cb[:], casb[:, (d2 - dc) * TA:(d2 - dc + 1) * TA],
                        start=(d2 == 0), stop=(d2 == DC - 1)),
                      [("ones_cb",), ("casb",)], [("PS", ps_q)], f"Css{t}.{d2}")
                    o("T", lambda e, d2=d2, ps_s=ps_s, aot=aot: e.matmul(
                        PS[ps_s][0:1, :], ones_cb[:], aot[:, d2, :],
                        start=(d2 == 0), stop=(d2 == DC - 1)),
                      [("ones_cb",), ("ao", par, d2)], [("PS", ps_s)], f"Csm{t}.{d2}")
            return th
        for dc in range(0, DC, 2):
            parts.append(mk(dc))
        return parts

    def c_mid_parts(t):
        par = t % 2
        xc = xtc2[par]
        aot = ao2[par]
        ps_s, ps_q = (3, 4) if par == 0 else (6, 7)
        tok = t * TA
        parts = []

        def head():
            m3 = rowF("m3"); v3 = rowF("v3"); t3r = rowF("t3r"); r3 = rowF("r3"); q3 = rowF("q3")
            o("V", lambda e, ps_s=ps_s: e.tensor_scalar_mul(m3, PS[ps_s][0:1, :], 1.0 / D), [("PS", ps_s)], [("R0",)])
            o("V", lambda e, ps_q=ps_q: e.tensor_scalar_mul(v3, PS[ps_q][0:1, :], 1.0 / D), [("PS", ps_q)], [("R1",)])
            o("V", lambda e: e.tensor_tensor(out=t3r, in0=m3, in1=m3, op=OP.mult), [("R0",)], [("R2",)])
            o("V", lambda e: e.tensor_tensor(out=v3, in0=v3, in1=t3r, op=OP.subtract),
              [("R1",), ("R2",)], [("R1",)])
            o("S", lambda e: e.activation(out=t3r, in_=v3, func=AF.Ln, bias=epsr[:]), [("R1",)], [("R2",)])
            o("S", lambda e: e.activation(out=r3, in_=t3r, func=AF.Exp, scale=-0.5, bias=zrow[:]),
              [("R2",)], [("R3",)])
            o("V", lambda e: e.tensor_tensor(out=q3, in0=m3, in1=r3, op=OP.mult),
              [("R0",), ("R3",)], [("R4",)])
            r3b = rowB("r3b"); q3b = rowB("q3b")
            o("V", lambda e: e.tensor_copy(r3b, r3), [("R3",)], [("RB0",)])
            o("V", lambda e: e.tensor_copy(q3b, q3), [("R4",)], [("RB1",)])
            o("T", lambda e: e.matmul(PS[2][:], ones_rb[:], r3b, start=True, stop=True),
              [("ones_rb",), ("RB0",)], [("PS", 2)], f"Ca3{t}")
            o("S", lambda e: e.activation(out=a3sb[:], in_=PS[2][:], func=AF.Copy),
              [("PS", 2)], [("a3sb",)], f"Ca3c{t}")
            o("T", lambda e: e.matmul(PS[2][:], ones_rb[:], q3b, start=True, stop=True),
              [("ones_rb",), ("RB1",)], [("PS", 2)], f"Cb3{t}")
            o("S", lambda e: e.activation(out=b3sb[:], in_=PS[2][:], func=AF.Copy),
              [("PS", 2)], [("b3sb",)], f"Cb3c{t}")
        parts.append(head)

        def mkdc(dc):
            def th():
                o("V", lambda e, dc=dc, aot=aot: e.tensor_tensor(
                    out=tc1b[:].rearrange("p (a n) -> p a n", a=2), in0=aot[:, dc:dc + 2, :],
                    in1=rep2(a3sb), op=OP.mult),
                  [("ao", par, dc), ("ao", par, dc + 1), ("a3sb",)], [("tc1b",)], f"Cy1{t}.{dc}")
                o("V", lambda e, dc=dc: e.tensor_tensor(
                    out=tc2b[:].rearrange("p (a n) -> p a n", a=2),
                    in0=tc1b[:].rearrange("p (a n) -> p a n", a=2),
                    in1=rep2(b3sb), op=OP.subtract),
                  [("tc1b",), ("b3sb",)], [("tc2b",)], f"Cy2{t}.{dc}")
                o("V", lambda e, dc=dc, xc=xc: e.tensor_tensor(
                    out=yt[:, dc:dc + 2, :], in0=tc2b[:].rearrange("p (a n) -> p a n", a=2),
                    in1=xc[:, dc:dc + 2, :], op=OP.add),
                  [("tc2b",), ("xtc", par)], [("yt", dc), ("yt", dc + 1)], f"Cy3{t}.{dc}")
                o("S", lambda e, dc=dc: e.activation(
                    out=casb[:], in_=yt[:, dc:dc + 2, :].rearrange("p a n -> p (a n)"),
                    func=AF.Square, bias=zcol[:]),
                  [("yt", dc), ("yt", dc + 1)], [("casb",)], f"Cysq{t}.{dc}")
                o("V", lambda e, dc=dc: e.tensor_copy(
                    cpl[:], yt[:, dc:dc + 2, :].rearrange("p a n -> p (a n)")),
                  [("yt", dc), ("yt", dc + 1)], [("cpl",)], f"Cyc{t}.{dc}")
                for d2 in (dc, dc + 1):
                    o("T", lambda e, d2=d2, dc=dc, ps_q=ps_q: e.matmul(
                        PS[ps_q][0:1, :], ones_cb[:], casb[:, (d2 - dc) * TA:(d2 - dc + 1) * TA],
                        start=(d2 == 0), stop=(d2 == DC - 1)),
                      [("ones_cb",), ("casb",)], [("PS", ps_q)], f"Cyss{t}.{d2}")
                    o("T", lambda e, d2=d2, dc=dc, ps_s=ps_s: e.matmul(
                        PS[ps_s][0:1, :], ones_cb[:], cpl[:, (d2 - dc) * TA:(d2 - dc + 1) * TA],
                        start=(d2 == 0), stop=(d2 == DC - 1)),
                      [("ones_cb",), ("cpl",)], [("PS", ps_s)], f"Cysm{t}.{d2}")
                    o("T", lambda e, d2=d2: e.matmul(PS[5][0:8, :], gwT_sb[:, d2, :], yt[:, d2, :],
                                                     start=(d2 == 0), stop=False),
                      [("gwT_sb",), ("yt", d2)], [("PS", 5)], f"Clg{t}.{d2}")
            return th
        for dc in range(0, DC, 2):
            parts.append(mkdc(dc))

        def tail():
            dma("QS", y_d.rearrange("dc p n -> p dc n")[:, :, tok:tok + TA], yt[:],
                [("yt", dc2) for dc2 in range(DC)], [("y_d", t)], tag=f"Cys{t}")
            mf = rowF("mf"); vf = rowF("vf"); tfr = rowF("tfr"); rf = rowF("rf")
            qf = rowF("qf"); nmf = rowF("nmf")
            o("V", lambda e, ps_s=ps_s: e.tensor_scalar_mul(mf, PS[ps_s][0:1, :], 1.0 / D), [("PS", ps_s)], [("R5",)])
            o("V", lambda e, ps_q=ps_q: e.tensor_scalar_mul(vf, PS[ps_q][0:1, :], 1.0 / D), [("PS", ps_q)], [("R6",)])
            o("V", lambda e: e.tensor_tensor(out=tfr, in0=mf, in1=mf, op=OP.mult), [("R5",)], [("R7",)])
            o("V", lambda e: e.tensor_tensor(out=vf, in0=vf, in1=tfr, op=OP.subtract),
              [("R6",), ("R7",)], [("R6",)])
            o("S", lambda e: e.activation(out=tfr, in_=vf, func=AF.Ln, bias=epsr[:]), [("R6",)], [("R7",)])
            o("S", lambda e: e.activation(out=rf, in_=tfr, func=AF.Exp, scale=-0.5, bias=zrow[:]),
              [("R7",)], [("R8",)])
            o("V", lambda e: e.tensor_tensor(out=qf, in0=mf, in1=rf, op=OP.mult),
              [("R5",), ("R8",)], [("R9",)])
            o("V", lambda e: e.tensor_scalar_mul(nmf, mf, -1.0), [("R5",)], [("R2",)])
            rfb = rowB("rfb"); qfb = rowB("qfb")
            o("V", lambda e: e.tensor_copy(rfb, rf), [("R8",)], [("RB2",)])
            o("V", lambda e: e.tensor_copy(qfb, qf), [("R9",)], [("RB3",)])
            o("T", lambda e: e.matmul(PS[5][0:8, :], srow[:], nmf, start=False, stop=True),
              [("srow",), ("R2",)], [("PS", 5)], f"Clgf{t}")
        parts.append(tail)
        return parts

    def c_epi_parts(t):
        tok = t * TA
        parts = []

        def bcasts():
            o("T", lambda e: e.matmul(PS[2][:], ones_rb[:], rowB("rfb"), start=True, stop=True),
              [("ones_rb",), ("RB2",)], [("PS", 2)], f"Caf{t}")
            o("S", lambda e: e.activation(out=afsb[:], in_=PS[2][:], func=AF.Copy),
              [("PS", 2)], [("afsb",)], f"Cafc{t}")
            o("T", lambda e: e.matmul(PS[2][:], ones_rb[:], rowB("qfb"), start=True, stop=True),
              [("ones_rb",), ("RB3",)], [("PS", 2)], f"Cbf{t}")
            o("S", lambda e: e.activation(out=bfsb[:], in_=PS[2][:], func=AF.Copy),
              [("PS", 2)], [("bfsb",)], f"Cbfc{t}")
        parts.append(bcasts)

        def mkz(dc0):
            def th():
                zb = zt2[(dc0 // 2) % 2]
                o("P", lambda e, dc0=dc0: e.tensor_tensor(
                    out=zsc[:].rearrange("p (a n) -> p a n", a=2), in0=yt[:, dc0:dc0 + 2, :],
                    in1=rep2(afsb), op=OP.mult),
                  [("yt", dc0), ("yt", dc0 + 1), ("afsb",)], [("zsc",)], f"Cz1{t}.{dc0}")
                o("P", lambda e, dc0=dc0, zb=zb: e.tensor_tensor(
                    out=zb[:].rearrange("p (a n) -> p a n", a=2),
                    in0=zsc[:].rearrange("p (a n) -> p a n", a=2),
                    in1=rep2(bfsb), op=OP.subtract),
                  [("zsc",), ("bfsb",)], [("zt2", (dc0 // 2) % 2)], f"Cz2{t}.{dc0}")
                dma("QS", z_d.rearrange("dc p n -> p dc n")[:, dc0:dc0 + 2, tok:tok + TA], zb[:].rearrange("p (a n) -> p a n", a=2),
                    [("zt2", (dc0 // 2) % 2)], [("z_d", t, dc0), ("z_d", t, dc0 + 1)], tag=f"Czs{t}.{dc0}")
            return th
        for dc0 in range(0, DC, 2):
            parts.append(mkz(dc0))

        def router1():
            o("V", lambda e: e.tensor_copy(esb[:], PS[5][0:8, :]), [("PS", 5)], [("esb",)], f"Cl0{t}")
            o("T", lambda e: e.matmul(PS[2][0:8, :], ones8f[:], rowF("rf"), start=True, stop=True),
              [("ones8f",), ("R8",)], [("PS", 2)], f"Crb{t}")
            o("V", lambda e: e.tensor_tensor(out=lgsb[:], in0=esb[:], in1=PS[2][0:8, :], op=OP.mult),
              [("esb",), ("PS", 2)], [("lgsb",)], f"Clg2{t}")
            o("S", lambda e: e.activation(out=esb[:], in_=lgsb[:], func=AF.Exp, bias=zcol[0:8, :]),
              [("lgsb",)], [("esb",)], f"Cexp{t}")
            o("T", lambda e: e.matmul(PS[5][0:1, :], o8col[:], esb[:], start=True, stop=True),
              [("o8col",), ("esb",)], [("PS", 5)], f"Cse{t}")
            sumr = rowF("sumr"); rsum = rowF("rsum")
            o("V", lambda e: e.tensor_copy(sumr, PS[5][0:1, :]), [("PS", 5)], [("R0",)])
            o("V", lambda e: e.reciprocal(rsum, sumr), [("R0",)], [("R1",)])
            o("T", lambda e: e.matmul(PS[2][0:8, :], ones8f[:], rsum, start=True, stop=True),
              [("ones8f",), ("R1",)], [("PS", 2)], f"Crs{t}")
            o("V", lambda e: e.tensor_tensor(out=scsb[:], in0=esb[:], in1=PS[2][0:8, :], op=OP.mult),
              [("esb",), ("PS", 2)], [("scsb",)], f"Csc{t}")
        parts.append(router1)

        def tmax(tag, src, skey, sc_a, akey, sc_b, bkey):
            o("T", lambda e, src=src: e.matmul(PS[5][0:8, :], perm8_sb[:, 0:8], src,
                                               start=True, stop=True),
              [("perm8",), skey], [("PS", 5)], tag + "p4")
            o("V", lambda e, src=src, sc_a=sc_a: e.tensor_tensor(
                out=sc_a[:], in0=src, in1=PS[5][0:8, :], op=OP.max),
              [skey, ("PS", 5)], [akey], tag + "m4")
            o("T", lambda e, sc_a=sc_a: e.matmul(PS[2][0:8, :], perm8_sb[:, 8:16], sc_a[:],
                                                 start=True, stop=True),
              [("perm8",), akey], [("PS", 2)], tag + "p2")
            o("V", lambda e, sc_a=sc_a, sc_b=sc_b: e.tensor_tensor(
                out=sc_b[:], in0=sc_a[:], in1=PS[2][0:8, :], op=OP.max),
              [akey, ("PS", 2)], [bkey], tag + "m2")
            o("T", lambda e, sc_b=sc_b: e.matmul(PS[5][0:8, :], perm8_sb[:, 16:24], sc_b[:],
                                                 start=True, stop=True),
              [("perm8",), bkey], [("PS", 5)], tag + "p1")
            o("V", lambda e, sc_b=sc_b, sc_a=sc_a: e.tensor_tensor(
                out=sc_a[:], in0=sc_b[:], in1=PS[5][0:8, :], op=OP.max),
              [bkey, ("PS", 5)], [akey], tag + "m1")

        def rtr2():
            tmax(f"Ct1{t}", scsb[:], ("scsb",), esb, ("esb",), lgsb, ("lgsb",))
            o("V", lambda e: e.tensor_tensor(out=mk1[:], in0=scsb[:], in1=esb[:], op=OP.is_ge),
              [("scsb",), ("esb",)], [("mk1",)], f"Cmk1{t}")
            o("V", lambda e: e.scalar_tensor_tensor(out=lgsb[:], in0=mk1[:], scalar=-1e30,
                                                    in1=scsb[:], op0=OP.mult, op1=OP.add),
              [("mk1",), ("scsb",)], [("lgsb",)], f"Cmsk{t}")
        parts.append(rtr2)

        def rtr3():
            tmax(f"Ct2{t}", lgsb[:], ("lgsb",), esb, ("esb",), lgsb, ("lgsb",))
            o("V", lambda e: e.tensor_tensor(out=mk1[:], in0=scsb[:], in1=esb[:], op=OP.is_ge),
              [("scsb",), ("esb",)], [("mk1",)], f"Cmk2{t}")
            o("V", lambda e: e.tensor_tensor(out=afm[:], in0=mk1[:], in1=scsb[:], op=OP.mult),
              [("mk1",), ("scsb",)], [("afm",)], f"Ca{t}")
            dma("QS", a_d[:, tok:tok + TA], afm[:], [("afm",)], [("a_d", t)], tag=f"Cas{t}")
            if debug_a:
                dma("QS", dbg_a[:, tok:tok + TA], afm[:], [("afm",)], [("dbg_a", t)], tag=f"Cad{t}")
                dma("QS", dbg_y.rearrange("dc p n -> p dc n")[:, :, tok:tok + TA], yt[:],
                    [("yt", dc2) for dc2 in range(DC)], [("dbg_y", t)], tag=f"Cyd{t}")
        parts.append(rtr3)
        return parts

    if "C" in run_phases:
        c_xload(0)
        c_head(0)
        for p in c_wo_parts(0):
            p()
        c_xload(1)
        c_head(1)
        for t in range(NT):
            wparts = c_wo_parts(t + 1) if t + 1 < NT else []
            rest = (c_epi_parts(t - 1) if t >= 1 else []) + c_mid_parts(t)
            wi = 0
            for ri, r in enumerate(rest):
                r()
                while wi * len(rest) < (ri + 1) * len(wparts):
                    wparts[wi]()
                    wi += 1
            while wi < len(wparts):
                wparts[wi]()
                wi += 1
            if t + 2 < NT:
                c_xload(t + 2)
                c_head(t + 2)
        for p in c_epi_parts(NT - 1):
            p()

    em.barrier("C|E")
    ctxC.close()
    ctxABC.close()

    ctxE = ExitStack()
    EE = ctxE.enter_context
    oh8_sb = EE(nc.sbuf_tensor("oh8_sb", [8, 1024], BF16))
    w13 = [EE(nc.sbuf_tensor(f"w13_{i}", [128, 8, 2 * D], BF16)) for i in range(2)]
    w2b = EE(nc.sbuf_tensor("w2b", [128, 8, D], BF16))
    zt2e = [EE(nc.sbuf_tensor(f"zt2e{i}", [128, DC, TE], BF16)) for i in range(2)]
    Xt = EE(nc.sbuf_tensor("Xt", [128, DC, SUB], BF16))
    Hs = EE(nc.sbuf_tensor("Hs", [128, DC, SUB], BF16))
    Yacc = EE(nc.sbuf_tensor("Yacc", [128, DC, TE], F32))
    s1e = EE(nc.sbuf_tensor("s1e", [128, SUB], BF16))
    asb2 = EE(nc.sbuf_tensor("asb2", [128, TE], BF16))
    arow = EE(nc.sbuf_tensor("arow", [8, TE], BF16))
    yep = EE(nc.sbuf_tensor("yep", [128, 2 * SUB], F32))
    outb = EE(nc.sbuf_tensor("outb", [128, 2 * SUB], F32))
    sqe = EE(nc.sbuf_tensor("sqe", [128, SUB], BF16))
    ybb = EE(nc.sbuf_tensor("ybb", [128, SUB], BF16))
    a4sb = EE(nc.sbuf_tensor("a4sb", [128, SUB], BF16))
    b4sb = EE(nc.sbuf_tensor("b4sb", [128, SUB], BF16))

    def wsrc(mat, e):
        if e == 8:
            base = {0: w1s, 1: w3s, 2: w2s}[mat]
            return base.rearrange("(kc p) m -> p kc m", p=128)
        base = {0: w1, 1: w3, 2: w2}[mat]
        return base[e].rearrange("(kc p) m -> p kc m", p=128)

    def e_abc_xt(tt, u, ei, sb2):
        if ei == 8:
            return
        tb = tt % 2
        zt = zt2e[tb]
        pab = 6 + u % 2
        ssl = slice(sb2 * SUB, (sb2 + 1) * SUB)
        o("T", lambda e, ei=ei, sb2=sb2, pab=pab: e.matmul(
            PS[pab][:], oh8_sb[:, ei * 128:(ei + 1) * 128],
            arow[:, sb2 * SUB:(sb2 + 1) * SUB], start=True, stop=True),
          [("oh8_sb",), ("arow",)], [("PS", pab)], f"Eab{tt}.{ei}.{sb2}")
        o("V", lambda e, sb2=sb2, pab=pab: e.tensor_copy(
            asb2[:, sb2 * SUB:(sb2 + 1) * SUB], PS[pab][:]),
          [("PS", pab)], [("asb2", sb2)], f"Eae{tt}.{ei}.{sb2}")
        for dc in range(0, DC, 2):
            arep = bass.AP(tensor=asb2.ap().tensor, offset=sb2 * SUB,
                           ap=[[TE, 128], [0, 2], [1, SUB]])
            o("V", lambda e, dc=dc, ssl=ssl, sb2=sb2, arep=arep, zt=zt: e.tensor_tensor(
                out=Xt[:, dc:dc + 2, :], in0=zt[:, dc:dc + 2, ssl],
                in1=arep, op=OP.mult),
              [("zt", tb, sb2), ("asb2", sb2)], [("Xt", dc), ("Xt", dc + 1)], f"Ex{tt}.{ei}.{sb2}.{dc}")

    def e_epilogue(tt):
        tok = tt * TE
        for sb2 in range(2):
            ssl = slice(sb2 * SUB, (sb2 + 1) * SUB)
            tok2 = tok + sb2 * SUB
            for dc in range(DC):
                o("S", lambda e, dc=dc, ssl=ssl: e.activation(out=sqe[:], in_=Yacc[:, dc, ssl],
                                                              func=AF.Square, bias=zcol[:]),
                  [("Yacc", dc, sb2)], [("sqe",)], f"Efs{tt}.{sb2}.{dc}")
                o("T", lambda e, dc=dc: e.matmul(PS[0][0:1, :], ones_cb[:], sqe[:],
                                                 start=(dc == 0), stop=(dc == DC - 1)),
                  [("ones_cb",), ("sqe",)], [("PS", 0)], f"Efss{tt}.{sb2}.{dc}")
                o("S", lambda e, dc=dc, ssl=ssl: e.activation(out=ybb[:], in_=Yacc[:, dc, ssl],
                                                              func=AF.Copy),
                  [("Yacc", dc, sb2)], [("ybb",)], f"Efc{tt}.{sb2}.{dc}")
                o("T", lambda e, dc=dc: e.matmul(PS[1][0:1, :], ones_cb[:], ybb[:],
                                                 start=(dc == 0), stop=(dc == DC - 1)),
                  [("ones_cb",), ("ybb",)], [("PS", 1)], f"Efsm{tt}.{sb2}.{dc}")
            m4 = rowF("m4"); v4 = rowF("v4"); t4r = rowF("t4r"); r4 = rowF("r4"); q4 = rowF("q4")
            o("V", lambda e: e.tensor_scalar_mul(m4, PS[1][0:1, :], 1.0 / D), [("PS", 1)], [("R0",)])
            o("V", lambda e: e.tensor_scalar_mul(v4, PS[0][0:1, :], 1.0 / D), [("PS", 0)], [("R1",)])
            o("V", lambda e: e.tensor_tensor(out=t4r, in0=m4, in1=m4, op=OP.mult), [("R0",)], [("R2",)])
            o("V", lambda e: e.tensor_tensor(out=v4, in0=v4, in1=t4r, op=OP.subtract),
              [("R1",), ("R2",)], [("R1",)])
            o("S", lambda e: e.activation(out=t4r, in_=v4, func=AF.Ln, bias=epsr[:]), [("R1",)], [("R2",)])
            o("S", lambda e: e.activation(out=r4, in_=t4r, func=AF.Exp, scale=-0.5, bias=zrow[:]),
              [("R2",)], [("R3",)])
            o("V", lambda e: e.tensor_tensor(out=q4, in0=m4, in1=r4, op=OP.mult),
              [("R0",), ("R3",)], [("R4",)])
            r4b = rowB("r4b"); q4b = rowB("q4b")
            o("V", lambda e: e.tensor_copy(r4b, r4), [("R3",)], [("RB0",)])
            o("V", lambda e: e.tensor_copy(q4b, q4), [("R4",)], [("RB1",)])
            o("T", lambda e: e.matmul(PS[0][:], ones_rb[:], r4b, start=True, stop=True),
              [("ones_rb",), ("RB0",)], [("PS", 0)], f"Ea4{tt}.{sb2}")
            o("S", lambda e: e.activation(out=a4sb[:], in_=PS[0][:], func=AF.Copy),
              [("PS", 0)], [("a4sb",)], f"Ea4c{tt}.{sb2}")
            o("T", lambda e: e.matmul(PS[1][:], ones_rb[:], q4b, start=True, stop=True),
              [("ones_rb",), ("RB1",)], [("PS", 1)], f"Eb4{tt}.{sb2}")
            o("S", lambda e: e.activation(out=b4sb[:], in_=PS[1][:], func=AF.Copy),
              [("PS", 1)], [("b4sb",)], f"Eb4c{tt}.{sb2}")
            for dc in range(0, DC, 2):
                dma("QS", yep[:].rearrange("p (a n) -> p a n", a=2),
                    y_d.rearrange("dc p n -> p dc n")[:, dc:dc + 2, tok2:tok2 + SUB],
                    [("y_d", 2 * tt + sb2)], [("yep",)], tag=f"Eyl{tt}.{sb2}.{dc}")
                o("P", lambda e, dc=dc, ssl=ssl: e.tensor_tensor(
                    out=outb[:].rearrange("p (a n) -> p a n", a=2),
                    in0=Yacc[:, dc:dc + 2, ssl], in1=rep2(a4sb, SUB), op=OP.mult),
                  [("Yacc", dc, sb2), ("Yacc", dc + 1, sb2), ("a4sb",)], [("outb",)],
                  f"Eo1{tt}.{sb2}.{dc}")
                o("P", lambda e, dc=dc: e.tensor_tensor(
                    out=outb[:].rearrange("p (a n) -> p a n", a=2),
                    in0=outb[:].rearrange("p (a n) -> p a n", a=2),
                    in1=rep2(b4sb, SUB), op=OP.subtract),
                  [("outb",), ("b4sb",)], [("outb",)], f"Eo2{tt}.{sb2}.{dc}")
                o("P", lambda e, dc=dc: e.tensor_tensor(
                    out=outb[:], in0=outb[:], in1=yep[:], op=OP.add),
                  [("outb",), ("yep",)], [("outb",)], f"Eo3{tt}.{sb2}.{dc}")
                dma("QS", outTr[:, dc:dc + 2, tok2:tok2 + SUB],
                    outb[:].rearrange("p (a n) -> p a n", a=2),
                    [("outb",)],
                    [("out_d", tt, sb2, dc), ("out_d", tt, sb2, dc + 1)], tag=f"Eos{tt}.{sb2}.{dc}")

    if "E" in run_phases:

        dma("QG", oh8_sb[:], oh8, [("d_in",)], [("oh8_sb",)])

        def e_loads(tt):
            tok = tt * TE
            tb = tt % 2
            for h in range(2):
                dma("QA", zt2e[tb][:, :, h * SUB:(h + 1) * SUB],
                    z_d.rearrange("dc p n -> p dc n")[:, :, tok + h * SUB:tok + (h + 1) * SUB],
                    [("z_d", 2 * tt + h)], [("zt", tb, h)], tag=f"Ez{tt}.{h}")

        def e_aload(tt):
            tok = tt * TE
            dma("QG", arow[:], a_d[:, tok:tok + TE], [("a_d", 2 * tt), ("a_d", 2 * tt + 1)],
                [("arow",)], tag=f"Ea{tt}")

        _egpos = [0]
        e_loads(0)
        e_aload(0)
        for tt in range(NTE):
            tok = tt * TE
            if tt + 1 < NTE:
                e_loads(tt + 1)
            units = [(ei, sb2) for ei in range(9) for sb2 in range(2)]
            e_abc_xt(tt, 0, *units[0])
            for u, (ei, sb2) in enumerate(units):
                if sb2 == 0:
                    _egpos[0] += 1
                wp = _egpos[0] % 2
                if sb2 == 0:
                    dma("QG", w13[wp][:, :, 0:D], wsrc(0, ei), [("d_in",)], [("w13", wp, 0)],
                        tag=f"Ew1.{tt}.{ei}")
                    dma("QG", w13[wp][:, :, D:2 * D], wsrc(1, ei), [("d_in",)], [("w13", wp, 1)],
                        tag=f"Ew3.{tt}.{ei}")
                    dma("QG", w2b[:], wsrc(2, ei), [("d_in",)], [("w2b",)],
                        tag=f"Ew2.{tt}.{ei}")
                ssl = slice(sb2 * SUB, (sb2 + 1) * SUB)
                zt = zt2e[tt % 2]
                if ei < 8:
                    xkeys = lambda kc: ("Xt", kc)
                else:
                    xkeys = lambda kc, sb2=sb2, tt=tt: ("zt", tt % 2, sb2)
                for hc in range(DC):
                    p1 = 0 if hc % 2 == 0 else 2
                    p3 = 1 if hc % 2 == 0 else 3
                    for kc in range(DC):
                        rh = Xt[:, kc, :] if ei < 8 else zt[:, kc, ssl]
                        o("T", lambda e, kc=kc, hc=hc, p1=p1, wp=wp, rh=rh: e.matmul(
                            PS[p1][:], w13[wp][:, kc, hc * 128:(hc + 1) * 128], rh,
                            start=(kc == 0), stop=(kc == DC - 1)),
                          [("w13", wp, 0), xkeys(kc)], [("PS", p1)], f"Ep1.{tt}.{ei}.{sb2}.{hc}.{kc}")
                    for kc in range(DC):
                        rh = Xt[:, kc, :] if ei < 8 else zt[:, kc, ssl]
                        o("T", lambda e, kc=kc, hc=hc, p3=p3, wp=wp, rh=rh: e.matmul(
                            PS[p3][:], w13[wp][:, kc, D + hc * 128:D + (hc + 1) * 128], rh,
                            start=(kc == 0), stop=(kc == DC - 1)),
                          [("w13", wp, 1), xkeys(kc)], [("PS", p3)], f"Ep3.{tt}.{ei}.{sb2}.{hc}.{kc}")
                    o("S", lambda e, p1=p1: e.activation(out=s1e[:], in_=PS[p1][:], func=AF.Silu, bias=zcol[:]),
                      [("PS", p1)], [("s1e",)], f"Esl{tt}.{ei}.{sb2}.{hc}")
                    o("V", lambda e, hc=hc, p3=p3: e.tensor_tensor(
                        out=Hs[:, hc, :], in0=s1e[:], in1=PS[p3][:], op=OP.mult),
                      [("s1e",), ("PS", p3)], [("Hs", hc)], f"Eh{tt}.{ei}.{sb2}.{hc}")
                if u == 0 and tt > 0:
                    e_epilogue(tt - 1)
                if u + 1 < len(units):
                    e_abc_xt(tt, u + 1, *units[u + 1])
                for dc in range(DC):
                    py = 4 + dc % 2
                    for kc in range(DC):
                        o("T", lambda e, kc=kc, dc=dc, py=py: e.matmul(
                            PS[py][:], w2b[:, kc, dc * 128:(dc + 1) * 128], Hs[:, kc, :],
                            start=(kc == 0), stop=(kc == DC - 1)),
                          [("w2b",), ("Hs", kc)], [("PS", py)], f"Ew2m.{tt}.{ei}.{sb2}.{dc}.{kc}")
                    ysl = Yacc[:, dc, ssl]
                    if ei == 0:
                        o("V", lambda e, ysl=ysl, py=py: e.tensor_copy(ysl, PS[py][:]),
                          [("PS", py)], [("Yacc", dc, sb2)], f"Eyc{tt}.{ei}.{sb2}.{dc}")
                    else:
                        o("V", lambda e, ysl=ysl, py=py: e.tensor_tensor(
                            out=ysl, in0=ysl, in1=PS[py][:], op=OP.add),
                          [("PS", py), ("Yacc", dc, sb2)], [("Yacc", dc, sb2)],
                          f"Eya{tt}.{ei}.{sb2}.{dc}")
            if u == len(units) - 1 and tt + 1 < NTE:
                e_aload(tt + 1)


    if "E" in run_phases:
        e_epilogue(NTE - 1)

    if debug_y and "E" not in run_phases:
        tmpy = Yacc[:, :, 0:TA]
        for t in range(NT):
            tok = t * TA
            dma("QA", tmpy, y_d.rearrange("dc p n -> p dc n")[:, :, tok:tok + TA],
                [("y_d", t)], [("tmpy",)])
            dma("QS", outTr[:, :, tok:tok + TA], tmpy, [("tmpy",)], [("out_d", t)])

    em.barrier("END")

    ops = em.finalize(nc, sems)
    totals = em.final_totals

    streams = {b: [] for b in set(BODY_OF.values())}
    bodies_all = sorted(streams.keys())
    for i, op_ in enumerate(ops):
        if op_["barrier"]:
            for b in bodies_all:
                streams[b].append(("bar", op_["totals"]))
            continue
        b = BODY_OF[op_["e"]]
        streams[b].append(("op", op_))

    blk = E(nc.Block())

    def emit_body(eng, body):
        for kind, item in streams[body]:
            if kind == "bar":
                if body in ("vector", "scalar", "gpsimd"):
                    eng.drain()
                for s, v in sorted(item.items()):
                    if v > 0:
                        eng.wait_ge(sems[s], v)
                continue
            op_ = item
            for s, v in op_["waits"]:
                eng.wait_ge(sems[s], v)
            if op_.get("need_drain"):
                eng.drain()
            inst = op_["fn"](eng)
            if op_["inc"]:
                n = 16 if op_["e"] in QUEUES else 1
                inst.then_inc(sems[op_["sem"]], n)

    @blk.tensor
    def _(eng):
        emit_body(eng, "tensor")

    @blk.vector
    def _(eng):
        emit_body(eng, "vector")

    @blk.scalar
    def _(eng):
        emit_body(eng, "scalar")

    @blk.gpsimd
    def _(eng):
        emit_body(eng, "gpsimd")

    @blk.sync
    def _(eng):
        emit_body(eng, "sync")

    ctxE.close()
    ctx.close()
    return nc


def kernel(L0_attn_x, L0_attn_norm_g, L0_attn_norm_b, L0_attn_W_h, L0_attn_b_h,
           L0_attn_W_qk, L0_attn_b_qk, L0_attn_qk_gamma, L0_attn_qk_beta,
           L0_attn_rel_emb, L0_attn_W_o, L0_attn_b_o, L0_ffn_gate_w,
           L0_ffn_w1, L0_ffn_w2, L0_ffn_w3, L0_ffn_w1s, L0_ffn_w2s, L0_ffn_w3s):
    f32 = np.float32
    x = np.asarray(L0_attn_x, dtype=f32)
    shared = dict(
        Wh=np.ascontiguousarray(L0_attn_W_h, dtype=f32),
        Wqk=np.ascontiguousarray(L0_attn_W_qk, dtype=f32),
        Wo=np.ascontiguousarray(L0_attn_W_o, dtype=f32),
        bh=np.ascontiguousarray(L0_attn_b_h, dtype=f32),
        bhp=np.ascontiguousarray(np.asarray(L0_attn_b_h, dtype=f32).reshape(32, 128).T),
        bqk=np.ascontiguousarray(L0_attn_b_qk, dtype=f32),
        ngv=np.ascontiguousarray(L0_attn_norm_g, dtype=f32),
        nbv=np.ascontiguousarray(L0_attn_norm_b, dtype=f32),
        nbp=np.ascontiguousarray(np.asarray(L0_attn_norm_b, dtype=f32).reshape(8, 128).T),
        bop=np.ascontiguousarray(np.asarray(L0_attn_b_o, dtype=f32).reshape(8, 128).T),
        bo=np.ascontiguousarray(L0_attn_b_o, dtype=f32),
        qgT=np.ascontiguousarray(np.asarray(L0_attn_qk_gamma, dtype=f32).T),
        qbT=np.ascontiguousarray(np.asarray(L0_attn_qk_beta, dtype=f32).T),
        qg3=np.ascontiguousarray(np.asarray(L0_attn_qk_gamma, dtype=f32)[3:4, :]),
        qb3=np.ascontiguousarray(np.asarray(L0_attn_qk_beta, dtype=f32)[3:4, :]),
        relE=np.ascontiguousarray(L0_attn_rel_emb, dtype=f32),
        gwT=np.ascontiguousarray(np.asarray(L0_ffn_gate_w, dtype=f32).T),
        w1=np.ascontiguousarray(L0_ffn_w1, dtype=f32),
        w2=np.ascontiguousarray(L0_ffn_w2, dtype=f32),
        w3=np.ascontiguousarray(L0_ffn_w3, dtype=f32),
        w1s=np.ascontiguousarray(np.asarray(L0_ffn_w1s, dtype=f32)[0]),
        w2s=np.ascontiguousarray(np.asarray(L0_ffn_w2s, dtype=f32)[0]),
        w3s=np.ascontiguousarray(np.asarray(L0_ffn_w3s, dtype=f32)[0]),
    )
    b1 = bucket1d_np()
    ohC = np.zeros((32, 512), dtype=f32)
    ohC[b1, np.arange(511)] = 1.0
    shared["ohC"] = ohC
    oh8 = np.zeros((8, 1024), dtype=f32)
    for e in range(8):
        oh8[e, e * 128:(e + 1) * 128] = 1.0
    shared["oh8"] = oh8
    shared["antiI"] = np.ascontiguousarray(np.eye(128, dtype=f32)[::-1])
    perm8 = np.zeros((8, 24), dtype=f32)
    for si, s in enumerate((4, 2, 1)):
        for m in range(8):
            perm8[(m + s) % 8, si * 8 + m] = 1.0
    shared["perm8"] = perm8

    nc = build_nc()
    in_maps = []
    for c in range(NCORES):
        m = dict(shared)
        m["xT"] = np.ascontiguousarray(x[c].T)
        in_maps.append(m)
    res = run_bass_kernel_spmd(nc, in_maps, core_ids=list(range(NCORES)))
    out = np.stack([np.asarray(r["outT"]).T for r in res.results])
    return np.ascontiguousarray(out)


if __name__ == "__main__":
    import time
    t0 = time.time()
    nc = build_nc()
    print(f"built ok in {time.time() - t0:.1f}s")


# version 27
# speedup vs baseline: 1.0153x; 1.0153x over previous
import math
import numpy as np
import ml_dtypes
import concourse.bass as bass
import concourse.mybir as mybir
from concourse.bass_utils import run_bass_kernel_spmd

F32 = mybir.dt.float32
BF16 = mybir.dt.bfloat16
AF = mybir.ActivationFunctionType
OP = mybir.AluOpType
AX = mybir.AxisListType

NCORES = 8
N = 4096
D = 1024
DC = 8
HID = 2048
HC = 16
QKD = 128
GSZ = 256
NG = N // GSZ
TA = 512
NT = N // TA
TE = 1024
NTE = N // TE
SUB = 512
EPS = 1e-5
NEXP = 8

PARANOID = False
QPOOL = {"QS": 10, "QA": 8, "QG": 6}
COMPUTE_ENGS = ("T", "V", "S", "P")
QUEUES = {"QS": "sync", "QA": "scalar", "QG": "gpsimd"}
BODY_OF = {"T": "tensor", "V": "vector", "S": "scalar", "P": "gpsimd",
           "QS": "sync", "QA": "scalar", "QG": "gpsimd"}


def bucket1d_np():
    d = np.arange(-255, 256)
    retv = (d < 0).astype(np.int32) * 16
    a = np.abs(d)
    lg = np.log(np.maximum(a, 1).astype(np.float32) / np.float32(8.0))
    vl = 8 + (lg / np.float32(math.log(16.0)) * np.float32(8.0)).astype(np.int32)
    vl = np.minimum(vl, 15)
    return retv + np.where(a < 8, a, vl)


class Emitter:

    def __init__(self):
        self.ops = []

    def op(self, eng, fn, reads=(), writes=(), tag="", serial=False):
        self.ops.append(dict(e=eng, fn=fn, r=tuple(reads), w=tuple(writes),
                             tag=tag, barrier=False, inc=False, serial=serial,
                             waits=None, dep=set(), drain_dep=0, myval=0))

    def barrier(self, tag="bar"):
        self.ops.append(dict(e=None, fn=None, r=(), w=(), tag=tag, barrier=True,
                             inc=False, serial=False, waits=None, dep=set(),
                             drain_dep=0, myval=0))

    def finalize(self, nc, sems):
        ops = self.ops
        last_w = {}
        readers = {}
        last_bar = -1
        last_op_of = {}
        for i, o in enumerate(ops):
            if o["barrier"]:
                for e, j in last_op_of.items():
                    if j > last_bar:
                        if ops[j]["e"] in COMPUTE_ENGS:
                            ops[j]["inc"] = True
                last_bar = i
                continue
            deps = set()
            for k in o["r"]:
                if k in last_w:
                    deps.add(last_w[k])
            for k in o["w"]:
                if k in last_w:
                    deps.add(last_w[k])
                for j in readers.get(k, ()):
                    deps.add(j)
            e = o["e"]
            for d in deps:
                if d <= last_bar:
                    continue
                pe = ops[d]["e"]
                if pe == e:
                    if e == "T":
                        continue
                    if e in QUEUES:
                        ops[d]["inc"] = True
                        o["dep"].add(d)
                    elif PARANOID:
                        ops[d]["inc"] = True
                        o["dep"].add(d)
                        o["drain_dep"] = max(o["drain_dep"], d + 1)
                    else:
                        o["drain_dep"] = max(o["drain_dep"], d + 1)
                elif pe in QUEUES and e in QUEUES and BODY_OF[pe] == BODY_OF[e]:
                    ops[d]["inc"] = True
                    o["dep"].add(d)
                else:
                    ops[d]["inc"] = True
                    o["dep"].add(d)
            for k in o["w"]:
                last_w[k] = i
                readers[k] = []
            for k in o["r"]:
                readers.setdefault(k, []).append(i)
            last_op_of[e] = i
        counters = {s: 0 for s in sems}
        semval = [0] * len(ops)
        qrot = {q: 0 for q in QUEUES}
        for i, o in enumerate(ops):
            if o["barrier"]:
                continue
            e = o["e"]
            if e in QUEUES:
                j = qrot[e] % QPOOL[e]
                qrot[e] += 1
                name = f"{e}.{j}"
                counters[name] += 16
                semval[i] = counters[name]
                o["inc"] = True
                o["myval"] = counters[name]
                o["sem"] = name
                o["guard"] = counters[name] - 16
            elif o["inc"]:
                counters[e] += 1
                semval[i] = counters[e]
                o["sem"] = e
        bodies = sorted(set(BODY_OF.values()))
        wm = {b: {s: 0 for s in sems} for b in bodies}
        totals_hist = []
        cnt2 = {s: 0 for s in sems}
        last_drain_pos = {b: 0 for b in bodies}
        for i, o in enumerate(ops):
            if o["barrier"]:
                o["totals"] = dict(cnt2)
                for b in bodies:
                    for s in sems:
                        wm[b][s] = max(wm[b][s], cnt2[s])
                continue
            e = o["e"]
            b = BODY_OF[e]
            waits = {}
            for d in o["dep"]:
                ps = ops[d].get("sem") or ops[d]["e"]
                v = semval[d]
                if v > wm[b].get(ps, 0):
                    waits[ps] = max(waits.get(ps, 0), v)
            if e in QUEUES and o.get("guard", 0) > 0:
                s = o["sem"]
                if o["guard"] > wm[b].get(s, 0):
                    waits[s] = max(waits.get(s, 0), o["guard"])
            for s, v in waits.items():
                wm[b][s] = v
            o["waits"] = sorted(waits.items())
            o["need_drain"] = (o["drain_dep"] > last_drain_pos[b])
            if o["need_drain"]:
                last_drain_pos[b] = i
            if o["inc"]:
                cnt2[o["sem"]] += 16 if o["e"] in QUEUES else 1
        self.final_totals = dict(cnt2)
        return ops


def build_nc(run_phases=("A", "B", "C", "E"), debug_y=False, debug_a=False):
    nc = bass.Bass("TRN2", target_bir_lowering=False)
    em = Emitter()

    xT = nc.dram_tensor("xT", [D, N], F32, kind="ExternalInput").ap()
    Wh = nc.dram_tensor("Wh", [D, 2 * HID], F32, kind="ExternalInput").ap()
    Wqk = nc.dram_tensor("Wqk", [D, QKD], F32, kind="ExternalInput").ap()
    Wo = nc.dram_tensor("Wo", [HID, D], F32, kind="ExternalInput").ap()
    bh = nc.dram_tensor("bh", [2 * HID], F32, kind="ExternalInput").ap()
    bhp = nc.dram_tensor("bhp", [128, 32], F32, kind="ExternalInput").ap()
    bqk = nc.dram_tensor("bqk", [QKD], F32, kind="ExternalInput").ap()
    ngv = nc.dram_tensor("ngv", [D], F32, kind="ExternalInput").ap()
    nbv = nc.dram_tensor("nbv", [D], F32, kind="ExternalInput").ap()
    bo = nc.dram_tensor("bo", [D], F32, kind="ExternalInput").ap()
    qgT = nc.dram_tensor("qgT", [128, 4], F32, kind="ExternalInput").ap()
    qbT = nc.dram_tensor("qbT", [128, 4], F32, kind="ExternalInput").ap()
    qg3 = nc.dram_tensor("qg3", [1, 128], F32, kind="ExternalInput").ap()
    qb3 = nc.dram_tensor("qb3", [1, 128], F32, kind="ExternalInput").ap()
    relE = nc.dram_tensor("relE", [32, 1], F32, kind="ExternalInput").ap()
    ohC = nc.dram_tensor("ohC", [32, 512], F32, kind="ExternalInput").ap()
    antiI = nc.dram_tensor("antiI", [128, 128], F32, kind="ExternalInput").ap()
    perm8 = nc.dram_tensor("perm8", [8, 24], F32, kind="ExternalInput").ap()
    nbp = nc.dram_tensor("nbp", [128, 8], F32, kind="ExternalInput").ap()
    bop = nc.dram_tensor("bop", [128, 8], F32, kind="ExternalInput").ap()
    gwT = nc.dram_tensor("gwT", [D, 8], F32, kind="ExternalInput").ap()
    w1 = nc.dram_tensor("w1", [NEXP, D, D], F32, kind="ExternalInput").ap()
    w2 = nc.dram_tensor("w2", [NEXP, D, D], F32, kind="ExternalInput").ap()
    w3 = nc.dram_tensor("w3", [NEXP, D, D], F32, kind="ExternalInput").ap()
    w1s = nc.dram_tensor("w1s", [D, D], F32, kind="ExternalInput").ap()
    w2s = nc.dram_tensor("w2s", [D, D], F32, kind="ExternalInput").ap()
    w3s = nc.dram_tensor("w3s", [D, D], F32, kind="ExternalInput").ap()

    oh8 = nc.dram_tensor("oh8", [8, 1024], F32, kind="ExternalInput").ap()
    outT = nc.dram_tensor("outT", [D, N], F32, kind="ExternalOutput").ap()
    dbg_a = nc.dram_tensor("dbg_a", [8, N], F32, kind="ExternalOutput").ap() if debug_a else None
    dbg_y = nc.dram_tensor("dbg_y", [DC, 128, N], F32, kind="ExternalOutput").ap() if debug_a else None

    vtm_d = nc.dram_tensor("vtm_d", [NT, 128, 4, HID], BF16, kind="Internal").ap()
    gate_d = nc.dram_tensor("gate_d", [NT, HC, 128, TA], BF16, kind="Internal").ap()
    og_d = nc.dram_tensor("og_d", [HC, 128, N], BF16, kind="Internal").ap()
    y_d = nc.dram_tensor("y_d", [DC, 128, N], F32, kind="Internal").ap()
    z_d = nc.dram_tensor("z_d", [DC, 128, N], BF16, kind="Internal").ap()
    a_d = nc.dram_tensor("a_d", [8, N], F32, kind="Internal").ap()
    grow_d = nc.dram_tensor("grow_d", [512], F32, kind="Internal").ap()

    xTr = xT.rearrange("(dc p) n -> p dc n", p=128)
    outTr = outT.rearrange("(dc p) n -> p dc n", p=128)

    from contextlib import ExitStack
    ctx = ExitStack()
    E = ctx.enter_context

    sems = {}
    for s in ("T", "V", "S", "P"):
        sems[s] = E(nc.semaphore("sem_" + s))
    for q, np_ in QPOOL.items():
        for j in range(np_):
            sems[f"{q}.{j}"] = E(nc.semaphore(f"sem_{q}_{j}"))

    PS = [E(nc.psum_tensor(f"PS{i}", [128, 512], F32)) for i in range(8)]
    arF = E(nc.sbuf_tensor("arF", [1, 5120], F32))
    arB = E(nc.sbuf_tensor("arB", [1, 4096], BF16))
    zcol = E(nc.sbuf_tensor("zcol", [128, 1], F32))
    epsr = E(nc.sbuf_tensor("epsr", [1, 1], F32))
    zrow = E(nc.sbuf_tensor("zrow", [1, 1], F32))
    ones_cf = E(nc.sbuf_tensor("ones_cf", [128, 1], F32))
    ones_cb = E(nc.sbuf_tensor("ones_cb", [128, 1], BF16))
    ones_rb = E(nc.sbuf_tensor("ones_rb", [1, 128], BF16))

    ctxABC = ExitStack()
    EABC = ctxABC.enter_context
    ng_row = EABC(nc.sbuf_tensor("ng_row", [1, D], BF16))
    bqk_rowb = EABC(nc.sbuf_tensor("bqk_rowb", [1, 128], BF16))
    bqk_col = EABC(nc.sbuf_tensor("bqk_col", [128, 1], F32))
    ones8f = EABC(nc.sbuf_tensor("ones8f", [1, 8], F32))
    o8col = EABC(nc.sbuf_tensor("o8col", [8, 1], F32))
    c4096 = EABC(nc.sbuf_tensor("c4096", [1, 128], F32))
    qg_pt = EABC(nc.sbuf_tensor("qg_pt", [128, 4], F32))
    qb_pt = EABC(nc.sbuf_tensor("qb_pt", [128, 4], F32))
    qgs = EABC(nc.sbuf_tensor("qgs", [128, 4], F32))
    qbs = EABC(nc.sbuf_tensor("qbs", [128, 4], F32))
    g3bc = EABC(nc.sbuf_tensor("g3bc", [128, 128], BF16))
    b3bc = EABC(nc.sbuf_tensor("b3bc", [128, 128], BF16))
    g3row = EABC(nc.sbuf_tensor("g3row", [1, 128], F32))
    b3row = EABC(nc.sbuf_tensor("b3row", [1, 128], F32))
    antiI_sb = EABC(nc.sbuf_tensor("antiI_sb", [128, 128], F32))
    H0 = EABC(nc.sbuf_tensor("H0", [128, 256], F32))
    H1 = EABC(nc.sbuf_tensor("H1", [128, 256], F32))
    gwT_sb = EABC(nc.sbuf_tensor("gwT_sb", [128, 8, 8], F32))
    srow = EABC(nc.sbuf_tensor("srow", [1, 8], F32))
    bhp_sb = EABC(nc.sbuf_tensor("bhp_sb", [128, 32], F32))
    perm8_sb = EABC(nc.sbuf_tensor("perm8_sb", [8, 24], F32))
    nbp_sb = EABC(nc.sbuf_tensor("nbp_sb", [128, 8], F32))
    bop_sb = EABC(nc.sbuf_tensor("bop_sb", [128, 8], F32))

    _ROW_SLOT = {
        "m1": 0, "vv": 1, "var": 2, "r1": 3, "v2": 4, "r2": 5, "Rr": 6, "Pp": 7, "tmp": 8,
        "m3": 0, "v3": 1, "t3r": 2, "r3": 3, "q3": 4, "mf": 5, "vf": 6, "tfr": 7,
        "rf": 8, "qf": 9, "nmf": 2, "sumr": 0, "rsum": 1, "mx1": 4, "mx2": 7,
        "m4": 0, "v4": 1, "t4r": 2, "r4": 3, "q4": 4,
    }
    def rowF(name, n=512):
        s = _ROW_SLOT[name]
        return arF[0:1, s * 512:s * 512 + n]
    _BROW_SLOT = {"r3b": 0, "q3b": 1, "rfb": 2, "qfb": 3, "r4b": 0, "q4b": 1,
                  "Rbs0": 4, "Pbs0": 5, "Rbs1": 6, "Pbs1": 7}
    def rowB(name, n=512):
        s = _BROW_SLOT[name]
        return arB[0:1, s * 512:s * 512 + n]

    o = em.op

    def dma(q, out, in_, reads, writes, tag="", serial=False, **kw):
        o(q, (lambda eng, out=out, in_=in_, kw=kw: eng.dma_start(out=out, in_=in_, **kw)),
          reads, writes, tag, serial=serial)

    o("V", lambda e: e.memset(zcol[:], 0.0), (), [("zcol",)])
    o("V", lambda e: e.memset(epsr[:], EPS), (), [("epsr",)])
    o("V", lambda e: e.memset(zrow[:], 0.0), (), [("zrow",)])
    o("V", lambda e: e.memset(ones_cf[:], 1.0), (), [("ones_cf",)])
    o("V", lambda e: e.memset(ones_cb[:], 1.0), (), [("ones_cb",)])
    o("V", lambda e: e.memset(ones_rb[:], 1.0), (), [("ones_rb",)])
    o("V", lambda e: e.memset(ones8f[:], 1.0), (), [("ones8f",)])
    o("V", lambda e: e.memset(o8col[:], 1.0), (), [("o8col",)])
    o("V", lambda e: e.memset(c4096[:], 1.0 / N), (), [("c4096",)])
    em.barrier("init")

    dma("QS", bhp_sb[:], bhp, [("d_in",)], [("bhp_sb",)])
    dma("QS", perm8_sb[:], perm8, [("d_in",)], [("perm8",)])
    dma("QS", nbp_sb[:], nbp, [("d_in",)], [("nbp_sb",)])
    dma("QS", bop_sb[:], bop, [("d_in",)], [("bop_sb",)])
    dma("QS", qg_pt[:], qgT, [("d_in",)], [("qg_pt",)])
    dma("QS", qb_pt[:], qbT, [("d_in",)], [("qb_pt",)])
    dma("QS", g3row[:], qg3, [("d_in",)], [("g3row",)])
    dma("QS", b3row[:], qb3, [("d_in",)], [("b3row",)])
    dma("QS", antiI_sb[:], antiI, [("d_in",)], [("antiI_sb",)])
    dma("QS", gwT_sb[:], gwT.rearrange("(kc p) e -> p kc e", p=128), [("d_in",)], [("gwT_sb",)])
    o("QS", lambda e: e.dma_start(out=bqk_col[:], in_=bqk.rearrange("(p u) -> p u", u=1),
                                  allow_slow_non_contiguous=True),
      [("d_in",)], [("bqk_col",)])
    dma("QG", ng_row[:], ngv.rearrange("(a n) -> a n", a=1), [("d_in",)], [("ng_row",)])
    dma("QG", bqk_rowb[:], bqk.rearrange("(a n) -> a n", a=1), [("d_in",)], [("bqk_rowb",)])

    for h, sc in ((0, 1.0 / GSZ), (1, 1.0), (2, 1.0)):
        o("V", lambda e, h=h, sc=sc: e.tensor_scalar_mul(qgs[:, h:h + 1], qg_pt[:, h:h + 1], sc),
          [("qg_pt",)], [("qgs",)])
        o("V", lambda e, h=h, sc=sc: e.tensor_scalar_mul(qbs[:, h:h + 1], qb_pt[:, h:h + 1], sc),
          [("qb_pt",)], [("qbs",)])
    o("T", lambda e: e.matmul(PS[0][:, 0:128], c4096[:], g3row[:], start=True, stop=True),
      [("c4096",), ("g3row",)], [("PS", 0)])
    o("V", lambda e: e.tensor_copy(g3bc[:], PS[0][:, 0:128]), [("PS", 0)], [("g3bc",)])
    o("T", lambda e: e.matmul(PS[1][:, 0:128], c4096[:], b3row[:], start=True, stop=True),
      [("c4096",), ("b3row",)], [("PS", 1)])
    o("V", lambda e: e.tensor_copy(b3bc[:], PS[1][:, 0:128]), [("PS", 1)], [("b3bc",)])
    for kc in range(8):
        o("T", lambda e, kc=kc: e.matmul(PS[3][0:1, 0:8], ones_cf[:], gwT_sb[:, kc, :],
                                         start=(kc == 0), stop=(kc == 7)),
          [("ones_cf",), ("gwT_sb",)], [("PS", 3)])
    o("V", lambda e: e.tensor_copy(srow[:], PS[3][0:1, 0:8]), [("PS", 3)], [("srow",)])

    sqfm = EABC(nc.sbuf_tensor("sqfm", [128, N], BF16))
    linkv = EABC(nc.sbuf_tensor("linkv", [128, HID], F32))
    linkvb = EABC(nc.sbuf_tensor("linkvb", [128, HID], BF16))
    r1b = EABC(nc.sbuf_tensor("r1b", [1, N], BF16))
    p1b = EABC(nc.sbuf_tensor("p1b", [1, N], BF16))

    ctxA = ExitStack()
    EA = ctxA.enter_context
    Wh_sb = EA(nc.sbuf_tensor("Wh_sb", [128, 8, 2 * HID], BF16))
    bh_row = EA(nc.sbuf_tensor("bh_row", [1, HID], BF16))
    ohC_sb = EA(nc.sbuf_tensor("ohC_sb", [32, 512], F32))
    rel_sb = EA(nc.sbuf_tensor("rel_sb", [32, 1], F32))
    grow_sb = EA(nc.sbuf_tensor("grow_sb", [1, 512], F32))
    Wqk_sb = EA(nc.sbuf_tensor("Wqk_sb", [128, 8, QKD], BF16))
    xt2 = [EA(nc.sbuf_tensor(f"xt2_{i}", [128, DC, TA], BF16)) for i in range(2)]
    normed2 = [EA(nc.sbuf_tensor(f"normed2_{i}", [128, DC, TA], BF16)) for i in range(2)]
    tnorm = [EA(nc.sbuf_tensor(f"tnorm{i}", [128, TA], F32)) for i in range(2)]
    sqtm = EA(nc.sbuf_tensor("sqtm", [128, 4, QKD], BF16))
    lktm = EA(nc.sbuf_tensor("lktm", [128, 4, QKD], BF16))
    lkt2 = EA(nc.sbuf_tensor("lkt2", [128, QKD], BF16))
    vtmb = [EA(nc.sbuf_tensor(f"vtmb{i}", [128, HID], BF16)) for i in range(2)]
    gateb = [EA(nc.sbuf_tensor(f"gateb{i}", [128, 2, TA], BF16)) for i in range(2)]

    dma("QG", bh_row[:], bh[0:HID].rearrange("(a n) -> a n", a=1), [("d_in",)], [("bh_row",)])
    dma("QS", ohC_sb[:], ohC, [("d_in",)], [("ohC_sb",)])
    o("QS", lambda e: e.dma_start(out=rel_sb[:], in_=relE, allow_slow_non_contiguous=True),
      [("d_in",)], [("rel_sb",)])
    o("T", lambda e: e.matmul(PS[2][0:1, :], rel_sb[:], ohC_sb[:], start=True, stop=True),
      [("rel_sb",), ("ohC_sb",)], [("PS", 2)])
    o("V", lambda e: e.tensor_scalar_mul(grow_sb[:], PS[2][0:1, :], math.sqrt(128.0)),
      [("PS", 2)], [("grow_sb",)])
    dma("QS", grow_d[0:512], grow_sb[:], [("grow_sb",)], [("grow_d",)])
    gsk0 = bass.AP(tensor=grow_d.tensor, offset=128, ap=[[1, 128], [1, 256]])
    gsk1 = bass.AP(tensor=grow_d.tensor, offset=0, ap=[[1, 128], [1, 256]])
    dma("QS", H0[:], gsk0, [("grow_d",)], [("H0",)])
    dma("QS", H1[:], gsk1, [("grow_d",)], [("H1",)])

    dma("QG", Wh_sb[:].rearrange("p kc (h m) -> p kc h m", m=2048),
        Wh.rearrange("(kc p) (h m) -> p kc h m", p=128, m=2048),
        [("d_in",)], [("Wh_sb",)])
    dma("QG", Wqk_sb[:], Wqk.rearrange("(kc p) m -> p kc m", p=128),
        [("d_in",)], [("Wqk_sb",)])

    def a_stats(t):
        par = t % 2
        xa = xt2[par]
        tok = t * TA
        dma("QG", xa[:], xTr[:, :, tok:tok + TA], [("d_in",)], [("xt", par)], tag=f"Ax{t}")
        xsq = normed2[par][:, 0, :]
        for dc in range(DC):
            o("S", lambda e, dc=dc, xa=xa, xsq=xsq: e.activation(out=xsq, in_=xa[:, dc, :], func=AF.Square, bias=zcol[:]),
              [("xt", par)], [("nsq", par)], f"A0sq{t}.{dc}")
            o("T", lambda e, dc=dc, xa=xa: e.matmul(PS[6][0:1, :], ones_cb[:], xa[:, dc, :],
                                                    start=(dc == 0), stop=(dc == DC - 1)),
              [("ones_cb",), ("xt", par)], [("PS", 6)], f"A0sx{t}.{dc}")
            o("T", lambda e, dc=dc, xsq=xsq: e.matmul(PS[7][0:1, :], ones_cb[:], xsq,
                                             start=(dc == 0), stop=(dc == DC - 1)),
              [("ones_cb",), ("nsq", par)], [("PS", 7)], f"A0sxx{t}.{dc}")

    def a_rows(t):
        par = t % 2
        tok = t * TA
        m1 = rowF("m1"); vv = rowF("vv"); var = rowF("var"); r1 = rowF("r1")
        v2 = rowF("v2"); r2 = rowF("r2"); Rr = rowF("Rr"); Pp = rowF("Pp")
        tmp = rowF("tmp")
        o("V", lambda e: e.tensor_scalar_mul(m1, PS[6][0:1, :], 1.0 / D), [("PS", 6)], [("R0",)])
        o("V", lambda e: e.tensor_scalar_mul(vv, PS[7][0:1, :], 1.0 / D), [("PS", 7)], [("R1",)])
        o("V", lambda e: e.tensor_tensor(out=tmp, in0=m1, in1=m1, op=OP.mult),
          [("R0",)], [("R8",)])
        o("V", lambda e: e.tensor_tensor(out=var, in0=vv, in1=tmp, op=OP.subtract),
          [("R1",), ("R8",)], [("R2",)])
        o("S", lambda e: e.activation(out=tmp, in_=var, func=AF.Ln, bias=epsr[:]),
          [("R2",)], [("R8",)])
        o("S", lambda e: e.activation(out=r1, in_=tmp, func=AF.Exp, scale=-0.5, bias=zrow[:]),
          [("R8",)], [("R3",)])
        o("V", lambda e: e.tensor_tensor(out=v2, in0=var, in1=r1, op=OP.mult),
          [("R2",), ("R3",)], [("R4",)])
        o("V", lambda e: e.tensor_tensor(out=v2, in0=v2, in1=r1, op=OP.mult),
          [("R4",), ("R3",)], [("R4",)])
        o("S", lambda e: e.activation(out=tmp, in_=v2, func=AF.Ln, bias=epsr[:]),
          [("R4",)], [("R8",)])
        o("S", lambda e: e.activation(out=r2, in_=tmp, func=AF.Exp, scale=-0.5, bias=zrow[:]),
          [("R8",)], [("R5",)])
        o("V", lambda e: e.tensor_tensor(out=Rr, in0=r1, in1=r2, op=OP.mult),
          [("R3",), ("R5",)], [("R6",)])
        o("V", lambda e: e.tensor_tensor(out=Pp, in0=m1, in1=Rr, op=OP.mult),
          [("R0",), ("R6",)], [("R7",)])
        Rbs = rowB(f"Rbs{par}"); Pbs = rowB(f"Pbs{par}")
        o("V", lambda e, Rbs=Rbs: e.tensor_copy(Rbs, Rr), [("R6",)], [("RBR", par)])
        o("V", lambda e, Pbs=Pbs: e.tensor_copy(Pbs, Pp), [("R7",)], [("RBP", par)])
        o("V", lambda e, tok=tok: e.tensor_copy(r1b[0:1, tok:tok + TA], r1), [("R3",)], [("r1b", t)])
        o("V", lambda e: e.tensor_tensor(out=tmp, in0=m1, in1=r1, op=OP.mult),
          [("R0",), ("R3",)], [("R8",)])
        o("V", lambda e, tok=tok: e.tensor_copy(p1b[0:1, tok:tok + TA], tmp), [("R8",)], [("p1b", t)])

    def a_normed(t):
        par = t % 2
        xa = xt2[par]
        normed = normed2[par]
        Rbs = rowB(f"Rbs{par}"); Pbs = rowB(f"Pbs{par}")
        for dc in range(DC):
            pa, pb = (0, 1) if dc % 2 == 0 else (2, 3)
            gsl = ng_row[0:1, dc * 128:(dc + 1) * 128]
            o("T", lambda e, pa=pa, gsl=gsl, Rbs=Rbs: e.matmul(
                PS[pa][:], gsl, Rbs, start=True, stop=True),
              [("ng_row",), ("RBR", par)], [("PS", pa)], f"Aba{t}.{dc}")
            o("T", lambda e, pb=pb, gsl=gsl, Pbs=Pbs: e.matmul(
                PS[pb][:], gsl, Pbs, start=True, stop=True),
              [("ng_row",), ("RBP", par)], [("PS", pb)], f"Abb1{t}.{dc}")
            sc = tnorm[dc % 2]
            o("V", lambda e, dc=dc, pa=pa, sc=sc, xa=xa: e.tensor_tensor(
                out=sc[:], in0=xa[:, dc, :], in1=PS[pa][:], op=OP.mult),
              [("xt", par), ("PS", pa)], [("tn", dc % 2)], f"An1{t}.{dc}")
            o("V", lambda e, dc=dc, pb=pb, sc=sc, normed=normed: e.scalar_tensor_tensor(
                out=normed[:, dc, :], in0=sc[:], scalar=nbp_sb[:, dc:dc + 1],
                in1=PS[pb][:], op0=OP.add, op1=OP.subtract),
              [("tn", dc % 2), ("PS", pb), ("nbp_sb",)],
              [("normed", par, dc), ("nsq", par)], f"An2{t}.{dc}")

    def a_proj(t):
        par = t % 2
        normed = normed2[par]
        tok = t * TA
        vg_ps = [0]
        def nxt_ps():
            vg_ps[0] = (vg_ps[0] + 1) % 4
            return 4 + vg_ps[0]
        for kc in range(DC):
            o("T", lambda e, kc=kc: e.matmul(PS[4][:], Wqk_sb[:, kc, :],
                                             normed[:, kc, :], start=(kc == 0), stop=(kc == DC - 1)),
              [("Wqk_sb",), ("normed", par, kc)], [("PS", 4)], f"Aqk{t}.{kc}")
        o("S", lambda e, tok=tok: e.activation(out=sqfm[:, tok:tok + TA], in_=PS[4][:],
                                               func=AF.Silu, bias=bqk_col[:]),
          [("PS", 4), ("bqk_col",)], [("sqfm", t)], f"Aqke{t}")
        for tc in range(4):
            tsl = slice(tc * 128, (tc + 1) * 128)
            for kc in range(DC):
                o("T", lambda e, kc=kc, tsl=tsl: e.matmul(
                    PS[5][:, 0:QKD], normed[:, kc, tsl], Wqk_sb[:, kc, :],
                    start=(kc == 0), stop=False),
                  [("normed", par, kc), ("Wqk_sb",)], [("PS", 5)], f"Aqt{t}.{tc}.{kc}")
            o("T", lambda e: e.matmul(PS[5][:, 0:QKD], ones_rb[:], bqk_rowb[:],
                                      start=False, stop=True),
              [("ones_rb",), ("bqk_rowb",)], [("PS", 5)], f"Aqtb{t}.{tc}")
            o("S", lambda e, tc=tc: e.activation(out=sqtm[:, tc, :], in_=PS[5][:, 0:QKD],
                                                 func=AF.Silu, bias=zcol[:]),
              [("PS", 5)], [("sqtm", tc)], f"Aqte{t}.{tc}")
            o("V", lambda e, tc=tc: e.tensor_tensor(out=lkt2[:], in0=sqtm[:, tc, :],
                                                    in1=g3bc[:], op=OP.mult),
              [("sqtm", tc), ("g3bc",)], [("lkt2",)], f"Alk1{t}.{tc}")
            o("V", lambda e, tc=tc: e.tensor_tensor(out=lktm[:, tc, :], in0=lkt2[:],
                                                    in1=b3bc[:], op=OP.add),
              [("lkt2",), ("b3bc",)], [("lktm", tc)], f"Alk2{t}.{tc}")
            par2 = (t * 4 + tc) % 2
            vb = vtmb[par2]
            for Ncc in range(4):
                ps = nxt_ps()
                for kc in range(DC):
                    o("T", lambda e, kc=kc, tsl=tsl, Ncc=Ncc, ps=ps: e.matmul(
                        PS[ps][:], normed[:, kc, tsl], Wh_sb[:, kc, Ncc * 512:(Ncc + 1) * 512],
                        start=(kc == 0), stop=False),
                      [("normed", par, kc), ("Wh_sb",)], [("PS", ps)], f"Av{t}.{tc}.{Ncc}.{kc}")
                o("T", lambda e, Ncc=Ncc, ps=ps: e.matmul(
                    PS[ps][:], ones_rb[:], bh_row[0:1, Ncc * 512:(Ncc + 1) * 512],
                    start=False, stop=True),
                  [("ones_rb",), ("bh_row",)], [("PS", ps)], f"Avb{t}.{tc}.{Ncc}")
                o("S", lambda e, Ncc=Ncc, ps=ps, vb=vb: e.activation(
                    out=vb[:, Ncc * 512:(Ncc + 1) * 512], in_=PS[ps][:], func=AF.Silu, bias=zcol[:]),
                  [("PS", ps)], [("vtmb", par2, Ncc)], f"Ave{t}.{tc}.{Ncc}")
            for Ncc in range(4):
                o("T", lambda e, tc=tc, Ncc=Ncc, vb=vb: e.matmul(
                    PS[Ncc][:], lktm[:, tc, :], vb[:, Ncc * 512:(Ncc + 1) * 512],
                    start=(tc == 0), stop=(tc == 3)),
                  [("lktm", tc), ("vtmb", par2, Ncc)], [("PS", Ncc)], f"Alkv{t}.{tc}.{Ncc}")
            dma("QS", vtm_d[t, :, tc, :], vb[:], [("vtmb", par2, 0), ("vtmb", par2, 1),
                ("vtmb", par2, 2), ("vtmb", par2, 3)], [("vtm_d", t, tc)], tag=f"Avs{t}.{tc}")
        for Ncc in range(4):
            if t == 0:
                o("V", lambda e, Ncc=Ncc: e.tensor_copy(linkv[:, Ncc * 512:(Ncc + 1) * 512], PS[Ncc][:]),
                  [("PS", Ncc)], [("linkv", Ncc)], f"Alkc{t}.{Ncc}")
            else:
                o("V", lambda e, Ncc=Ncc: e.tensor_tensor(
                    out=linkv[:, Ncc * 512:(Ncc + 1) * 512],
                    in0=linkv[:, Ncc * 512:(Ncc + 1) * 512], in1=PS[Ncc][:], op=OP.add),
                  [("PS", Ncc), ("linkv", Ncc)], [("linkv", Ncc)], f"Alka{t}.{Ncc}")
        def gates():
          for hc in range(HC):
            ps = nxt_ps()
            gb = gateb[(hc // 2) % 2]
            for kc in range(DC):
                o("T", lambda e, kc=kc, hc=hc, ps=ps: e.matmul(
                    PS[ps][:], Wh_sb[:, kc, HID + hc * 128:HID + (hc + 1) * 128],
                    normed[:, kc, :], start=(kc == 0), stop=(kc == DC - 1)),
                  [("Wh_sb",), ("normed", par, kc)], [("PS", ps)], f"Ag{t}.{hc}.{kc}")
            o("S", lambda e, hc=hc, ps=ps, gb=gb: e.activation(
                out=gb[:, hc % 2, :], in_=PS[ps][:], func=AF.Silu,
                bias=bhp_sb[:, 16 + hc:17 + hc]),
              [("PS", ps), ("bhp_sb",)], [("gateb", (hc // 2) % 2, hc % 2)], f"Age{t}.{hc}")
            if hc % 2 == 1:
                h2 = hc // 2
                gb2 = gateb[h2 % 2]
                dma("QS", gate_d[t].rearrange("hc p n -> p hc n")[:, h2 * 2:(h2 + 1) * 2, :],
                    gb2[:], [("gateb", h2 % 2, 0), ("gateb", h2 % 2, 1)],
                    [("gate_d", t, h2)], tag=f"Ags{t}.{h2}")
        return gates

    if "A" in run_phases:
        a_stats(0)
        a_rows(0)
        a_normed(0)
        for t in range(NT):
            if t + 1 < NT:
                a_stats(t + 1)
                a_rows(t + 1)
            gates = a_proj(t)
            if t + 1 < NT:
                a_normed(t + 1)
            gates()
        for Ncc in range(4):
            o("V", lambda e, Ncc=Ncc: e.tensor_copy(linkvb[:, Ncc * 512:(Ncc + 1) * 512],
                                                    linkv[:, Ncc * 512:(Ncc + 1) * 512]),
              [("linkv", Ncc)], [("linkvb",)], f"Alkb{Ncc}")

    em.barrier("A|B")
    ctxA.close()

    ctxB = ExitStack()
    EB = ctxB.enter_context
    vtmg = [EB(nc.sbuf_tensor(f"vtmg{i}", [128, 2, HID], BF16)) for i in range(2)]
    gateg = [EB(nc.sbuf_tensor(f"gateg{i}", [128, HC, 256], BF16)) for i in range(2)]
    hq = [EB(nc.sbuf_tensor(f"hq{i}", [128, 256], BF16)) for i in range(2)]
    hk = [EB(nc.sbuf_tensor(f"hk{i}", [128, 256], BF16)) for i in range(2)]
    hlq = [EB(nc.sbuf_tensor(f"hlq{i}", [128, 256], BF16)) for i in range(2)]
    rsc = [EB(nc.sbuf_tensor(f"rsc{i}", [128, 256], F32)) for i in range(2)]
    attnT = [EB(nc.sbuf_tensor(f"attnT{i}", [128, 256], BF16)) for i in range(4)]
    ogb = [EB(nc.sbuf_tensor(f"ogb{i}", [128, HC, 256], BF16)) for i in range(2)]

    def b_pre(g):
            t, half = g // 2, g % 2
            tok = g * GSZ
            par = g % 2
            vg = vtmg[par]; gg = gateg[par]
            dma("QA", vg[:], vtm_d[t, :, 2 * half:2 * half + 2, :],
                [("vtm_d", t, 2 * half), ("vtm_d", t, 2 * half + 1)], [("vtmg", par)], tag=f"Bv{g}")
            dma("QA", gg[:], gate_d[t].rearrange("hc p n -> p hc n")[:, :, half * 256:half * 256 + 256],
                [("gate_d", t, h8) for h8 in range(8)],
                [("gateg", par)], tag=f"Bg{g}")
            sq = sqfm[:, tok:tok + 256]
            o("V", lambda e, sq=sq, par=par: e.tensor_scalar(
                out=hq[par][:], in0=sq, scalar1=qgs[:, 0:1], scalar2=qbs[:, 0:1],
                op0=OP.mult, op1=OP.add),
              [("sqfm", t), ("qgs",), ("qbs",)], [("hq", par)], f"Bhq{g}")
            o("V", lambda e, sq=sq, par=par: e.tensor_scalar(
                out=hlq[par][:], in0=sq, scalar1=qgs[:, 1:2], scalar2=qbs[:, 1:2],
                op0=OP.mult, op1=OP.add),
              [("sqfm", t), ("qgs",), ("qbs",)], [("hlq", par)], f"Bhlq{g}")
            o("V", lambda e, sq=sq, par=par: e.tensor_scalar(
                out=hk[par][:], in0=sq, scalar1=qgs[:, 2:3], scalar2=qbs[:, 2:3],
                op0=OP.mult, op1=OP.add),
              [("sqfm", t), ("qgs",), ("qbs",)], [("hk", par)], f"Bhk{g}")
            for jc in range(2):
                ps = jc
                Hj = H0 if jc == 0 else H1
                o("T", lambda e, ps=ps, par=par, jc=jc: e.matmul(
                    PS[ps][:, 0:256], hk[par][:, jc * 128:(jc + 1) * 128], hq[par][:],
                    start=True, stop=False),
                  [("hk", par), ("hq", par)], [("PS", ps)], f"Bsim{g}.{jc}")
                o("T", lambda e, ps=ps, Hj=Hj: e.matmul(
                    PS[ps][:, 0:256], antiI_sb[:], Hj[:], start=False, stop=True),
                  [("antiI_sb",), ("H0",), ("H1",)], [("PS", ps)], f"Bbias{g}.{jc}")
                o("S", lambda e, ps=ps, jc=jc: e.activation(out=rsc[jc][:], in_=PS[ps][:, 0:256],
                                                            func=AF.Relu, bias=zcol[:]),
                  [("PS", ps)], [("rsc", jc)], f"Brelu{g}.{jc}")
                at = attnT[par * 2 + jc]
                o("V", lambda e, jc=jc, at=at: e.tensor_tensor(out=at[:], in0=rsc[jc][:],
                                                               in1=rsc[jc][:], op=OP.mult),
                  [("rsc", jc)], [("attnT", par * 2 + jc)], f"Bsq{g}.{jc}")
    def b_qolo(g):
            t, half = g // 2, g % 2
            tok = g * GSZ
            par = g % 2
            ob = ogb[par]
            for ec in range(HC):
                ps = 2 + ec % 4
                o("T", lambda e, ec=ec, ps=ps, par=par: e.matmul(
                    PS[ps][:, 0:256], vtmg[par][:, 0, ec * 128:(ec + 1) * 128], attnT[par * 2][:],
                    start=True, stop=False),
                  [("vtmg", par), ("attnT", par * 2)], [("PS", ps)], f"Bqo1{g}.{ec}")
                o("T", lambda e, ec=ec, ps=ps, par=par: e.matmul(
                    PS[ps][:, 0:256], vtmg[par][:, 1, ec * 128:(ec + 1) * 128], attnT[par * 2 + 1][:],
                    start=False, stop=False),
                  [("vtmg", par), ("attnT", par * 2 + 1)], [("PS", ps)], f"Bqo2{g}.{ec}")
                o("T", lambda e, ec=ec, ps=ps, par=par: e.matmul(
                    PS[ps][:, 0:256], linkvb[:, ec * 128:(ec + 1) * 128], hlq[par][:],
                    start=False, stop=True),
                  [("linkvb",), ("hlq", par)], [("PS", ps)], f"Blo{g}.{ec}")
                o("V", lambda e, ec=ec, ps=ps, par=par, ob=ob: e.tensor_tensor(
                    out=ob[:, ec, :], in0=gateg[par][:, ec, :], in1=PS[ps][:, 0:256], op=OP.mult),
                  [("gateg", par), ("PS", ps)], [("ogb", par, ec)], f"Bog{g}.{ec}")
            dma("QS", og_d.rearrange("ec p n -> p ec n")[:, :, tok:tok + 256], ob[:],
                [("ogb", par, ec2) for ec2 in range(HC)], [("og_d", g)], tag=f"Bogs{g}")

    if "B" in run_phases:
        b_pre(0)
        for g in range(NG):
            if g + 1 < NG:
                b_pre(g + 1)
            b_qolo(g)

    em.barrier("B|C")
    ctxB.close()

    ctxC = ExitStack()
    EC = ctxC.enter_context
    Wo_sb = EC(nc.sbuf_tensor("Wo_sb", [128, HC, D], BF16))
    ogt = EC(nc.sbuf_tensor("ogt", [128, HC, TA], BF16))
    xtc2 = [EC(nc.sbuf_tensor(f"xtc2_{i}", [128, DC, TA], BF16)) for i in range(2)]
    ao2 = [EC(nc.sbuf_tensor(f"ao2_{i}", [128, DC, TA], BF16)) for i in range(2)]
    yt = EC(nc.sbuf_tensor("yt", [128, DC, TA], F32))
    zt2 = [EC(nc.sbuf_tensor(f"zt2_{i}", [128, 2 * TA], BF16)) for i in range(2)]
    a1sb = EC(nc.sbuf_tensor("a1sb", [128, TA], BF16))
    pb1sb = EC(nc.sbuf_tensor("pb1sb", [128, TA], BF16))
    a3sb = EC(nc.sbuf_tensor("a3sb", [128, TA], BF16))
    b3sb = EC(nc.sbuf_tensor("b3sb", [128, TA], BF16))
    afsb = EC(nc.sbuf_tensor("afsb", [128, TA], BF16))
    zsc = EC(nc.sbuf_tensor("zsc", [128, 2 * TA], F32))
    bfsb = EC(nc.sbuf_tensor("bfsb", [128, TA], BF16))
    tc1b = EC(nc.sbuf_tensor("tc1b", [128, 2 * TA], F32))
    tc2b = EC(nc.sbuf_tensor("tc2b", [128, 2 * TA], F32))
    casb = EC(nc.sbuf_tensor("casb", [128, 2 * TA], BF16))
    cpl = EC(nc.sbuf_tensor("cpl", [128, 2 * TA], BF16))
    lgsb = EC(nc.sbuf_tensor("lgsb", [8, TA], F32))
    esb = EC(nc.sbuf_tensor("esb", [8, TA], F32))
    scsb = EC(nc.sbuf_tensor("scsb", [8, TA], F32))
    mk1 = EC(nc.sbuf_tensor("mk1", [8, TA], BF16))
    afm = EC(nc.sbuf_tensor("afm", [8, TA], F32))

    dma("QG", Wo_sb[:], Wo.rearrange("(ec p) m -> p ec m", p=128),
        [("d_in",)], [("Wo_sb",)])


    def rep2(tens, n=TA):
        return bass.AP(tensor=tens.ap().tensor, offset=0, ap=[[n, 128], [0, 2], [1, n]])

    def c_xload(t):
        par = t % 2
        tok = t * TA
        dma("QG", xtc2[par][:], xTr[:, :, tok:tok + TA], [("d_in",)], [("xtc", par)], tag=f"Cx{t}")

    def c_head(t):
        tok = t * TA
        dma("QA", ogt[:], og_d.rearrange("ec p n -> p ec n")[:, :, tok:tok + TA],
            [("og_d", 2 * t), ("og_d", 2 * t + 1)], [("ogt",)], tag=f"Cog{t}")
        o("T", lambda e, tok=tok: e.matmul(PS[2][:], ones_rb[:], r1b[0:1, tok:tok + TA],
                                           start=True, stop=True),
          [("ones_rb",), ("r1b", t)], [("PS", 2)], f"Ca1{t}")
        o("S", lambda e: e.activation(out=a1sb[:], in_=PS[2][:], func=AF.Copy),
          [("PS", 2)], [("a1sb",)], f"Ca1c{t}")
        o("T", lambda e, tok=tok: e.matmul(PS[2][:], ones_rb[:], p1b[0:1, tok:tok + TA],
                                           start=True, stop=True),
          [("ones_rb",), ("p1b", t)], [("PS", 2)], f"Cb1{t}")
        o("S", lambda e: e.activation(out=pb1sb[:], in_=PS[2][:], func=AF.Copy),
          [("PS", 2)], [("pb1sb",)], f"Cb1c{t}")

    def c_wo_parts(t):
        par = t % 2
        xc = xtc2[par]
        aot = ao2[par]
        ps_s, ps_q = (3, 4) if par == 0 else (6, 7)
        parts = []

        def mk(dc):
            def th():
                for d2 in (dc, dc + 1):
                    pwo = d2 % 2
                    for ec in range(HC):
                        o("T", lambda e, d2=d2, ec=ec, pwo=pwo: e.matmul(
                            PS[pwo][:], Wo_sb[:, ec, d2 * 128:(d2 + 1) * 128], ogt[:, ec, :],
                            start=(ec == 0), stop=(ec == HC - 1)),
                          [("Wo_sb",), ("ogt",)], [("PS", pwo)], f"Cwo{t}.{d2}.{ec}")
                o("V", lambda e, dc=dc, xc=xc: e.tensor_tensor(
                    out=tc1b[:].rearrange("p (a n) -> p a n", a=2), in0=xc[:, dc:dc + 2, :],
                    in1=rep2(a1sb), op=OP.mult),
                  [("xtc", par), ("a1sb",)], [("tc1b",)], f"Cu1{t}.{dc}")
                for d2 in (dc, dc + 1):
                    pwo = d2 % 2
                    o("V", lambda e, d2=d2, pwo=pwo: e.scalar_tensor_tensor(
                        out=tc2b[:, (d2 - dc) * TA:(d2 - dc + 1) * TA],
                        in0=tc1b[:, (d2 - dc) * TA:(d2 - dc + 1) * TA],
                        scalar=bop_sb[:, d2:d2 + 1], in1=PS[pwo][:],
                        op0=OP.add, op1=OP.add),
                      [("tc1b",), ("PS", pwo), ("bop_sb",)], [("tc2b",)], f"Cu2{t}.{d2}")
                o("V", lambda e, dc=dc, aot=aot: e.tensor_tensor(
                    out=aot[:, dc:dc + 2, :], in0=tc2b[:].rearrange("p (a n) -> p a n", a=2),
                    in1=rep2(pb1sb), op=OP.subtract),
                  [("tc2b",), ("pb1sb",)], [("ao", par, dc), ("ao", par, dc + 1)], f"Cao{t}.{dc}")
                o("S", lambda e, dc=dc, aot=aot: e.activation(
                    out=casb[:], in_=aot[:, dc:dc + 2, :].rearrange("p a n -> p (a n)"),
                    func=AF.Square, bias=zcol[:]),
                  [("ao", par, dc), ("ao", par, dc + 1)], [("casb",)], f"Csq{t}.{dc}")
                for d2 in (dc, dc + 1):
                    o("T", lambda e, d2=d2, dc=dc, ps_q=ps_q: e.matmul(
                        PS[ps_q][0:1, :], ones_cb[:], casb[:, (d2 - dc) * TA:(d2 - dc + 1) * TA],
                        start=(d2 == 0), stop=(d2 == DC - 1)),
                      [("ones_cb",), ("casb",)], [("PS", ps_q)], f"Css{t}.{d2}")
                    o("T", lambda e, d2=d2, ps_s=ps_s, aot=aot: e.matmul(
                        PS[ps_s][0:1, :], ones_cb[:], aot[:, d2, :],
                        start=(d2 == 0), stop=(d2 == DC - 1)),
                      [("ones_cb",), ("ao", par, d2)], [("PS", ps_s)], f"Csm{t}.{d2}")
            return th
        for dc in range(0, DC, 2):
            parts.append(mk(dc))
        return parts

    def c_mid_parts(t):
        par = t % 2
        xc = xtc2[par]
        aot = ao2[par]
        ps_s, ps_q = (3, 4) if par == 0 else (6, 7)
        tok = t * TA
        parts = []

        def head():
            m3 = rowF("m3"); v3 = rowF("v3"); t3r = rowF("t3r"); r3 = rowF("r3"); q3 = rowF("q3")
            o("V", lambda e, ps_s=ps_s: e.tensor_scalar_mul(m3, PS[ps_s][0:1, :], 1.0 / D), [("PS", ps_s)], [("R0",)])
            o("V", lambda e, ps_q=ps_q: e.tensor_scalar_mul(v3, PS[ps_q][0:1, :], 1.0 / D), [("PS", ps_q)], [("R1",)])
            o("V", lambda e: e.tensor_tensor(out=t3r, in0=m3, in1=m3, op=OP.mult), [("R0",)], [("R2",)])
            o("V", lambda e: e.tensor_tensor(out=v3, in0=v3, in1=t3r, op=OP.subtract),
              [("R1",), ("R2",)], [("R1",)])
            o("S", lambda e: e.activation(out=t3r, in_=v3, func=AF.Ln, bias=epsr[:]), [("R1",)], [("R2",)])
            o("S", lambda e: e.activation(out=r3, in_=t3r, func=AF.Exp, scale=-0.5, bias=zrow[:]),
              [("R2",)], [("R3",)])
            o("V", lambda e: e.tensor_tensor(out=q3, in0=m3, in1=r3, op=OP.mult),
              [("R0",), ("R3",)], [("R4",)])
            r3b = rowB("r3b"); q3b = rowB("q3b")
            o("V", lambda e: e.tensor_copy(r3b, r3), [("R3",)], [("RB0",)])
            o("V", lambda e: e.tensor_copy(q3b, q3), [("R4",)], [("RB1",)])
            o("T", lambda e: e.matmul(PS[2][:], ones_rb[:], r3b, start=True, stop=True),
              [("ones_rb",), ("RB0",)], [("PS", 2)], f"Ca3{t}")
            o("S", lambda e: e.activation(out=a3sb[:], in_=PS[2][:], func=AF.Copy),
              [("PS", 2)], [("a3sb",)], f"Ca3c{t}")
            o("T", lambda e: e.matmul(PS[2][:], ones_rb[:], q3b, start=True, stop=True),
              [("ones_rb",), ("RB1",)], [("PS", 2)], f"Cb3{t}")
            o("S", lambda e: e.activation(out=b3sb[:], in_=PS[2][:], func=AF.Copy),
              [("PS", 2)], [("b3sb",)], f"Cb3c{t}")
        parts.append(head)

        def mkdc(dc):
            def th():
                o("V", lambda e, dc=dc, aot=aot: e.tensor_tensor(
                    out=tc1b[:].rearrange("p (a n) -> p a n", a=2), in0=aot[:, dc:dc + 2, :],
                    in1=rep2(a3sb), op=OP.mult),
                  [("ao", par, dc), ("ao", par, dc + 1), ("a3sb",)], [("tc1b",)], f"Cy1{t}.{dc}")
                o("V", lambda e, dc=dc: e.tensor_tensor(
                    out=tc2b[:].rearrange("p (a n) -> p a n", a=2),
                    in0=tc1b[:].rearrange("p (a n) -> p a n", a=2),
                    in1=rep2(b3sb), op=OP.subtract),
                  [("tc1b",), ("b3sb",)], [("tc2b",)], f"Cy2{t}.{dc}")
                o("V", lambda e, dc=dc, xc=xc: e.tensor_tensor(
                    out=yt[:, dc:dc + 2, :], in0=tc2b[:].rearrange("p (a n) -> p a n", a=2),
                    in1=xc[:, dc:dc + 2, :], op=OP.add),
                  [("tc2b",), ("xtc", par)], [("yt", dc), ("yt", dc + 1)], f"Cy3{t}.{dc}")
                o("S", lambda e, dc=dc: e.activation(
                    out=casb[:], in_=yt[:, dc:dc + 2, :].rearrange("p a n -> p (a n)"),
                    func=AF.Square, bias=zcol[:]),
                  [("yt", dc), ("yt", dc + 1)], [("casb",)], f"Cysq{t}.{dc}")
                o("V", lambda e, dc=dc: e.tensor_copy(
                    cpl[:], yt[:, dc:dc + 2, :].rearrange("p a n -> p (a n)")),
                  [("yt", dc), ("yt", dc + 1)], [("cpl",)], f"Cyc{t}.{dc}")
                for d2 in (dc, dc + 1):
                    o("T", lambda e, d2=d2, dc=dc, ps_q=ps_q: e.matmul(
                        PS[ps_q][0:1, :], ones_cb[:], casb[:, (d2 - dc) * TA:(d2 - dc + 1) * TA],
                        start=(d2 == 0), stop=(d2 == DC - 1)),
                      [("ones_cb",), ("casb",)], [("PS", ps_q)], f"Cyss{t}.{d2}")
                    o("T", lambda e, d2=d2, dc=dc, ps_s=ps_s: e.matmul(
                        PS[ps_s][0:1, :], ones_cb[:], cpl[:, (d2 - dc) * TA:(d2 - dc + 1) * TA],
                        start=(d2 == 0), stop=(d2 == DC - 1)),
                      [("ones_cb",), ("cpl",)], [("PS", ps_s)], f"Cysm{t}.{d2}")
                    o("T", lambda e, d2=d2: e.matmul(PS[5][0:8, :], gwT_sb[:, d2, :], yt[:, d2, :],
                                                     start=(d2 == 0), stop=False),
                      [("gwT_sb",), ("yt", d2)], [("PS", 5)], f"Clg{t}.{d2}")
            return th
        for dc in range(0, DC, 2):
            parts.append(mkdc(dc))

        def tail():
            dma("QS", y_d.rearrange("dc p n -> p dc n")[:, :, tok:tok + TA], yt[:],
                [("yt", dc2) for dc2 in range(DC)], [("y_d", t)], tag=f"Cys{t}")
            mf = rowF("mf"); vf = rowF("vf"); tfr = rowF("tfr"); rf = rowF("rf")
            qf = rowF("qf"); nmf = rowF("nmf")
            o("V", lambda e, ps_s=ps_s: e.tensor_scalar_mul(mf, PS[ps_s][0:1, :], 1.0 / D), [("PS", ps_s)], [("R5",)])
            o("V", lambda e, ps_q=ps_q: e.tensor_scalar_mul(vf, PS[ps_q][0:1, :], 1.0 / D), [("PS", ps_q)], [("R6",)])
            o("V", lambda e: e.tensor_tensor(out=tfr, in0=mf, in1=mf, op=OP.mult), [("R5",)], [("R7",)])
            o("V", lambda e: e.tensor_tensor(out=vf, in0=vf, in1=tfr, op=OP.subtract),
              [("R6",), ("R7",)], [("R6",)])
            o("S", lambda e: e.activation(out=tfr, in_=vf, func=AF.Ln, bias=epsr[:]), [("R6",)], [("R7",)])
            o("S", lambda e: e.activation(out=rf, in_=tfr, func=AF.Exp, scale=-0.5, bias=zrow[:]),
              [("R7",)], [("R8",)])
            o("V", lambda e: e.tensor_tensor(out=qf, in0=mf, in1=rf, op=OP.mult),
              [("R5",), ("R8",)], [("R9",)])
            o("V", lambda e: e.tensor_scalar_mul(nmf, mf, -1.0), [("R5",)], [("R2",)])
            rfb = rowB("rfb"); qfb = rowB("qfb")
            o("V", lambda e: e.tensor_copy(rfb, rf), [("R8",)], [("RB2",)])
            o("V", lambda e: e.tensor_copy(qfb, qf), [("R9",)], [("RB3",)])
            o("T", lambda e: e.matmul(PS[5][0:8, :], srow[:], nmf, start=False, stop=True),
              [("srow",), ("R2",)], [("PS", 5)], f"Clgf{t}")
        parts.append(tail)
        return parts

    def c_epi_parts(t):
        tok = t * TA
        parts = []

        def bcasts():
            o("T", lambda e: e.matmul(PS[2][:], ones_rb[:], rowB("rfb"), start=True, stop=True),
              [("ones_rb",), ("RB2",)], [("PS", 2)], f"Caf{t}")
            o("S", lambda e: e.activation(out=afsb[:], in_=PS[2][:], func=AF.Copy),
              [("PS", 2)], [("afsb",)], f"Cafc{t}")
            o("T", lambda e: e.matmul(PS[2][:], ones_rb[:], rowB("qfb"), start=True, stop=True),
              [("ones_rb",), ("RB3",)], [("PS", 2)], f"Cbf{t}")
            o("S", lambda e: e.activation(out=bfsb[:], in_=PS[2][:], func=AF.Copy),
              [("PS", 2)], [("bfsb",)], f"Cbfc{t}")
        parts.append(bcasts)

        def mkz(dc0):
            def th():
                zb = zt2[(dc0 // 2) % 2]
                o("P", lambda e, dc0=dc0: e.tensor_tensor(
                    out=zsc[:].rearrange("p (a n) -> p a n", a=2), in0=yt[:, dc0:dc0 + 2, :],
                    in1=rep2(afsb), op=OP.mult),
                  [("yt", dc0), ("yt", dc0 + 1), ("afsb",)], [("zsc",)], f"Cz1{t}.{dc0}")
                o("P", lambda e, dc0=dc0, zb=zb: e.tensor_tensor(
                    out=zb[:].rearrange("p (a n) -> p a n", a=2),
                    in0=zsc[:].rearrange("p (a n) -> p a n", a=2),
                    in1=rep2(bfsb), op=OP.subtract),
                  [("zsc",), ("bfsb",)], [("zt2", (dc0 // 2) % 2)], f"Cz2{t}.{dc0}")
                dma("QS", z_d.rearrange("dc p n -> p dc n")[:, dc0:dc0 + 2, tok:tok + TA], zb[:].rearrange("p (a n) -> p a n", a=2),
                    [("zt2", (dc0 // 2) % 2)], [("z_d", t, dc0), ("z_d", t, dc0 + 1)], tag=f"Czs{t}.{dc0}")
            return th
        for dc0 in range(0, DC, 2):
            parts.append(mkz(dc0))

        def router1():
            o("V", lambda e: e.tensor_copy(esb[:], PS[5][0:8, :]), [("PS", 5)], [("esb",)], f"Cl0{t}")
            o("T", lambda e: e.matmul(PS[2][0:8, :], ones8f[:], rowF("rf"), start=True, stop=True),
              [("ones8f",), ("R8",)], [("PS", 2)], f"Crb{t}")
            o("V", lambda e: e.tensor_tensor(out=lgsb[:], in0=esb[:], in1=PS[2][0:8, :], op=OP.mult),
              [("esb",), ("PS", 2)], [("lgsb",)], f"Clg2{t}")
            o("S", lambda e: e.activation(out=esb[:], in_=lgsb[:], func=AF.Exp, bias=zcol[0:8, :]),
              [("lgsb",)], [("esb",)], f"Cexp{t}")
            o("T", lambda e: e.matmul(PS[5][0:1, :], o8col[:], esb[:], start=True, stop=True),
              [("o8col",), ("esb",)], [("PS", 5)], f"Cse{t}")
            sumr = rowF("sumr"); rsum = rowF("rsum")
            o("V", lambda e: e.tensor_copy(sumr, PS[5][0:1, :]), [("PS", 5)], [("R0",)])
            o("V", lambda e: e.reciprocal(rsum, sumr), [("R0",)], [("R1",)])
            o("T", lambda e: e.matmul(PS[2][0:8, :], ones8f[:], rsum, start=True, stop=True),
              [("ones8f",), ("R1",)], [("PS", 2)], f"Crs{t}")
            o("V", lambda e: e.tensor_tensor(out=scsb[:], in0=esb[:], in1=PS[2][0:8, :], op=OP.mult),
              [("esb",), ("PS", 2)], [("scsb",)], f"Csc{t}")
        parts.append(router1)

        def tmax(tag, src, skey, sc_a, akey, sc_b, bkey):
            o("T", lambda e, src=src: e.matmul(PS[5][0:8, :], perm8_sb[:, 0:8], src,
                                               start=True, stop=True),
              [("perm8",), skey], [("PS", 5)], tag + "p4")
            o("V", lambda e, src=src, sc_a=sc_a: e.tensor_tensor(
                out=sc_a[:], in0=src, in1=PS[5][0:8, :], op=OP.max),
              [skey, ("PS", 5)], [akey], tag + "m4")
            o("T", lambda e, sc_a=sc_a: e.matmul(PS[2][0:8, :], perm8_sb[:, 8:16], sc_a[:],
                                                 start=True, stop=True),
              [("perm8",), akey], [("PS", 2)], tag + "p2")
            o("V", lambda e, sc_a=sc_a, sc_b=sc_b: e.tensor_tensor(
                out=sc_b[:], in0=sc_a[:], in1=PS[2][0:8, :], op=OP.max),
              [akey, ("PS", 2)], [bkey], tag + "m2")
            o("T", lambda e, sc_b=sc_b: e.matmul(PS[5][0:8, :], perm8_sb[:, 16:24], sc_b[:],
                                                 start=True, stop=True),
              [("perm8",), bkey], [("PS", 5)], tag + "p1")
            o("V", lambda e, sc_b=sc_b, sc_a=sc_a: e.tensor_tensor(
                out=sc_a[:], in0=sc_b[:], in1=PS[5][0:8, :], op=OP.max),
              [bkey, ("PS", 5)], [akey], tag + "m1")

        def rtr2():
            tmax(f"Ct1{t}", scsb[:], ("scsb",), esb, ("esb",), lgsb, ("lgsb",))
            o("V", lambda e: e.tensor_tensor(out=mk1[:], in0=scsb[:], in1=esb[:], op=OP.is_ge),
              [("scsb",), ("esb",)], [("mk1",)], f"Cmk1{t}")
            o("V", lambda e: e.scalar_tensor_tensor(out=lgsb[:], in0=mk1[:], scalar=-1e30,
                                                    in1=scsb[:], op0=OP.mult, op1=OP.add),
              [("mk1",), ("scsb",)], [("lgsb",)], f"Cmsk{t}")
        parts.append(rtr2)

        def rtr3():
            tmax(f"Ct2{t}", lgsb[:], ("lgsb",), esb, ("esb",), lgsb, ("lgsb",))
            o("V", lambda e: e.tensor_tensor(out=mk1[:], in0=scsb[:], in1=esb[:], op=OP.is_ge),
              [("scsb",), ("esb",)], [("mk1",)], f"Cmk2{t}")
            o("V", lambda e: e.tensor_tensor(out=afm[:], in0=mk1[:], in1=scsb[:], op=OP.mult),
              [("mk1",), ("scsb",)], [("afm",)], f"Ca{t}")
            dma("QS", a_d[:, tok:tok + TA], afm[:], [("afm",)], [("a_d", t)], tag=f"Cas{t}")
            if debug_a:
                dma("QS", dbg_a[:, tok:tok + TA], afm[:], [("afm",)], [("dbg_a", t)], tag=f"Cad{t}")
                dma("QS", dbg_y.rearrange("dc p n -> p dc n")[:, :, tok:tok + TA], yt[:],
                    [("yt", dc2) for dc2 in range(DC)], [("dbg_y", t)], tag=f"Cyd{t}")
        parts.append(rtr3)
        return parts

    if "C" in run_phases:
        c_xload(0)
        c_head(0)
        for p in c_wo_parts(0):
            p()
        c_xload(1)
        c_head(1)
        for t in range(NT):
            wparts = c_wo_parts(t + 1) if t + 1 < NT else []
            rest = (c_epi_parts(t - 1) if t >= 1 else []) + c_mid_parts(t)
            wi = 0
            for ri, r in enumerate(rest):
                r()
                while wi * len(rest) < (ri + 1) * len(wparts):
                    wparts[wi]()
                    wi += 1
            while wi < len(wparts):
                wparts[wi]()
                wi += 1
            if t + 2 < NT:
                c_xload(t + 2)
                c_head(t + 2)
        for p in c_epi_parts(NT - 1):
            p()

    em.barrier("C|E")
    ctxC.close()
    ctxABC.close()

    ctxE = ExitStack()
    EE = ctxE.enter_context
    oh8_sb = EE(nc.sbuf_tensor("oh8_sb", [8, 1024], BF16))
    w13 = [EE(nc.sbuf_tensor(f"w13_{i}", [128, 8, 2 * D], BF16)) for i in range(2)]
    w2b = EE(nc.sbuf_tensor("w2b", [128, 8, D], BF16))
    zt2e = [EE(nc.sbuf_tensor(f"zt2e{i}", [128, DC, TE], BF16)) for i in range(2)]
    Xt = EE(nc.sbuf_tensor("Xt", [128, DC, SUB], BF16))
    Hs = EE(nc.sbuf_tensor("Hs", [128, DC, SUB], BF16))
    Yacc = EE(nc.sbuf_tensor("Yacc", [128, DC, TE], F32))
    s1e = EE(nc.sbuf_tensor("s1e", [128, SUB], BF16))
    asb2 = EE(nc.sbuf_tensor("asb2", [128, TE], BF16))
    arow = EE(nc.sbuf_tensor("arow", [8, TE], BF16))
    yep = EE(nc.sbuf_tensor("yep", [128, 2 * SUB], F32))
    outb = EE(nc.sbuf_tensor("outb", [128, 2 * SUB], F32))
    sqe = EE(nc.sbuf_tensor("sqe", [128, SUB], BF16))
    ybb = EE(nc.sbuf_tensor("ybb", [128, SUB], BF16))
    a4sb = EE(nc.sbuf_tensor("a4sb", [128, SUB], BF16))
    b4sb = EE(nc.sbuf_tensor("b4sb", [128, SUB], BF16))

    def wsrc(mat, e):
        if e == 8:
            base = {0: w1s, 1: w3s, 2: w2s}[mat]
            return base.rearrange("(kc p) m -> p kc m", p=128)
        base = {0: w1, 1: w3, 2: w2}[mat]
        return base[e].rearrange("(kc p) m -> p kc m", p=128)

    def e_abc_xt(tt, u, ei, sb2):
        if ei == 8:
            return
        tb = tt % 2
        zt = zt2e[tb]
        pab = 6 + u % 2
        ssl = slice(sb2 * SUB, (sb2 + 1) * SUB)
        o("T", lambda e, ei=ei, sb2=sb2, pab=pab: e.matmul(
            PS[pab][:], oh8_sb[:, ei * 128:(ei + 1) * 128],
            arow[:, sb2 * SUB:(sb2 + 1) * SUB], start=True, stop=True),
          [("oh8_sb",), ("arow",)], [("PS", pab)], f"Eab{tt}.{ei}.{sb2}")
        o("V", lambda e, sb2=sb2, pab=pab: e.tensor_copy(
            asb2[:, sb2 * SUB:(sb2 + 1) * SUB], PS[pab][:]),
          [("PS", pab)], [("asb2", sb2)], f"Eae{tt}.{ei}.{sb2}")
        for dc in range(0, DC, 2):
            arep = bass.AP(tensor=asb2.ap().tensor, offset=sb2 * SUB,
                           ap=[[TE, 128], [0, 2], [1, SUB]])
            o("V", lambda e, dc=dc, ssl=ssl, sb2=sb2, arep=arep, zt=zt: e.tensor_tensor(
                out=Xt[:, dc:dc + 2, :], in0=zt[:, dc:dc + 2, ssl],
                in1=arep, op=OP.mult),
              [("zt", tb, sb2), ("asb2", sb2)], [("Xt", dc), ("Xt", dc + 1)], f"Ex{tt}.{ei}.{sb2}.{dc}")

    def e_epilogue(tt):
        tok = tt * TE
        for sb2 in range(2):
            ssl = slice(sb2 * SUB, (sb2 + 1) * SUB)
            tok2 = tok + sb2 * SUB
            for dc in range(DC):
                o("S", lambda e, dc=dc, ssl=ssl: e.activation(out=sqe[:], in_=Yacc[:, dc, ssl],
                                                              func=AF.Square, bias=zcol[:]),
                  [("Yacc", dc, sb2)], [("sqe",)], f"Efs{tt}.{sb2}.{dc}")
                o("T", lambda e, dc=dc: e.matmul(PS[0][0:1, :], ones_cb[:], sqe[:],
                                                 start=(dc == 0), stop=(dc == DC - 1)),
                  [("ones_cb",), ("sqe",)], [("PS", 0)], f"Efss{tt}.{sb2}.{dc}")
                o("S", lambda e, dc=dc, ssl=ssl: e.activation(out=ybb[:], in_=Yacc[:, dc, ssl],
                                                              func=AF.Copy),
                  [("Yacc", dc, sb2)], [("ybb",)], f"Efc{tt}.{sb2}.{dc}")
                o("T", lambda e, dc=dc: e.matmul(PS[1][0:1, :], ones_cb[:], ybb[:],
                                                 start=(dc == 0), stop=(dc == DC - 1)),
                  [("ones_cb",), ("ybb",)], [("PS", 1)], f"Efsm{tt}.{sb2}.{dc}")
            m4 = rowF("m4"); v4 = rowF("v4"); t4r = rowF("t4r"); r4 = rowF("r4"); q4 = rowF("q4")
            o("V", lambda e: e.tensor_scalar_mul(m4, PS[1][0:1, :], 1.0 / D), [("PS", 1)], [("R0",)])
            o("V", lambda e: e.tensor_scalar_mul(v4, PS[0][0:1, :], 1.0 / D), [("PS", 0)], [("R1",)])
            o("V", lambda e: e.tensor_tensor(out=t4r, in0=m4, in1=m4, op=OP.mult), [("R0",)], [("R2",)])
            o("V", lambda e: e.tensor_tensor(out=v4, in0=v4, in1=t4r, op=OP.subtract),
              [("R1",), ("R2",)], [("R1",)])
            o("S", lambda e: e.activation(out=t4r, in_=v4, func=AF.Ln, bias=epsr[:]), [("R1",)], [("R2",)])
            o("S", lambda e: e.activation(out=r4, in_=t4r, func=AF.Exp, scale=-0.5, bias=zrow[:]),
              [("R2",)], [("R3",)])
            o("V", lambda e: e.tensor_tensor(out=q4, in0=m4, in1=r4, op=OP.mult),
              [("R0",), ("R3",)], [("R4",)])
            r4b = rowB("r4b"); q4b = rowB("q4b")
            o("V", lambda e: e.tensor_copy(r4b, r4), [("R3",)], [("RB0",)])
            o("V", lambda e: e.tensor_copy(q4b, q4), [("R4",)], [("RB1",)])
            o("T", lambda e: e.matmul(PS[0][:], ones_rb[:], r4b, start=True, stop=True),
              [("ones_rb",), ("RB0",)], [("PS", 0)], f"Ea4{tt}.{sb2}")
            o("S", lambda e: e.activation(out=a4sb[:], in_=PS[0][:], func=AF.Copy),
              [("PS", 0)], [("a4sb",)], f"Ea4c{tt}.{sb2}")
            o("T", lambda e: e.matmul(PS[1][:], ones_rb[:], q4b, start=True, stop=True),
              [("ones_rb",), ("RB1",)], [("PS", 1)], f"Eb4{tt}.{sb2}")
            o("S", lambda e: e.activation(out=b4sb[:], in_=PS[1][:], func=AF.Copy),
              [("PS", 1)], [("b4sb",)], f"Eb4c{tt}.{sb2}")
            for dc in range(0, DC, 2):
                dma("QS", yep[:].rearrange("p (a n) -> p a n", a=2),
                    y_d.rearrange("dc p n -> p dc n")[:, dc:dc + 2, tok2:tok2 + SUB],
                    [("y_d", 2 * tt + sb2)], [("yep",)], tag=f"Eyl{tt}.{sb2}.{dc}")
                o("P", lambda e, dc=dc, ssl=ssl: e.tensor_tensor(
                    out=outb[:].rearrange("p (a n) -> p a n", a=2),
                    in0=Yacc[:, dc:dc + 2, ssl], in1=rep2(a4sb, SUB), op=OP.mult),
                  [("Yacc", dc, sb2), ("Yacc", dc + 1, sb2), ("a4sb",)], [("outb",)],
                  f"Eo1{tt}.{sb2}.{dc}")
                o("P", lambda e, dc=dc: e.tensor_tensor(
                    out=outb[:].rearrange("p (a n) -> p a n", a=2),
                    in0=outb[:].rearrange("p (a n) -> p a n", a=2),
                    in1=rep2(b4sb, SUB), op=OP.subtract),
                  [("outb",), ("b4sb",)], [("outb",)], f"Eo2{tt}.{sb2}.{dc}")
                o("P", lambda e, dc=dc: e.tensor_tensor(
                    out=outb[:], in0=outb[:], in1=yep[:], op=OP.add),
                  [("outb",), ("yep",)], [("outb",)], f"Eo3{tt}.{sb2}.{dc}")
                dma("QS", outTr[:, dc:dc + 2, tok2:tok2 + SUB],
                    outb[:].rearrange("p (a n) -> p a n", a=2),
                    [("outb",)],
                    [("out_d", tt, sb2, dc), ("out_d", tt, sb2, dc + 1)], tag=f"Eos{tt}.{sb2}.{dc}")

    if "E" in run_phases:

        dma("QG", oh8_sb[:], oh8, [("d_in",)], [("oh8_sb",)])

        def e_loads(tt):
            tok = tt * TE
            tb = tt % 2
            for h in range(2):
                dma("QA", zt2e[tb][:, :, h * SUB:(h + 1) * SUB],
                    z_d.rearrange("dc p n -> p dc n")[:, :, tok + h * SUB:tok + (h + 1) * SUB],
                    [("z_d", 2 * tt + h)], [("zt", tb, h)], tag=f"Ez{tt}.{h}")

        def e_aload(tt):
            tok = tt * TE
            dma("QG", arow[:], a_d[:, tok:tok + TE], [("a_d", 2 * tt), ("a_d", 2 * tt + 1)],
                [("arow",)], tag=f"Ea{tt}")

        _egpos = [0]
        e_loads(0)
        e_aload(0)
        for tt in range(NTE):
            tok = tt * TE
            if tt + 1 < NTE:
                e_loads(tt + 1)
            units = [(ei, sb2) for ei in range(9) for sb2 in range(2)]
            e_abc_xt(tt, 0, *units[0])
            for u, (ei, sb2) in enumerate(units):
                if sb2 == 0:
                    _egpos[0] += 1
                wp = _egpos[0] % 2
                if sb2 == 0:
                    dma("QG", w13[wp][:, :, 0:D], wsrc(0, ei), [("d_in",)], [("w13", wp, 0)],
                        tag=f"Ew1.{tt}.{ei}")
                    dma("QG", w13[wp][:, :, D:2 * D], wsrc(1, ei), [("d_in",)], [("w13", wp, 1)],
                        tag=f"Ew3.{tt}.{ei}")
                    dma("QG", w2b[:], wsrc(2, ei), [("d_in",)], [("w2b",)],
                        tag=f"Ew2.{tt}.{ei}")
                ssl = slice(sb2 * SUB, (sb2 + 1) * SUB)
                zt = zt2e[tt % 2]
                if ei < 8:
                    xkeys = lambda kc: ("Xt", kc)
                else:
                    xkeys = lambda kc, sb2=sb2, tt=tt: ("zt", tt % 2, sb2)
                for hc in range(DC):
                    p1 = 0 if hc % 2 == 0 else 2
                    p3 = 1 if hc % 2 == 0 else 3
                    for kc in range(DC):
                        rh = Xt[:, kc, :] if ei < 8 else zt[:, kc, ssl]
                        o("T", lambda e, kc=kc, hc=hc, p1=p1, wp=wp, rh=rh: e.matmul(
                            PS[p1][:], w13[wp][:, kc, hc * 128:(hc + 1) * 128], rh,
                            start=(kc == 0), stop=(kc == DC - 1)),
                          [("w13", wp, 0), xkeys(kc)], [("PS", p1)], f"Ep1.{tt}.{ei}.{sb2}.{hc}.{kc}")
                    for kc in range(DC):
                        rh = Xt[:, kc, :] if ei < 8 else zt[:, kc, ssl]
                        o("T", lambda e, kc=kc, hc=hc, p3=p3, wp=wp, rh=rh: e.matmul(
                            PS[p3][:], w13[wp][:, kc, D + hc * 128:D + (hc + 1) * 128], rh,
                            start=(kc == 0), stop=(kc == DC - 1)),
                          [("w13", wp, 1), xkeys(kc)], [("PS", p3)], f"Ep3.{tt}.{ei}.{sb2}.{hc}.{kc}")
                    o("S", lambda e, p1=p1: e.activation(out=s1e[:], in_=PS[p1][:], func=AF.Silu, bias=zcol[:]),
                      [("PS", p1)], [("s1e",)], f"Esl{tt}.{ei}.{sb2}.{hc}")
                    o("V", lambda e, hc=hc, p3=p3: e.tensor_tensor(
                        out=Hs[:, hc, :], in0=s1e[:], in1=PS[p3][:], op=OP.mult),
                      [("s1e",), ("PS", p3)], [("Hs", hc)], f"Eh{tt}.{ei}.{sb2}.{hc}")
                if u == 0 and tt > 0:
                    e_epilogue(tt - 1)
                if u + 1 < len(units):
                    e_abc_xt(tt, u + 1, *units[u + 1])
                for dc in range(DC):
                    py = 4 + dc % 2
                    for kc in range(DC):
                        o("T", lambda e, kc=kc, dc=dc, py=py: e.matmul(
                            PS[py][:], w2b[:, kc, dc * 128:(dc + 1) * 128], Hs[:, kc, :],
                            start=(kc == 0), stop=(kc == DC - 1)),
                          [("w2b",), ("Hs", kc)], [("PS", py)], f"Ew2m.{tt}.{ei}.{sb2}.{dc}.{kc}")
                    ysl = Yacc[:, dc, ssl]
                    if ei == 0:
                        o("V", lambda e, ysl=ysl, py=py: e.tensor_copy(ysl, PS[py][:]),
                          [("PS", py)], [("Yacc", dc, sb2)], f"Eyc{tt}.{ei}.{sb2}.{dc}")
                    else:
                        o("V", lambda e, ysl=ysl, py=py: e.tensor_tensor(
                            out=ysl, in0=ysl, in1=PS[py][:], op=OP.add),
                          [("PS", py), ("Yacc", dc, sb2)], [("Yacc", dc, sb2)],
                          f"Eya{tt}.{ei}.{sb2}.{dc}")
            if u == len(units) - 1 and tt + 1 < NTE:
                e_aload(tt + 1)


    if "E" in run_phases:
        e_epilogue(NTE - 1)

    if debug_y and "E" not in run_phases:
        tmpy = Yacc[:, :, 0:TA]
        for t in range(NT):
            tok = t * TA
            dma("QA", tmpy, y_d.rearrange("dc p n -> p dc n")[:, :, tok:tok + TA],
                [("y_d", t)], [("tmpy",)])
            dma("QS", outTr[:, :, tok:tok + TA], tmpy, [("tmpy",)], [("out_d", t)])

    em.barrier("END")

    ops = em.finalize(nc, sems)
    totals = em.final_totals

    streams = {b: [] for b in set(BODY_OF.values())}
    bodies_all = sorted(streams.keys())
    for i, op_ in enumerate(ops):
        if op_["barrier"]:
            for b in bodies_all:
                streams[b].append(("bar", op_["totals"]))
            continue
        b = BODY_OF[op_["e"]]
        streams[b].append(("op", op_))

    blk = E(nc.Block())

    def emit_body(eng, body):
        for kind, item in streams[body]:
            if kind == "bar":
                if body in ("vector", "scalar", "gpsimd"):
                    eng.drain()
                for s, v in sorted(item.items()):
                    if v > 0:
                        eng.wait_ge(sems[s], v)
                continue
            op_ = item
            for s, v in op_["waits"]:
                eng.wait_ge(sems[s], v)
            if op_.get("need_drain"):
                eng.drain()
            inst = op_["fn"](eng)
            if op_["inc"]:
                n = 16 if op_["e"] in QUEUES else 1
                inst.then_inc(sems[op_["sem"]], n)

    @blk.tensor
    def _(eng):
        emit_body(eng, "tensor")

    @blk.vector
    def _(eng):
        emit_body(eng, "vector")

    @blk.scalar
    def _(eng):
        emit_body(eng, "scalar")

    @blk.gpsimd
    def _(eng):
        emit_body(eng, "gpsimd")

    @blk.sync
    def _(eng):
        emit_body(eng, "sync")

    ctxE.close()
    ctx.close()
    return nc


def kernel(L0_attn_x, L0_attn_norm_g, L0_attn_norm_b, L0_attn_W_h, L0_attn_b_h,
           L0_attn_W_qk, L0_attn_b_qk, L0_attn_qk_gamma, L0_attn_qk_beta,
           L0_attn_rel_emb, L0_attn_W_o, L0_attn_b_o, L0_ffn_gate_w,
           L0_ffn_w1, L0_ffn_w2, L0_ffn_w3, L0_ffn_w1s, L0_ffn_w2s, L0_ffn_w3s):
    f32 = np.float32
    x = np.asarray(L0_attn_x, dtype=f32)
    shared = dict(
        Wh=np.ascontiguousarray(L0_attn_W_h, dtype=f32),
        Wqk=np.ascontiguousarray(L0_attn_W_qk, dtype=f32),
        Wo=np.ascontiguousarray(L0_attn_W_o, dtype=f32),
        bh=np.ascontiguousarray(L0_attn_b_h, dtype=f32),
        bhp=np.ascontiguousarray(np.asarray(L0_attn_b_h, dtype=f32).reshape(32, 128).T),
        bqk=np.ascontiguousarray(L0_attn_b_qk, dtype=f32),
        ngv=np.ascontiguousarray(L0_attn_norm_g, dtype=f32),
        nbv=np.ascontiguousarray(L0_attn_norm_b, dtype=f32),
        nbp=np.ascontiguousarray(np.asarray(L0_attn_norm_b, dtype=f32).reshape(8, 128).T),
        bop=np.ascontiguousarray(np.asarray(L0_attn_b_o, dtype=f32).reshape(8, 128).T),
        bo=np.ascontiguousarray(L0_attn_b_o, dtype=f32),
        qgT=np.ascontiguousarray(np.asarray(L0_attn_qk_gamma, dtype=f32).T),
        qbT=np.ascontiguousarray(np.asarray(L0_attn_qk_beta, dtype=f32).T),
        qg3=np.ascontiguousarray(np.asarray(L0_attn_qk_gamma, dtype=f32)[3:4, :]),
        qb3=np.ascontiguousarray(np.asarray(L0_attn_qk_beta, dtype=f32)[3:4, :]),
        relE=np.ascontiguousarray(L0_attn_rel_emb, dtype=f32),
        gwT=np.ascontiguousarray(np.asarray(L0_ffn_gate_w, dtype=f32).T),
        w1=np.ascontiguousarray(L0_ffn_w1, dtype=f32),
        w2=np.ascontiguousarray(L0_ffn_w2, dtype=f32),
        w3=np.ascontiguousarray(L0_ffn_w3, dtype=f32),
        w1s=np.ascontiguousarray(np.asarray(L0_ffn_w1s, dtype=f32)[0]),
        w2s=np.ascontiguousarray(np.asarray(L0_ffn_w2s, dtype=f32)[0]),
        w3s=np.ascontiguousarray(np.asarray(L0_ffn_w3s, dtype=f32)[0]),
    )
    b1 = bucket1d_np()
    ohC = np.zeros((32, 512), dtype=f32)
    ohC[b1, np.arange(511)] = 1.0
    shared["ohC"] = ohC
    oh8 = np.zeros((8, 1024), dtype=f32)
    for e in range(8):
        oh8[e, e * 128:(e + 1) * 128] = 1.0
    shared["oh8"] = oh8
    shared["antiI"] = np.ascontiguousarray(np.eye(128, dtype=f32)[::-1])
    perm8 = np.zeros((8, 24), dtype=f32)
    for si, s in enumerate((4, 2, 1)):
        for m in range(8):
            perm8[(m + s) % 8, si * 8 + m] = 1.0
    shared["perm8"] = perm8

    nc = build_nc()
    in_maps = []
    for c in range(NCORES):
        m = dict(shared)
        m["xT"] = np.ascontiguousarray(x[c].T)
        in_maps.append(m)
    res = run_bass_kernel_spmd(nc, in_maps, core_ids=list(range(NCORES)))
    out = np.stack([np.asarray(r["outT"]).T for r in res.results])
    return np.ascontiguousarray(out)


if __name__ == "__main__":
    import time
    t0 = time.time()
    nc = build_nc()
    print(f"built ok in {time.time() - t0:.1f}s")


# version 28
# speedup vs baseline: 1.0156x; 1.0003x over previous
import math
import numpy as np
import ml_dtypes
import concourse.bass as bass
import concourse.mybir as mybir
from concourse.bass_utils import run_bass_kernel_spmd

F32 = mybir.dt.float32
BF16 = mybir.dt.bfloat16
AF = mybir.ActivationFunctionType
OP = mybir.AluOpType
AX = mybir.AxisListType

NCORES = 8
N = 4096
D = 1024
DC = 8
HID = 2048
HC = 16
QKD = 128
GSZ = 256
NG = N // GSZ
TA = 512
NT = N // TA
TE = 1024
NTE = N // TE
SUB = 512
EPS = 1e-5
NEXP = 8

PARANOID = False
QPOOL = {"QS": 14, "QA": 12, "QG": 10}
COMPUTE_ENGS = ("T", "V", "S", "P")
QUEUES = {"QS": "sync", "QA": "scalar", "QG": "gpsimd"}
BODY_OF = {"T": "tensor", "V": "vector", "S": "scalar", "P": "gpsimd",
           "QS": "sync", "QA": "scalar", "QG": "gpsimd"}


def bucket1d_np():
    d = np.arange(-255, 256)
    retv = (d < 0).astype(np.int32) * 16
    a = np.abs(d)
    lg = np.log(np.maximum(a, 1).astype(np.float32) / np.float32(8.0))
    vl = 8 + (lg / np.float32(math.log(16.0)) * np.float32(8.0)).astype(np.int32)
    vl = np.minimum(vl, 15)
    return retv + np.where(a < 8, a, vl)


class Emitter:

    def __init__(self):
        self.ops = []

    def op(self, eng, fn, reads=(), writes=(), tag="", serial=False):
        self.ops.append(dict(e=eng, fn=fn, r=tuple(reads), w=tuple(writes),
                             tag=tag, barrier=False, inc=False, serial=serial,
                             waits=None, dep=set(), drain_dep=0, myval=0))

    def barrier(self, tag="bar"):
        self.ops.append(dict(e=None, fn=None, r=(), w=(), tag=tag, barrier=True,
                             inc=False, serial=False, waits=None, dep=set(),
                             drain_dep=0, myval=0))

    def finalize(self, nc, sems):
        ops = self.ops
        last_w = {}
        readers = {}
        last_bar = -1
        last_op_of = {}
        for i, o in enumerate(ops):
            if o["barrier"]:
                for e, j in last_op_of.items():
                    if j > last_bar:
                        if ops[j]["e"] in COMPUTE_ENGS:
                            ops[j]["inc"] = True
                last_bar = i
                continue
            deps = set()
            for k in o["r"]:
                if k in last_w:
                    deps.add(last_w[k])
            for k in o["w"]:
                if k in last_w:
                    deps.add(last_w[k])
                for j in readers.get(k, ()):
                    deps.add(j)
            e = o["e"]
            for d in deps:
                if d <= last_bar:
                    continue
                pe = ops[d]["e"]
                if pe == e:
                    if e == "T":
                        continue
                    if e in QUEUES:
                        ops[d]["inc"] = True
                        o["dep"].add(d)
                    elif PARANOID:
                        ops[d]["inc"] = True
                        o["dep"].add(d)
                        o["drain_dep"] = max(o["drain_dep"], d + 1)
                    else:
                        o["drain_dep"] = max(o["drain_dep"], d + 1)
                elif pe in QUEUES and e in QUEUES and BODY_OF[pe] == BODY_OF[e]:
                    ops[d]["inc"] = True
                    o["dep"].add(d)
                else:
                    ops[d]["inc"] = True
                    o["dep"].add(d)
            for k in o["w"]:
                last_w[k] = i
                readers[k] = []
            for k in o["r"]:
                readers.setdefault(k, []).append(i)
            last_op_of[e] = i
        counters = {s: 0 for s in sems}
        semval = [0] * len(ops)
        qrot = {q: 0 for q in QUEUES}
        for i, o in enumerate(ops):
            if o["barrier"]:
                continue
            e = o["e"]
            if e in QUEUES:
                j = qrot[e] % QPOOL[e]
                qrot[e] += 1
                name = f"{e}.{j}"
                counters[name] += 16
                semval[i] = counters[name]
                o["inc"] = True
                o["myval"] = counters[name]
                o["sem"] = name
                o["guard"] = counters[name] - 16
            elif o["inc"]:
                counters[e] += 1
                semval[i] = counters[e]
                o["sem"] = e
        bodies = sorted(set(BODY_OF.values()))
        wm = {b: {s: 0 for s in sems} for b in bodies}
        totals_hist = []
        cnt2 = {s: 0 for s in sems}
        last_drain_pos = {b: 0 for b in bodies}
        for i, o in enumerate(ops):
            if o["barrier"]:
                o["totals"] = dict(cnt2)
                for b in bodies:
                    for s in sems:
                        wm[b][s] = max(wm[b][s], cnt2[s])
                continue
            e = o["e"]
            b = BODY_OF[e]
            waits = {}
            for d in o["dep"]:
                ps = ops[d].get("sem") or ops[d]["e"]
                v = semval[d]
                if v > wm[b].get(ps, 0):
                    waits[ps] = max(waits.get(ps, 0), v)
            if e in QUEUES and o.get("guard", 0) > 0:
                s = o["sem"]
                if o["guard"] > wm[b].get(s, 0):
                    waits[s] = max(waits.get(s, 0), o["guard"])
            for s, v in waits.items():
                wm[b][s] = v
            o["waits"] = sorted(waits.items())
            o["need_drain"] = (o["drain_dep"] > last_drain_pos[b])
            if o["need_drain"]:
                last_drain_pos[b] = i
            if o["inc"]:
                cnt2[o["sem"]] += 16 if o["e"] in QUEUES else 1
        self.final_totals = dict(cnt2)
        return ops


def build_nc(run_phases=("A", "B", "C", "E"), debug_y=False, debug_a=False):
    nc = bass.Bass("TRN2", target_bir_lowering=False)
    em = Emitter()

    xT = nc.dram_tensor("xT", [D, N], F32, kind="ExternalInput").ap()
    Wh = nc.dram_tensor("Wh", [D, 2 * HID], F32, kind="ExternalInput").ap()
    Wqk = nc.dram_tensor("Wqk", [D, QKD], F32, kind="ExternalInput").ap()
    Wo = nc.dram_tensor("Wo", [HID, D], F32, kind="ExternalInput").ap()
    bh = nc.dram_tensor("bh", [2 * HID], F32, kind="ExternalInput").ap()
    bhp = nc.dram_tensor("bhp", [128, 32], F32, kind="ExternalInput").ap()
    bqk = nc.dram_tensor("bqk", [QKD], F32, kind="ExternalInput").ap()
    ngv = nc.dram_tensor("ngv", [D], F32, kind="ExternalInput").ap()
    nbv = nc.dram_tensor("nbv", [D], F32, kind="ExternalInput").ap()
    bo = nc.dram_tensor("bo", [D], F32, kind="ExternalInput").ap()
    qgT = nc.dram_tensor("qgT", [128, 4], F32, kind="ExternalInput").ap()
    qbT = nc.dram_tensor("qbT", [128, 4], F32, kind="ExternalInput").ap()
    qg3 = nc.dram_tensor("qg3", [1, 128], F32, kind="ExternalInput").ap()
    qb3 = nc.dram_tensor("qb3", [1, 128], F32, kind="ExternalInput").ap()
    relE = nc.dram_tensor("relE", [32, 1], F32, kind="ExternalInput").ap()
    ohC = nc.dram_tensor("ohC", [32, 512], F32, kind="ExternalInput").ap()
    antiI = nc.dram_tensor("antiI", [128, 128], F32, kind="ExternalInput").ap()
    perm8 = nc.dram_tensor("perm8", [8, 24], F32, kind="ExternalInput").ap()
    nbp = nc.dram_tensor("nbp", [128, 8], F32, kind="ExternalInput").ap()
    bop = nc.dram_tensor("bop", [128, 8], F32, kind="ExternalInput").ap()
    gwT = nc.dram_tensor("gwT", [D, 8], F32, kind="ExternalInput").ap()
    w1 = nc.dram_tensor("w1", [NEXP, D, D], F32, kind="ExternalInput").ap()
    w2 = nc.dram_tensor("w2", [NEXP, D, D], F32, kind="ExternalInput").ap()
    w3 = nc.dram_tensor("w3", [NEXP, D, D], F32, kind="ExternalInput").ap()
    w1s = nc.dram_tensor("w1s", [D, D], F32, kind="ExternalInput").ap()
    w2s = nc.dram_tensor("w2s", [D, D], F32, kind="ExternalInput").ap()
    w3s = nc.dram_tensor("w3s", [D, D], F32, kind="ExternalInput").ap()

    oh8 = nc.dram_tensor("oh8", [8, 1024], F32, kind="ExternalInput").ap()
    outT = nc.dram_tensor("outT", [D, N], F32, kind="ExternalOutput").ap()
    dbg_a = nc.dram_tensor("dbg_a", [8, N], F32, kind="ExternalOutput").ap() if debug_a else None
    dbg_y = nc.dram_tensor("dbg_y", [DC, 128, N], F32, kind="ExternalOutput").ap() if debug_a else None

    vtm_d = nc.dram_tensor("vtm_d", [NT, 128, 4, HID], BF16, kind="Internal").ap()
    gate_d = nc.dram_tensor("gate_d", [NT, HC, 128, TA], BF16, kind="Internal").ap()
    og_d = nc.dram_tensor("og_d", [HC, 128, N], BF16, kind="Internal").ap()
    y_d = nc.dram_tensor("y_d", [DC, 128, N], F32, kind="Internal").ap()
    z_d = nc.dram_tensor("z_d", [DC, 128, N], BF16, kind="Internal").ap()
    a_d = nc.dram_tensor("a_d", [8, N], F32, kind="Internal").ap()
    grow_d = nc.dram_tensor("grow_d", [512], F32, kind="Internal").ap()

    xTr = xT.rearrange("(dc p) n -> p dc n", p=128)
    outTr = outT.rearrange("(dc p) n -> p dc n", p=128)

    from contextlib import ExitStack
    ctx = ExitStack()
    E = ctx.enter_context

    sems = {}
    for s in ("T", "V", "S", "P"):
        sems[s] = E(nc.semaphore("sem_" + s))
    for q, np_ in QPOOL.items():
        for j in range(np_):
            sems[f"{q}.{j}"] = E(nc.semaphore(f"sem_{q}_{j}"))

    PS = [E(nc.psum_tensor(f"PS{i}", [128, 512], F32)) for i in range(8)]
    arF = E(nc.sbuf_tensor("arF", [1, 5120], F32))
    arB = E(nc.sbuf_tensor("arB", [1, 4096], BF16))
    zcol = E(nc.sbuf_tensor("zcol", [128, 1], F32))
    epsr = E(nc.sbuf_tensor("epsr", [1, 1], F32))
    zrow = E(nc.sbuf_tensor("zrow", [1, 1], F32))
    ones_cf = E(nc.sbuf_tensor("ones_cf", [128, 1], F32))
    ones_cb = E(nc.sbuf_tensor("ones_cb", [128, 1], BF16))
    ones_rb = E(nc.sbuf_tensor("ones_rb", [1, 128], BF16))

    ctxABC = ExitStack()
    EABC = ctxABC.enter_context
    ng_row = EABC(nc.sbuf_tensor("ng_row", [1, D], BF16))
    bqk_rowb = EABC(nc.sbuf_tensor("bqk_rowb", [1, 128], BF16))
    bqk_col = EABC(nc.sbuf_tensor("bqk_col", [128, 1], F32))
    ones8f = EABC(nc.sbuf_tensor("ones8f", [1, 8], F32))
    o8col = EABC(nc.sbuf_tensor("o8col", [8, 1], F32))
    c4096 = EABC(nc.sbuf_tensor("c4096", [1, 128], F32))
    qg_pt = EABC(nc.sbuf_tensor("qg_pt", [128, 4], F32))
    qb_pt = EABC(nc.sbuf_tensor("qb_pt", [128, 4], F32))
    qgs = EABC(nc.sbuf_tensor("qgs", [128, 4], F32))
    qbs = EABC(nc.sbuf_tensor("qbs", [128, 4], F32))
    g3bc = EABC(nc.sbuf_tensor("g3bc", [128, 128], BF16))
    b3bc = EABC(nc.sbuf_tensor("b3bc", [128, 128], BF16))
    g3row = EABC(nc.sbuf_tensor("g3row", [1, 128], F32))
    b3row = EABC(nc.sbuf_tensor("b3row", [1, 128], F32))
    antiI_sb = EABC(nc.sbuf_tensor("antiI_sb", [128, 128], F32))
    H0 = EABC(nc.sbuf_tensor("H0", [128, 256], F32))
    H1 = EABC(nc.sbuf_tensor("H1", [128, 256], F32))
    gwT_sb = EABC(nc.sbuf_tensor("gwT_sb", [128, 8, 8], F32))
    srow = EABC(nc.sbuf_tensor("srow", [1, 8], F32))
    bhp_sb = EABC(nc.sbuf_tensor("bhp_sb", [128, 32], F32))
    perm8_sb = EABC(nc.sbuf_tensor("perm8_sb", [8, 24], F32))
    nbp_sb = EABC(nc.sbuf_tensor("nbp_sb", [128, 8], F32))
    bop_sb = EABC(nc.sbuf_tensor("bop_sb", [128, 8], F32))

    _ROW_SLOT = {
        "m1": 0, "vv": 1, "var": 2, "r1": 3, "v2": 4, "r2": 5, "Rr": 6, "Pp": 7, "tmp": 8,
        "m3": 0, "v3": 1, "t3r": 2, "r3": 3, "q3": 4, "mf": 5, "vf": 6, "tfr": 7,
        "rf": 8, "qf": 9, "nmf": 2, "sumr": 0, "rsum": 1, "mx1": 4, "mx2": 7,
        "m4": 0, "v4": 1, "t4r": 2, "r4": 3, "q4": 4,
    }
    def rowF(name, n=512):
        s = _ROW_SLOT[name]
        return arF[0:1, s * 512:s * 512 + n]
    _BROW_SLOT = {"r3b": 0, "q3b": 1, "rfb": 2, "qfb": 3, "r4b": 0, "q4b": 1,
                  "Rbs0": 4, "Pbs0": 5, "Rbs1": 6, "Pbs1": 7}
    def rowB(name, n=512):
        s = _BROW_SLOT[name]
        return arB[0:1, s * 512:s * 512 + n]

    o = em.op

    def dma(q, out, in_, reads, writes, tag="", serial=False, **kw):
        o(q, (lambda eng, out=out, in_=in_, kw=kw: eng.dma_start(out=out, in_=in_, **kw)),
          reads, writes, tag, serial=serial)

    o("V", lambda e: e.memset(zcol[:], 0.0), (), [("zcol",)])
    o("V", lambda e: e.memset(epsr[:], EPS), (), [("epsr",)])
    o("V", lambda e: e.memset(zrow[:], 0.0), (), [("zrow",)])
    o("V", lambda e: e.memset(ones_cf[:], 1.0), (), [("ones_cf",)])
    o("V", lambda e: e.memset(ones_cb[:], 1.0), (), [("ones_cb",)])
    o("V", lambda e: e.memset(ones_rb[:], 1.0), (), [("ones_rb",)])
    o("V", lambda e: e.memset(ones8f[:], 1.0), (), [("ones8f",)])
    o("V", lambda e: e.memset(o8col[:], 1.0), (), [("o8col",)])
    o("V", lambda e: e.memset(c4096[:], 1.0 / N), (), [("c4096",)])
    em.barrier("init")

    dma("QS", bhp_sb[:], bhp, [("d_in",)], [("bhp_sb",)])
    dma("QS", perm8_sb[:], perm8, [("d_in",)], [("perm8",)])
    dma("QS", nbp_sb[:], nbp, [("d_in",)], [("nbp_sb",)])
    dma("QS", bop_sb[:], bop, [("d_in",)], [("bop_sb",)])
    dma("QS", qg_pt[:], qgT, [("d_in",)], [("qg_pt",)])
    dma("QS", qb_pt[:], qbT, [("d_in",)], [("qb_pt",)])
    dma("QS", g3row[:], qg3, [("d_in",)], [("g3row",)])
    dma("QS", b3row[:], qb3, [("d_in",)], [("b3row",)])
    dma("QS", antiI_sb[:], antiI, [("d_in",)], [("antiI_sb",)])
    dma("QS", gwT_sb[:], gwT.rearrange("(kc p) e -> p kc e", p=128), [("d_in",)], [("gwT_sb",)])
    o("QS", lambda e: e.dma_start(out=bqk_col[:], in_=bqk.rearrange("(p u) -> p u", u=1),
                                  allow_slow_non_contiguous=True),
      [("d_in",)], [("bqk_col",)])
    dma("QG", ng_row[:], ngv.rearrange("(a n) -> a n", a=1), [("d_in",)], [("ng_row",)])
    dma("QG", bqk_rowb[:], bqk.rearrange("(a n) -> a n", a=1), [("d_in",)], [("bqk_rowb",)])

    for h, sc in ((0, 1.0 / GSZ), (1, 1.0), (2, 1.0)):
        o("V", lambda e, h=h, sc=sc: e.tensor_scalar_mul(qgs[:, h:h + 1], qg_pt[:, h:h + 1], sc),
          [("qg_pt",)], [("qgs",)])
        o("V", lambda e, h=h, sc=sc: e.tensor_scalar_mul(qbs[:, h:h + 1], qb_pt[:, h:h + 1], sc),
          [("qb_pt",)], [("qbs",)])
    o("T", lambda e: e.matmul(PS[0][:, 0:128], c4096[:], g3row[:], start=True, stop=True),
      [("c4096",), ("g3row",)], [("PS", 0)])
    o("V", lambda e: e.tensor_copy(g3bc[:], PS[0][:, 0:128]), [("PS", 0)], [("g3bc",)])
    o("T", lambda e: e.matmul(PS[1][:, 0:128], c4096[:], b3row[:], start=True, stop=True),
      [("c4096",), ("b3row",)], [("PS", 1)])
    o("V", lambda e: e.tensor_copy(b3bc[:], PS[1][:, 0:128]), [("PS", 1)], [("b3bc",)])
    for kc in range(8):
        o("T", lambda e, kc=kc: e.matmul(PS[3][0:1, 0:8], ones_cf[:], gwT_sb[:, kc, :],
                                         start=(kc == 0), stop=(kc == 7)),
          [("ones_cf",), ("gwT_sb",)], [("PS", 3)])
    o("V", lambda e: e.tensor_copy(srow[:], PS[3][0:1, 0:8]), [("PS", 3)], [("srow",)])

    sqfm = EABC(nc.sbuf_tensor("sqfm", [128, N], BF16))
    linkv = EABC(nc.sbuf_tensor("linkv", [128, HID], F32))
    linkvb = EABC(nc.sbuf_tensor("linkvb", [128, HID], BF16))
    r1b = EABC(nc.sbuf_tensor("r1b", [1, N], BF16))
    p1b = EABC(nc.sbuf_tensor("p1b", [1, N], BF16))

    ctxA = ExitStack()
    EA = ctxA.enter_context
    Wh_sb = EA(nc.sbuf_tensor("Wh_sb", [128, 8, 2 * HID], BF16))
    bh_row = EA(nc.sbuf_tensor("bh_row", [1, HID], BF16))
    ohC_sb = EA(nc.sbuf_tensor("ohC_sb", [32, 512], F32))
    rel_sb = EA(nc.sbuf_tensor("rel_sb", [32, 1], F32))
    grow_sb = EA(nc.sbuf_tensor("grow_sb", [1, 512], F32))
    Wqk_sb = EA(nc.sbuf_tensor("Wqk_sb", [128, 8, QKD], BF16))
    xt2 = [EA(nc.sbuf_tensor(f"xt2_{i}", [128, DC, TA], BF16)) for i in range(2)]
    normed2 = [EA(nc.sbuf_tensor(f"normed2_{i}", [128, DC, TA], BF16)) for i in range(2)]
    tnorm = [EA(nc.sbuf_tensor(f"tnorm{i}", [128, TA], F32)) for i in range(2)]
    sqtm = EA(nc.sbuf_tensor("sqtm", [128, 4, QKD], BF16))
    lktm = EA(nc.sbuf_tensor("lktm", [128, 4, QKD], BF16))
    lkt2 = EA(nc.sbuf_tensor("lkt2", [128, QKD], BF16))
    vtmb = [EA(nc.sbuf_tensor(f"vtmb{i}", [128, HID], BF16)) for i in range(2)]
    gateb = [EA(nc.sbuf_tensor(f"gateb{i}", [128, 2, TA], BF16)) for i in range(2)]

    dma("QG", bh_row[:], bh[0:HID].rearrange("(a n) -> a n", a=1), [("d_in",)], [("bh_row",)])
    dma("QS", ohC_sb[:], ohC, [("d_in",)], [("ohC_sb",)])
    o("QS", lambda e: e.dma_start(out=rel_sb[:], in_=relE, allow_slow_non_contiguous=True),
      [("d_in",)], [("rel_sb",)])
    o("T", lambda e: e.matmul(PS[2][0:1, :], rel_sb[:], ohC_sb[:], start=True, stop=True),
      [("rel_sb",), ("ohC_sb",)], [("PS", 2)])
    o("V", lambda e: e.tensor_scalar_mul(grow_sb[:], PS[2][0:1, :], math.sqrt(128.0)),
      [("PS", 2)], [("grow_sb",)])
    dma("QS", grow_d[0:512], grow_sb[:], [("grow_sb",)], [("grow_d",)])
    gsk0 = bass.AP(tensor=grow_d.tensor, offset=128, ap=[[1, 128], [1, 256]])
    gsk1 = bass.AP(tensor=grow_d.tensor, offset=0, ap=[[1, 128], [1, 256]])
    dma("QS", H0[:], gsk0, [("grow_d",)], [("H0",)])
    dma("QS", H1[:], gsk1, [("grow_d",)], [("H1",)])

    dma("QG", Wh_sb[:].rearrange("p kc (h m) -> p kc h m", m=2048),
        Wh.rearrange("(kc p) (h m) -> p kc h m", p=128, m=2048),
        [("d_in",)], [("Wh_sb",)])
    dma("QG", Wqk_sb[:], Wqk.rearrange("(kc p) m -> p kc m", p=128),
        [("d_in",)], [("Wqk_sb",)])

    def a_stats(t):
        par = t % 2
        xa = xt2[par]
        tok = t * TA
        dma("QG", xa[:], xTr[:, :, tok:tok + TA], [("d_in",)], [("xt", par)], tag=f"Ax{t}")
        xsq = normed2[par][:, 0, :]
        for dc in range(DC):
            o("S", lambda e, dc=dc, xa=xa, xsq=xsq: e.activation(out=xsq, in_=xa[:, dc, :], func=AF.Square, bias=zcol[:]),
              [("xt", par)], [("nsq", par)], f"A0sq{t}.{dc}")
            o("T", lambda e, dc=dc, xa=xa: e.matmul(PS[6][0:1, :], ones_cb[:], xa[:, dc, :],
                                                    start=(dc == 0), stop=(dc == DC - 1)),
              [("ones_cb",), ("xt", par)], [("PS", 6)], f"A0sx{t}.{dc}")
            o("T", lambda e, dc=dc, xsq=xsq: e.matmul(PS[7][0:1, :], ones_cb[:], xsq,
                                             start=(dc == 0), stop=(dc == DC - 1)),
              [("ones_cb",), ("nsq", par)], [("PS", 7)], f"A0sxx{t}.{dc}")

    def a_rows(t):
        par = t % 2
        tok = t * TA
        m1 = rowF("m1"); vv = rowF("vv"); var = rowF("var"); r1 = rowF("r1")
        v2 = rowF("v2"); r2 = rowF("r2"); Rr = rowF("Rr"); Pp = rowF("Pp")
        tmp = rowF("tmp")
        o("V", lambda e: e.tensor_scalar_mul(m1, PS[6][0:1, :], 1.0 / D), [("PS", 6)], [("R0",)])
        o("V", lambda e: e.tensor_scalar_mul(vv, PS[7][0:1, :], 1.0 / D), [("PS", 7)], [("R1",)])
        o("V", lambda e: e.tensor_tensor(out=tmp, in0=m1, in1=m1, op=OP.mult),
          [("R0",)], [("R8",)])
        o("V", lambda e: e.tensor_tensor(out=var, in0=vv, in1=tmp, op=OP.subtract),
          [("R1",), ("R8",)], [("R2",)])
        o("S", lambda e: e.activation(out=tmp, in_=var, func=AF.Ln, bias=epsr[:]),
          [("R2",)], [("R8",)])
        o("S", lambda e: e.activation(out=r1, in_=tmp, func=AF.Exp, scale=-0.5, bias=zrow[:]),
          [("R8",)], [("R3",)])
        o("V", lambda e: e.tensor_tensor(out=v2, in0=var, in1=r1, op=OP.mult),
          [("R2",), ("R3",)], [("R4",)])
        o("V", lambda e: e.tensor_tensor(out=v2, in0=v2, in1=r1, op=OP.mult),
          [("R4",), ("R3",)], [("R4",)])
        o("S", lambda e: e.activation(out=tmp, in_=v2, func=AF.Ln, bias=epsr[:]),
          [("R4",)], [("R8",)])
        o("S", lambda e: e.activation(out=r2, in_=tmp, func=AF.Exp, scale=-0.5, bias=zrow[:]),
          [("R8",)], [("R5",)])
        o("V", lambda e: e.tensor_tensor(out=Rr, in0=r1, in1=r2, op=OP.mult),
          [("R3",), ("R5",)], [("R6",)])
        o("V", lambda e: e.tensor_tensor(out=Pp, in0=m1, in1=Rr, op=OP.mult),
          [("R0",), ("R6",)], [("R7",)])
        Rbs = rowB(f"Rbs{par}"); Pbs = rowB(f"Pbs{par}")
        o("V", lambda e, Rbs=Rbs: e.tensor_copy(Rbs, Rr), [("R6",)], [("RBR", par)])
        o("V", lambda e, Pbs=Pbs: e.tensor_copy(Pbs, Pp), [("R7",)], [("RBP", par)])
        o("V", lambda e, tok=tok: e.tensor_copy(r1b[0:1, tok:tok + TA], r1), [("R3",)], [("r1b", t)])
        o("V", lambda e: e.tensor_tensor(out=tmp, in0=m1, in1=r1, op=OP.mult),
          [("R0",), ("R3",)], [("R8",)])
        o("V", lambda e, tok=tok: e.tensor_copy(p1b[0:1, tok:tok + TA], tmp), [("R8",)], [("p1b", t)])

    def a_normed(t):
        par = t % 2
        xa = xt2[par]
        normed = normed2[par]
        Rbs = rowB(f"Rbs{par}"); Pbs = rowB(f"Pbs{par}")
        for dc in range(DC):
            pa, pb = (0, 1) if dc % 2 == 0 else (2, 3)
            gsl = ng_row[0:1, dc * 128:(dc + 1) * 128]
            o("T", lambda e, pa=pa, gsl=gsl, Rbs=Rbs: e.matmul(
                PS[pa][:], gsl, Rbs, start=True, stop=True),
              [("ng_row",), ("RBR", par)], [("PS", pa)], f"Aba{t}.{dc}")
            o("T", lambda e, pb=pb, gsl=gsl, Pbs=Pbs: e.matmul(
                PS[pb][:], gsl, Pbs, start=True, stop=True),
              [("ng_row",), ("RBP", par)], [("PS", pb)], f"Abb1{t}.{dc}")
            sc = tnorm[dc % 2]
            o("V", lambda e, dc=dc, pa=pa, sc=sc, xa=xa: e.tensor_tensor(
                out=sc[:], in0=xa[:, dc, :], in1=PS[pa][:], op=OP.mult),
              [("xt", par), ("PS", pa)], [("tn", dc % 2)], f"An1{t}.{dc}")
            o("V", lambda e, dc=dc, pb=pb, sc=sc, normed=normed: e.scalar_tensor_tensor(
                out=normed[:, dc, :], in0=sc[:], scalar=nbp_sb[:, dc:dc + 1],
                in1=PS[pb][:], op0=OP.add, op1=OP.subtract),
              [("tn", dc % 2), ("PS", pb), ("nbp_sb",)],
              [("normed", par, dc), ("nsq", par)], f"An2{t}.{dc}")

    def a_proj(t):
        par = t % 2
        normed = normed2[par]
        tok = t * TA
        vg_ps = [0]
        def nxt_ps():
            vg_ps[0] = (vg_ps[0] + 1) % 4
            return 4 + vg_ps[0]
        for kc in range(DC):
            o("T", lambda e, kc=kc: e.matmul(PS[4][:], Wqk_sb[:, kc, :],
                                             normed[:, kc, :], start=(kc == 0), stop=(kc == DC - 1)),
              [("Wqk_sb",), ("normed", par, kc)], [("PS", 4)], f"Aqk{t}.{kc}")
        o("S", lambda e, tok=tok: e.activation(out=sqfm[:, tok:tok + TA], in_=PS[4][:],
                                               func=AF.Silu, bias=bqk_col[:]),
          [("PS", 4), ("bqk_col",)], [("sqfm", t)], f"Aqke{t}")
        for tc in range(4):
            tsl = slice(tc * 128, (tc + 1) * 128)
            for kc in range(DC):
                o("T", lambda e, kc=kc, tsl=tsl: e.matmul(
                    PS[5][:, 0:QKD], normed[:, kc, tsl], Wqk_sb[:, kc, :],
                    start=(kc == 0), stop=False),
                  [("normed", par, kc), ("Wqk_sb",)], [("PS", 5)], f"Aqt{t}.{tc}.{kc}")
            o("T", lambda e: e.matmul(PS[5][:, 0:QKD], ones_rb[:], bqk_rowb[:],
                                      start=False, stop=True),
              [("ones_rb",), ("bqk_rowb",)], [("PS", 5)], f"Aqtb{t}.{tc}")
            o("S", lambda e, tc=tc: e.activation(out=sqtm[:, tc, :], in_=PS[5][:, 0:QKD],
                                                 func=AF.Silu, bias=zcol[:]),
              [("PS", 5)], [("sqtm", tc)], f"Aqte{t}.{tc}")
            o("V", lambda e, tc=tc: e.tensor_tensor(out=lkt2[:], in0=sqtm[:, tc, :],
                                                    in1=g3bc[:], op=OP.mult),
              [("sqtm", tc), ("g3bc",)], [("lkt2",)], f"Alk1{t}.{tc}")
            o("V", lambda e, tc=tc: e.tensor_tensor(out=lktm[:, tc, :], in0=lkt2[:],
                                                    in1=b3bc[:], op=OP.add),
              [("lkt2",), ("b3bc",)], [("lktm", tc)], f"Alk2{t}.{tc}")
            par2 = (t * 4 + tc) % 2
            vb = vtmb[par2]
            for Ncc in range(4):
                ps = nxt_ps()
                for kc in range(DC):
                    o("T", lambda e, kc=kc, tsl=tsl, Ncc=Ncc, ps=ps: e.matmul(
                        PS[ps][:], normed[:, kc, tsl], Wh_sb[:, kc, Ncc * 512:(Ncc + 1) * 512],
                        start=(kc == 0), stop=False),
                      [("normed", par, kc), ("Wh_sb",)], [("PS", ps)], f"Av{t}.{tc}.{Ncc}.{kc}")
                o("T", lambda e, Ncc=Ncc, ps=ps: e.matmul(
                    PS[ps][:], ones_rb[:], bh_row[0:1, Ncc * 512:(Ncc + 1) * 512],
                    start=False, stop=True),
                  [("ones_rb",), ("bh_row",)], [("PS", ps)], f"Avb{t}.{tc}.{Ncc}")
                o("S", lambda e, Ncc=Ncc, ps=ps, vb=vb: e.activation(
                    out=vb[:, Ncc * 512:(Ncc + 1) * 512], in_=PS[ps][:], func=AF.Silu, bias=zcol[:]),
                  [("PS", ps)], [("vtmb", par2, Ncc)], f"Ave{t}.{tc}.{Ncc}")
            for Ncc in range(4):
                o("T", lambda e, tc=tc, Ncc=Ncc, vb=vb: e.matmul(
                    PS[Ncc][:], lktm[:, tc, :], vb[:, Ncc * 512:(Ncc + 1) * 512],
                    start=(tc == 0), stop=(tc == 3)),
                  [("lktm", tc), ("vtmb", par2, Ncc)], [("PS", Ncc)], f"Alkv{t}.{tc}.{Ncc}")
            dma("QS", vtm_d[t, :, tc, :], vb[:], [("vtmb", par2, 0), ("vtmb", par2, 1),
                ("vtmb", par2, 2), ("vtmb", par2, 3)], [("vtm_d", t, tc)], tag=f"Avs{t}.{tc}")
        for Ncc in range(4):
            if t == 0:
                o("V", lambda e, Ncc=Ncc: e.tensor_copy(linkv[:, Ncc * 512:(Ncc + 1) * 512], PS[Ncc][:]),
                  [("PS", Ncc)], [("linkv", Ncc)], f"Alkc{t}.{Ncc}")
            else:
                o("V", lambda e, Ncc=Ncc: e.tensor_tensor(
                    out=linkv[:, Ncc * 512:(Ncc + 1) * 512],
                    in0=linkv[:, Ncc * 512:(Ncc + 1) * 512], in1=PS[Ncc][:], op=OP.add),
                  [("PS", Ncc), ("linkv", Ncc)], [("linkv", Ncc)], f"Alka{t}.{Ncc}")
        def gates():
          for hc in range(HC):
            ps = nxt_ps()
            gb = gateb[(hc // 2) % 2]
            for kc in range(DC):
                o("T", lambda e, kc=kc, hc=hc, ps=ps: e.matmul(
                    PS[ps][:], Wh_sb[:, kc, HID + hc * 128:HID + (hc + 1) * 128],
                    normed[:, kc, :], start=(kc == 0), stop=(kc == DC - 1)),
                  [("Wh_sb",), ("normed", par, kc)], [("PS", ps)], f"Ag{t}.{hc}.{kc}")
            o("S", lambda e, hc=hc, ps=ps, gb=gb: e.activation(
                out=gb[:, hc % 2, :], in_=PS[ps][:], func=AF.Silu,
                bias=bhp_sb[:, 16 + hc:17 + hc]),
              [("PS", ps), ("bhp_sb",)], [("gateb", (hc // 2) % 2, hc % 2)], f"Age{t}.{hc}")
            if hc % 2 == 1:
                h2 = hc // 2
                gb2 = gateb[h2 % 2]
                dma("QS", gate_d[t].rearrange("hc p n -> p hc n")[:, h2 * 2:(h2 + 1) * 2, :],
                    gb2[:], [("gateb", h2 % 2, 0), ("gateb", h2 % 2, 1)],
                    [("gate_d", t, h2)], tag=f"Ags{t}.{h2}")
        return gates

    if "A" in run_phases:
        a_stats(0)
        a_rows(0)
        a_normed(0)
        for t in range(NT):
            if t + 1 < NT:
                a_stats(t + 1)
                a_rows(t + 1)
            gates = a_proj(t)
            if t + 1 < NT:
                a_normed(t + 1)
            gates()
        for Ncc in range(4):
            o("V", lambda e, Ncc=Ncc: e.tensor_copy(linkvb[:, Ncc * 512:(Ncc + 1) * 512],
                                                    linkv[:, Ncc * 512:(Ncc + 1) * 512]),
              [("linkv", Ncc)], [("linkvb",)], f"Alkb{Ncc}")

    em.barrier("A|B")
    ctxA.close()

    ctxB = ExitStack()
    EB = ctxB.enter_context
    vtmg = [EB(nc.sbuf_tensor(f"vtmg{i}", [128, 2, HID], BF16)) for i in range(2)]
    gateg = [EB(nc.sbuf_tensor(f"gateg{i}", [128, HC, 256], BF16)) for i in range(2)]
    hq = [EB(nc.sbuf_tensor(f"hq{i}", [128, 256], BF16)) for i in range(2)]
    hk = [EB(nc.sbuf_tensor(f"hk{i}", [128, 256], BF16)) for i in range(2)]
    hlq = [EB(nc.sbuf_tensor(f"hlq{i}", [128, 256], BF16)) for i in range(2)]
    rsc = [EB(nc.sbuf_tensor(f"rsc{i}", [128, 256], F32)) for i in range(2)]
    attnT = [EB(nc.sbuf_tensor(f"attnT{i}", [128, 256], BF16)) for i in range(4)]
    ogb = [EB(nc.sbuf_tensor(f"ogb{i}", [128, HC, 256], BF16)) for i in range(2)]

    def b_pre(g):
            t, half = g // 2, g % 2
            tok = g * GSZ
            par = g % 2
            vg = vtmg[par]; gg = gateg[par]
            dma("QA", vg[:], vtm_d[t, :, 2 * half:2 * half + 2, :],
                [("vtm_d", t, 2 * half), ("vtm_d", t, 2 * half + 1)], [("vtmg", par)], tag=f"Bv{g}")
            dma("QA", gg[:], gate_d[t].rearrange("hc p n -> p hc n")[:, :, half * 256:half * 256 + 256],
                [("gate_d", t, h8) for h8 in range(8)],
                [("gateg", par)], tag=f"Bg{g}")
            sq = sqfm[:, tok:tok + 256]
            o("V", lambda e, sq=sq, par=par: e.tensor_scalar(
                out=hq[par][:], in0=sq, scalar1=qgs[:, 0:1], scalar2=qbs[:, 0:1],
                op0=OP.mult, op1=OP.add),
              [("sqfm", t), ("qgs",), ("qbs",)], [("hq", par)], f"Bhq{g}")
            o("V", lambda e, sq=sq, par=par: e.tensor_scalar(
                out=hlq[par][:], in0=sq, scalar1=qgs[:, 1:2], scalar2=qbs[:, 1:2],
                op0=OP.mult, op1=OP.add),
              [("sqfm", t), ("qgs",), ("qbs",)], [("hlq", par)], f"Bhlq{g}")
            o("V", lambda e, sq=sq, par=par: e.tensor_scalar(
                out=hk[par][:], in0=sq, scalar1=qgs[:, 2:3], scalar2=qbs[:, 2:3],
                op0=OP.mult, op1=OP.add),
              [("sqfm", t), ("qgs",), ("qbs",)], [("hk", par)], f"Bhk{g}")
            for jc in range(2):
                ps = jc
                Hj = H0 if jc == 0 else H1
                o("T", lambda e, ps=ps, par=par, jc=jc: e.matmul(
                    PS[ps][:, 0:256], hk[par][:, jc * 128:(jc + 1) * 128], hq[par][:],
                    start=True, stop=False),
                  [("hk", par), ("hq", par)], [("PS", ps)], f"Bsim{g}.{jc}")
                o("T", lambda e, ps=ps, Hj=Hj: e.matmul(
                    PS[ps][:, 0:256], antiI_sb[:], Hj[:], start=False, stop=True),
                  [("antiI_sb",), ("H0",), ("H1",)], [("PS", ps)], f"Bbias{g}.{jc}")
                o("S", lambda e, ps=ps, jc=jc: e.activation(out=rsc[jc][:], in_=PS[ps][:, 0:256],
                                                            func=AF.Relu, bias=zcol[:]),
                  [("PS", ps)], [("rsc", jc)], f"Brelu{g}.{jc}")
                at = attnT[par * 2 + jc]
                o("V", lambda e, jc=jc, at=at: e.tensor_tensor(out=at[:], in0=rsc[jc][:],
                                                               in1=rsc[jc][:], op=OP.mult),
                  [("rsc", jc)], [("attnT", par * 2 + jc)], f"Bsq{g}.{jc}")
    def b_qolo(g):
            t, half = g // 2, g % 2
            tok = g * GSZ
            par = g % 2
            ob = ogb[par]
            for ec in range(HC):
                ps = 2 + ec % 4
                o("T", lambda e, ec=ec, ps=ps, par=par: e.matmul(
                    PS[ps][:, 0:256], vtmg[par][:, 0, ec * 128:(ec + 1) * 128], attnT[par * 2][:],
                    start=True, stop=False),
                  [("vtmg", par), ("attnT", par * 2)], [("PS", ps)], f"Bqo1{g}.{ec}")
                o("T", lambda e, ec=ec, ps=ps, par=par: e.matmul(
                    PS[ps][:, 0:256], vtmg[par][:, 1, ec * 128:(ec + 1) * 128], attnT[par * 2 + 1][:],
                    start=False, stop=False),
                  [("vtmg", par), ("attnT", par * 2 + 1)], [("PS", ps)], f"Bqo2{g}.{ec}")
                o("T", lambda e, ec=ec, ps=ps, par=par: e.matmul(
                    PS[ps][:, 0:256], linkvb[:, ec * 128:(ec + 1) * 128], hlq[par][:],
                    start=False, stop=True),
                  [("linkvb",), ("hlq", par)], [("PS", ps)], f"Blo{g}.{ec}")
                o("V", lambda e, ec=ec, ps=ps, par=par, ob=ob: e.tensor_tensor(
                    out=ob[:, ec, :], in0=gateg[par][:, ec, :], in1=PS[ps][:, 0:256], op=OP.mult),
                  [("gateg", par), ("PS", ps)], [("ogb", par, ec)], f"Bog{g}.{ec}")
            dma("QS", og_d.rearrange("ec p n -> p ec n")[:, :, tok:tok + 256], ob[:],
                [("ogb", par, ec2) for ec2 in range(HC)], [("og_d", g)], tag=f"Bogs{g}")

    if "B" in run_phases:
        b_pre(0)
        for g in range(NG):
            if g + 1 < NG:
                b_pre(g + 1)
            b_qolo(g)

    em.barrier("B|C")
    ctxB.close()

    ctxC = ExitStack()
    EC = ctxC.enter_context
    Wo_sb = EC(nc.sbuf_tensor("Wo_sb", [128, HC, D], BF16))
    ogt = EC(nc.sbuf_tensor("ogt", [128, HC, TA], BF16))
    xtc2 = [EC(nc.sbuf_tensor(f"xtc2_{i}", [128, DC, TA], BF16)) for i in range(2)]
    ao2 = [EC(nc.sbuf_tensor(f"ao2_{i}", [128, DC, TA], BF16)) for i in range(2)]
    yt = EC(nc.sbuf_tensor("yt", [128, DC, TA], F32))
    zt2 = [EC(nc.sbuf_tensor(f"zt2_{i}", [128, 2 * TA], BF16)) for i in range(2)]
    a1sb = EC(nc.sbuf_tensor("a1sb", [128, TA], BF16))
    pb1sb = EC(nc.sbuf_tensor("pb1sb", [128, TA], BF16))
    a3sb = EC(nc.sbuf_tensor("a3sb", [128, TA], BF16))
    b3sb = EC(nc.sbuf_tensor("b3sb", [128, TA], BF16))
    afsb = EC(nc.sbuf_tensor("afsb", [128, TA], BF16))
    zsc = EC(nc.sbuf_tensor("zsc", [128, 2 * TA], F32))
    bfsb = EC(nc.sbuf_tensor("bfsb", [128, TA], BF16))
    tc1b = EC(nc.sbuf_tensor("tc1b", [128, 2 * TA], F32))
    tc2b = EC(nc.sbuf_tensor("tc2b", [128, 2 * TA], F32))
    casb = EC(nc.sbuf_tensor("casb", [128, 2 * TA], BF16))
    cpl = EC(nc.sbuf_tensor("cpl", [128, 2 * TA], BF16))
    lgsb = EC(nc.sbuf_tensor("lgsb", [8, TA], F32))
    esb = EC(nc.sbuf_tensor("esb", [8, TA], F32))
    scsb = EC(nc.sbuf_tensor("scsb", [8, TA], F32))
    mk1 = EC(nc.sbuf_tensor("mk1", [8, TA], BF16))
    afm = EC(nc.sbuf_tensor("afm", [8, TA], F32))

    dma("QG", Wo_sb[:], Wo.rearrange("(ec p) m -> p ec m", p=128),
        [("d_in",)], [("Wo_sb",)])


    def rep2(tens, n=TA):
        return bass.AP(tensor=tens.ap().tensor, offset=0, ap=[[n, 128], [0, 2], [1, n]])

    def c_xload(t):
        par = t % 2
        tok = t * TA
        dma("QG", xtc2[par][:], xTr[:, :, tok:tok + TA], [("d_in",)], [("xtc", par)], tag=f"Cx{t}")

    def c_head(t):
        tok = t * TA
        dma("QS", ogt[:], og_d.rearrange("ec p n -> p ec n")[:, :, tok:tok + TA],
            [("og_d", 2 * t), ("og_d", 2 * t + 1)], [("ogt",)], tag=f"Cog{t}")
        o("T", lambda e, tok=tok: e.matmul(PS[2][:], ones_rb[:], r1b[0:1, tok:tok + TA],
                                           start=True, stop=True),
          [("ones_rb",), ("r1b", t)], [("PS", 2)], f"Ca1{t}")
        o("S", lambda e: e.activation(out=a1sb[:], in_=PS[2][:], func=AF.Copy),
          [("PS", 2)], [("a1sb",)], f"Ca1c{t}")
        o("T", lambda e, tok=tok: e.matmul(PS[2][:], ones_rb[:], p1b[0:1, tok:tok + TA],
                                           start=True, stop=True),
          [("ones_rb",), ("p1b", t)], [("PS", 2)], f"Cb1{t}")
        o("S", lambda e: e.activation(out=pb1sb[:], in_=PS[2][:], func=AF.Copy),
          [("PS", 2)], [("pb1sb",)], f"Cb1c{t}")

    def c_wo_parts(t):
        par = t % 2
        xc = xtc2[par]
        aot = ao2[par]
        ps_s, ps_q = (3, 4) if par == 0 else (6, 7)
        parts = []

        def mk(dc):
            def th():
                for d2 in (dc, dc + 1):
                    pwo = d2 % 2
                    for ec in range(HC):
                        o("T", lambda e, d2=d2, ec=ec, pwo=pwo: e.matmul(
                            PS[pwo][:], Wo_sb[:, ec, d2 * 128:(d2 + 1) * 128], ogt[:, ec, :],
                            start=(ec == 0), stop=(ec == HC - 1)),
                          [("Wo_sb",), ("ogt",)], [("PS", pwo)], f"Cwo{t}.{d2}.{ec}")
                o("V", lambda e, dc=dc, xc=xc: e.tensor_tensor(
                    out=tc1b[:].rearrange("p (a n) -> p a n", a=2), in0=xc[:, dc:dc + 2, :],
                    in1=rep2(a1sb), op=OP.mult),
                  [("xtc", par), ("a1sb",)], [("tc1b",)], f"Cu1{t}.{dc}")
                for d2 in (dc, dc + 1):
                    pwo = d2 % 2
                    o("V", lambda e, d2=d2, pwo=pwo: e.scalar_tensor_tensor(
                        out=tc2b[:, (d2 - dc) * TA:(d2 - dc + 1) * TA],
                        in0=tc1b[:, (d2 - dc) * TA:(d2 - dc + 1) * TA],
                        scalar=bop_sb[:, d2:d2 + 1], in1=PS[pwo][:],
                        op0=OP.add, op1=OP.add),
                      [("tc1b",), ("PS", pwo), ("bop_sb",)], [("tc2b",)], f"Cu2{t}.{d2}")
                o("V", lambda e, dc=dc, aot=aot: e.tensor_tensor(
                    out=aot[:, dc:dc + 2, :], in0=tc2b[:].rearrange("p (a n) -> p a n", a=2),
                    in1=rep2(pb1sb), op=OP.subtract),
                  [("tc2b",), ("pb1sb",)], [("ao", par, dc), ("ao", par, dc + 1)], f"Cao{t}.{dc}")
                o("S", lambda e, dc=dc, aot=aot: e.activation(
                    out=casb[:], in_=aot[:, dc:dc + 2, :].rearrange("p a n -> p (a n)"),
                    func=AF.Square, bias=zcol[:]),
                  [("ao", par, dc), ("ao", par, dc + 1)], [("casb",)], f"Csq{t}.{dc}")
                for d2 in (dc, dc + 1):
                    o("T", lambda e, d2=d2, dc=dc, ps_q=ps_q: e.matmul(
                        PS[ps_q][0:1, :], ones_cb[:], casb[:, (d2 - dc) * TA:(d2 - dc + 1) * TA],
                        start=(d2 == 0), stop=(d2 == DC - 1)),
                      [("ones_cb",), ("casb",)], [("PS", ps_q)], f"Css{t}.{d2}")
                    o("T", lambda e, d2=d2, ps_s=ps_s, aot=aot: e.matmul(
                        PS[ps_s][0:1, :], ones_cb[:], aot[:, d2, :],
                        start=(d2 == 0), stop=(d2 == DC - 1)),
                      [("ones_cb",), ("ao", par, d2)], [("PS", ps_s)], f"Csm{t}.{d2}")
            return th
        for dc in range(0, DC, 2):
            parts.append(mk(dc))
        return parts

    def c_mid_parts(t):
        par = t % 2
        xc = xtc2[par]
        aot = ao2[par]
        ps_s, ps_q = (3, 4) if par == 0 else (6, 7)
        tok = t * TA
        parts = []

        def head():
            m3 = rowF("m3"); v3 = rowF("v3"); t3r = rowF("t3r"); r3 = rowF("r3"); q3 = rowF("q3")
            o("V", lambda e, ps_s=ps_s: e.tensor_scalar_mul(m3, PS[ps_s][0:1, :], 1.0 / D), [("PS", ps_s)], [("R0",)])
            o("V", lambda e, ps_q=ps_q: e.tensor_scalar_mul(v3, PS[ps_q][0:1, :], 1.0 / D), [("PS", ps_q)], [("R1",)])
            o("V", lambda e: e.tensor_tensor(out=t3r, in0=m3, in1=m3, op=OP.mult), [("R0",)], [("R2",)])
            o("V", lambda e: e.tensor_tensor(out=v3, in0=v3, in1=t3r, op=OP.subtract),
              [("R1",), ("R2",)], [("R1",)])
            o("S", lambda e: e.activation(out=t3r, in_=v3, func=AF.Ln, bias=epsr[:]), [("R1",)], [("R2",)])
            o("S", lambda e: e.activation(out=r3, in_=t3r, func=AF.Exp, scale=-0.5, bias=zrow[:]),
              [("R2",)], [("R3",)])
            o("V", lambda e: e.tensor_tensor(out=q3, in0=m3, in1=r3, op=OP.mult),
              [("R0",), ("R3",)], [("R4",)])
            r3b = rowB("r3b"); q3b = rowB("q3b")
            o("V", lambda e: e.tensor_copy(r3b, r3), [("R3",)], [("RB0",)])
            o("V", lambda e: e.tensor_copy(q3b, q3), [("R4",)], [("RB1",)])
            o("T", lambda e: e.matmul(PS[2][:], ones_rb[:], r3b, start=True, stop=True),
              [("ones_rb",), ("RB0",)], [("PS", 2)], f"Ca3{t}")
            o("S", lambda e: e.activation(out=a3sb[:], in_=PS[2][:], func=AF.Copy),
              [("PS", 2)], [("a3sb",)], f"Ca3c{t}")
            o("T", lambda e: e.matmul(PS[2][:], ones_rb[:], q3b, start=True, stop=True),
              [("ones_rb",), ("RB1",)], [("PS", 2)], f"Cb3{t}")
            o("S", lambda e: e.activation(out=b3sb[:], in_=PS[2][:], func=AF.Copy),
              [("PS", 2)], [("b3sb",)], f"Cb3c{t}")
        parts.append(head)

        def mkdc(dc):
            def th():
                o("V", lambda e, dc=dc, aot=aot: e.tensor_tensor(
                    out=tc1b[:].rearrange("p (a n) -> p a n", a=2), in0=aot[:, dc:dc + 2, :],
                    in1=rep2(a3sb), op=OP.mult),
                  [("ao", par, dc), ("ao", par, dc + 1), ("a3sb",)], [("tc1b",)], f"Cy1{t}.{dc}")
                o("V", lambda e, dc=dc: e.tensor_tensor(
                    out=tc2b[:].rearrange("p (a n) -> p a n", a=2),
                    in0=tc1b[:].rearrange("p (a n) -> p a n", a=2),
                    in1=rep2(b3sb), op=OP.subtract),
                  [("tc1b",), ("b3sb",)], [("tc2b",)], f"Cy2{t}.{dc}")
                o("V", lambda e, dc=dc, xc=xc: e.tensor_tensor(
                    out=yt[:, dc:dc + 2, :], in0=tc2b[:].rearrange("p (a n) -> p a n", a=2),
                    in1=xc[:, dc:dc + 2, :], op=OP.add),
                  [("tc2b",), ("xtc", par)], [("yt", dc), ("yt", dc + 1)], f"Cy3{t}.{dc}")
                o("S", lambda e, dc=dc: e.activation(
                    out=casb[:], in_=yt[:, dc:dc + 2, :].rearrange("p a n -> p (a n)"),
                    func=AF.Square, bias=zcol[:]),
                  [("yt", dc), ("yt", dc + 1)], [("casb",)], f"Cysq{t}.{dc}")
                o("V", lambda e, dc=dc: e.tensor_copy(
                    cpl[:], yt[:, dc:dc + 2, :].rearrange("p a n -> p (a n)")),
                  [("yt", dc), ("yt", dc + 1)], [("cpl",)], f"Cyc{t}.{dc}")
                for d2 in (dc, dc + 1):
                    o("T", lambda e, d2=d2, dc=dc, ps_q=ps_q: e.matmul(
                        PS[ps_q][0:1, :], ones_cb[:], casb[:, (d2 - dc) * TA:(d2 - dc + 1) * TA],
                        start=(d2 == 0), stop=(d2 == DC - 1)),
                      [("ones_cb",), ("casb",)], [("PS", ps_q)], f"Cyss{t}.{d2}")
                    o("T", lambda e, d2=d2, dc=dc, ps_s=ps_s: e.matmul(
                        PS[ps_s][0:1, :], ones_cb[:], cpl[:, (d2 - dc) * TA:(d2 - dc + 1) * TA],
                        start=(d2 == 0), stop=(d2 == DC - 1)),
                      [("ones_cb",), ("cpl",)], [("PS", ps_s)], f"Cysm{t}.{d2}")
                    o("T", lambda e, d2=d2: e.matmul(PS[5][0:8, :], gwT_sb[:, d2, :], yt[:, d2, :],
                                                     start=(d2 == 0), stop=False),
                      [("gwT_sb",), ("yt", d2)], [("PS", 5)], f"Clg{t}.{d2}")
            return th
        for dc in range(0, DC, 2):
            parts.append(mkdc(dc))

        def tail():
            dma("QS", y_d.rearrange("dc p n -> p dc n")[:, :, tok:tok + TA], yt[:],
                [("yt", dc2) for dc2 in range(DC)], [("y_d", t)], tag=f"Cys{t}")
            mf = rowF("mf"); vf = rowF("vf"); tfr = rowF("tfr"); rf = rowF("rf")
            qf = rowF("qf"); nmf = rowF("nmf")
            o("V", lambda e, ps_s=ps_s: e.tensor_scalar_mul(mf, PS[ps_s][0:1, :], 1.0 / D), [("PS", ps_s)], [("R5",)])
            o("V", lambda e, ps_q=ps_q: e.tensor_scalar_mul(vf, PS[ps_q][0:1, :], 1.0 / D), [("PS", ps_q)], [("R6",)])
            o("V", lambda e: e.tensor_tensor(out=tfr, in0=mf, in1=mf, op=OP.mult), [("R5",)], [("R7",)])
            o("V", lambda e: e.tensor_tensor(out=vf, in0=vf, in1=tfr, op=OP.subtract),
              [("R6",), ("R7",)], [("R6",)])
            o("S", lambda e: e.activation(out=tfr, in_=vf, func=AF.Ln, bias=epsr[:]), [("R6",)], [("R7",)])
            o("S", lambda e: e.activation(out=rf, in_=tfr, func=AF.Exp, scale=-0.5, bias=zrow[:]),
              [("R7",)], [("R8",)])
            o("V", lambda e: e.tensor_tensor(out=qf, in0=mf, in1=rf, op=OP.mult),
              [("R5",), ("R8",)], [("R9",)])
            o("V", lambda e: e.tensor_scalar_mul(nmf, mf, -1.0), [("R5",)], [("R2",)])
            rfb = rowB("rfb"); qfb = rowB("qfb")
            o("V", lambda e: e.tensor_copy(rfb, rf), [("R8",)], [("RB2",)])
            o("V", lambda e: e.tensor_copy(qfb, qf), [("R9",)], [("RB3",)])
            o("T", lambda e: e.matmul(PS[5][0:8, :], srow[:], nmf, start=False, stop=True),
              [("srow",), ("R2",)], [("PS", 5)], f"Clgf{t}")
        parts.append(tail)
        return parts

    def c_epi_parts(t):
        tok = t * TA
        parts = []

        def bcasts():
            o("T", lambda e: e.matmul(PS[2][:], ones_rb[:], rowB("rfb"), start=True, stop=True),
              [("ones_rb",), ("RB2",)], [("PS", 2)], f"Caf{t}")
            o("S", lambda e: e.activation(out=afsb[:], in_=PS[2][:], func=AF.Copy),
              [("PS", 2)], [("afsb",)], f"Cafc{t}")
            o("T", lambda e: e.matmul(PS[2][:], ones_rb[:], rowB("qfb"), start=True, stop=True),
              [("ones_rb",), ("RB3",)], [("PS", 2)], f"Cbf{t}")
            o("S", lambda e: e.activation(out=bfsb[:], in_=PS[2][:], func=AF.Copy),
              [("PS", 2)], [("bfsb",)], f"Cbfc{t}")
        parts.append(bcasts)

        def mkz(dc0):
            def th():
                zb = zt2[(dc0 // 2) % 2]
                o("P", lambda e, dc0=dc0: e.tensor_tensor(
                    out=zsc[:].rearrange("p (a n) -> p a n", a=2), in0=yt[:, dc0:dc0 + 2, :],
                    in1=rep2(afsb), op=OP.mult),
                  [("yt", dc0), ("yt", dc0 + 1), ("afsb",)], [("zsc",)], f"Cz1{t}.{dc0}")
                o("P", lambda e, dc0=dc0, zb=zb: e.tensor_tensor(
                    out=zb[:].rearrange("p (a n) -> p a n", a=2),
                    in0=zsc[:].rearrange("p (a n) -> p a n", a=2),
                    in1=rep2(bfsb), op=OP.subtract),
                  [("zsc",), ("bfsb",)], [("zt2", (dc0 // 2) % 2)], f"Cz2{t}.{dc0}")
                dma("QS", z_d.rearrange("dc p n -> p dc n")[:, dc0:dc0 + 2, tok:tok + TA], zb[:].rearrange("p (a n) -> p a n", a=2),
                    [("zt2", (dc0 // 2) % 2)], [("z_d", t, dc0), ("z_d", t, dc0 + 1)], tag=f"Czs{t}.{dc0}")
            return th
        for dc0 in range(0, DC, 2):
            parts.append(mkz(dc0))

        def router1():
            o("V", lambda e: e.tensor_copy(esb[:], PS[5][0:8, :]), [("PS", 5)], [("esb",)], f"Cl0{t}")
            o("T", lambda e: e.matmul(PS[2][0:8, :], ones8f[:], rowF("rf"), start=True, stop=True),
              [("ones8f",), ("R8",)], [("PS", 2)], f"Crb{t}")
            o("V", lambda e: e.tensor_tensor(out=lgsb[:], in0=esb[:], in1=PS[2][0:8, :], op=OP.mult),
              [("esb",), ("PS", 2)], [("lgsb",)], f"Clg2{t}")
            o("S", lambda e: e.activation(out=esb[:], in_=lgsb[:], func=AF.Exp, bias=zcol[0:8, :]),
              [("lgsb",)], [("esb",)], f"Cexp{t}")
            o("T", lambda e: e.matmul(PS[5][0:1, :], o8col[:], esb[:], start=True, stop=True),
              [("o8col",), ("esb",)], [("PS", 5)], f"Cse{t}")
            sumr = rowF("sumr"); rsum = rowF("rsum")
            o("V", lambda e: e.tensor_copy(sumr, PS[5][0:1, :]), [("PS", 5)], [("R0",)])
            o("V", lambda e: e.reciprocal(rsum, sumr), [("R0",)], [("R1",)])
            o("T", lambda e: e.matmul(PS[2][0:8, :], ones8f[:], rsum, start=True, stop=True),
              [("ones8f",), ("R1",)], [("PS", 2)], f"Crs{t}")
            o("V", lambda e: e.tensor_tensor(out=scsb[:], in0=esb[:], in1=PS[2][0:8, :], op=OP.mult),
              [("esb",), ("PS", 2)], [("scsb",)], f"Csc{t}")
        parts.append(router1)

        def tmax(tag, src, skey, sc_a, akey, sc_b, bkey):
            o("T", lambda e, src=src: e.matmul(PS[5][0:8, :], perm8_sb[:, 0:8], src,
                                               start=True, stop=True),
              [("perm8",), skey], [("PS", 5)], tag + "p4")
            o("V", lambda e, src=src, sc_a=sc_a: e.tensor_tensor(
                out=sc_a[:], in0=src, in1=PS[5][0:8, :], op=OP.max),
              [skey, ("PS", 5)], [akey], tag + "m4")
            o("T", lambda e, sc_a=sc_a: e.matmul(PS[2][0:8, :], perm8_sb[:, 8:16], sc_a[:],
                                                 start=True, stop=True),
              [("perm8",), akey], [("PS", 2)], tag + "p2")
            o("V", lambda e, sc_a=sc_a, sc_b=sc_b: e.tensor_tensor(
                out=sc_b[:], in0=sc_a[:], in1=PS[2][0:8, :], op=OP.max),
              [akey, ("PS", 2)], [bkey], tag + "m2")
            o("T", lambda e, sc_b=sc_b: e.matmul(PS[5][0:8, :], perm8_sb[:, 16:24], sc_b[:],
                                                 start=True, stop=True),
              [("perm8",), bkey], [("PS", 5)], tag + "p1")
            o("V", lambda e, sc_b=sc_b, sc_a=sc_a: e.tensor_tensor(
                out=sc_a[:], in0=sc_b[:], in1=PS[5][0:8, :], op=OP.max),
              [bkey, ("PS", 5)], [akey], tag + "m1")

        def rtr2():
            tmax(f"Ct1{t}", scsb[:], ("scsb",), esb, ("esb",), lgsb, ("lgsb",))
            o("V", lambda e: e.tensor_tensor(out=mk1[:], in0=scsb[:], in1=esb[:], op=OP.is_ge),
              [("scsb",), ("esb",)], [("mk1",)], f"Cmk1{t}")
            o("V", lambda e: e.scalar_tensor_tensor(out=lgsb[:], in0=mk1[:], scalar=-1e30,
                                                    in1=scsb[:], op0=OP.mult, op1=OP.add),
              [("mk1",), ("scsb",)], [("lgsb",)], f"Cmsk{t}")
        parts.append(rtr2)

        def rtr3():
            tmax(f"Ct2{t}", lgsb[:], ("lgsb",), esb, ("esb",), lgsb, ("lgsb",))
            o("V", lambda e: e.tensor_tensor(out=mk1[:], in0=scsb[:], in1=esb[:], op=OP.is_ge),
              [("scsb",), ("esb",)], [("mk1",)], f"Cmk2{t}")
            o("V", lambda e: e.tensor_tensor(out=afm[:], in0=mk1[:], in1=scsb[:], op=OP.mult),
              [("mk1",), ("scsb",)], [("afm",)], f"Ca{t}")
            dma("QS", a_d[:, tok:tok + TA], afm[:], [("afm",)], [("a_d", t)], tag=f"Cas{t}")
            if debug_a:
                dma("QS", dbg_a[:, tok:tok + TA], afm[:], [("afm",)], [("dbg_a", t)], tag=f"Cad{t}")
                dma("QS", dbg_y.rearrange("dc p n -> p dc n")[:, :, tok:tok + TA], yt[:],
                    [("yt", dc2) for dc2 in range(DC)], [("dbg_y", t)], tag=f"Cyd{t}")
        parts.append(rtr3)
        return parts

    if "C" in run_phases:
        c_xload(0)
        c_head(0)
        for p in c_wo_parts(0):
            p()
        c_xload(1)
        c_head(1)
        for t in range(NT):
            wparts = c_wo_parts(t + 1) if t + 1 < NT else []
            rest = (c_epi_parts(t - 1) if t >= 1 else []) + c_mid_parts(t)
            wi = 0
            for ri, r in enumerate(rest):
                r()
                while wi * len(rest) < (ri + 1) * len(wparts):
                    wparts[wi]()
                    wi += 1
            while wi < len(wparts):
                wparts[wi]()
                wi += 1
            if t + 2 < NT:
                c_xload(t + 2)
                c_head(t + 2)
        for p in c_epi_parts(NT - 1):
            p()

    em.barrier("C|E")
    ctxC.close()
    ctxABC.close()

    ctxE = ExitStack()
    EE = ctxE.enter_context
    oh8_sb = EE(nc.sbuf_tensor("oh8_sb", [8, 1024], BF16))
    w13 = [EE(nc.sbuf_tensor(f"w13_{i}", [128, 8, 2 * D], BF16)) for i in range(2)]
    w2b = EE(nc.sbuf_tensor("w2b", [128, 8, D], BF16))
    zt2e = [EE(nc.sbuf_tensor(f"zt2e{i}", [128, DC, TE], BF16)) for i in range(2)]
    Xt = EE(nc.sbuf_tensor("Xt", [128, DC, SUB], BF16))
    Hs = EE(nc.sbuf_tensor("Hs", [128, DC, SUB], BF16))
    Yacc = EE(nc.sbuf_tensor("Yacc", [128, DC, TE], F32))
    s1e = EE(nc.sbuf_tensor("s1e", [128, SUB], BF16))
    asb2 = EE(nc.sbuf_tensor("asb2", [128, TE], BF16))
    arow = EE(nc.sbuf_tensor("arow", [8, TE], BF16))
    yep = EE(nc.sbuf_tensor("yep", [128, 2 * SUB], F32))
    outb = EE(nc.sbuf_tensor("outb", [128, 2 * SUB], F32))
    sqe = EE(nc.sbuf_tensor("sqe", [128, SUB], BF16))
    ybb = EE(nc.sbuf_tensor("ybb", [128, SUB], BF16))
    a4sb = EE(nc.sbuf_tensor("a4sb", [128, SUB], BF16))
    b4sb = EE(nc.sbuf_tensor("b4sb", [128, SUB], BF16))

    def wsrc(mat, e):
        if e == 8:
            base = {0: w1s, 1: w3s, 2: w2s}[mat]
            return base.rearrange("(kc p) m -> p kc m", p=128)
        base = {0: w1, 1: w3, 2: w2}[mat]
        return base[e].rearrange("(kc p) m -> p kc m", p=128)

    def e_abc_xt(tt, u, ei, sb2):
        if ei == 8:
            return
        tb = tt % 2
        zt = zt2e[tb]
        pab = 6 + u % 2
        ssl = slice(sb2 * SUB, (sb2 + 1) * SUB)
        o("T", lambda e, ei=ei, sb2=sb2, pab=pab: e.matmul(
            PS[pab][:], oh8_sb[:, ei * 128:(ei + 1) * 128],
            arow[:, sb2 * SUB:(sb2 + 1) * SUB], start=True, stop=True),
          [("oh8_sb",), ("arow",)], [("PS", pab)], f"Eab{tt}.{ei}.{sb2}")
        o("V", lambda e, sb2=sb2, pab=pab: e.tensor_copy(
            asb2[:, sb2 * SUB:(sb2 + 1) * SUB], PS[pab][:]),
          [("PS", pab)], [("asb2", sb2)], f"Eae{tt}.{ei}.{sb2}")
        for dc in range(0, DC, 2):
            arep = bass.AP(tensor=asb2.ap().tensor, offset=sb2 * SUB,
                           ap=[[TE, 128], [0, 2], [1, SUB]])
            o("V", lambda e, dc=dc, ssl=ssl, sb2=sb2, arep=arep, zt=zt: e.tensor_tensor(
                out=Xt[:, dc:dc + 2, :], in0=zt[:, dc:dc + 2, ssl],
                in1=arep, op=OP.mult),
              [("zt", tb, sb2), ("asb2", sb2)], [("Xt", dc), ("Xt", dc + 1)], f"Ex{tt}.{ei}.{sb2}.{dc}")

    def e_epilogue(tt):
        tok = tt * TE
        for sb2 in range(2):
            ssl = slice(sb2 * SUB, (sb2 + 1) * SUB)
            tok2 = tok + sb2 * SUB
            for dc in range(DC):
                o("S", lambda e, dc=dc, ssl=ssl: e.activation(out=sqe[:], in_=Yacc[:, dc, ssl],
                                                              func=AF.Square, bias=zcol[:]),
                  [("Yacc", dc, sb2)], [("sqe",)], f"Efs{tt}.{sb2}.{dc}")
                o("T", lambda e, dc=dc: e.matmul(PS[0][0:1, :], ones_cb[:], sqe[:],
                                                 start=(dc == 0), stop=(dc == DC - 1)),
                  [("ones_cb",), ("sqe",)], [("PS", 0)], f"Efss{tt}.{sb2}.{dc}")
                o("S", lambda e, dc=dc, ssl=ssl: e.activation(out=ybb[:], in_=Yacc[:, dc, ssl],
                                                              func=AF.Copy),
                  [("Yacc", dc, sb2)], [("ybb",)], f"Efc{tt}.{sb2}.{dc}")
                o("T", lambda e, dc=dc: e.matmul(PS[1][0:1, :], ones_cb[:], ybb[:],
                                                 start=(dc == 0), stop=(dc == DC - 1)),
                  [("ones_cb",), ("ybb",)], [("PS", 1)], f"Efsm{tt}.{sb2}.{dc}")
            m4 = rowF("m4"); v4 = rowF("v4"); t4r = rowF("t4r"); r4 = rowF("r4"); q4 = rowF("q4")
            o("V", lambda e: e.tensor_scalar_mul(m4, PS[1][0:1, :], 1.0 / D), [("PS", 1)], [("R0",)])
            o("V", lambda e: e.tensor_scalar_mul(v4, PS[0][0:1, :], 1.0 / D), [("PS", 0)], [("R1",)])
            o("V", lambda e: e.tensor_tensor(out=t4r, in0=m4, in1=m4, op=OP.mult), [("R0",)], [("R2",)])
            o("V", lambda e: e.tensor_tensor(out=v4, in0=v4, in1=t4r, op=OP.subtract),
              [("R1",), ("R2",)], [("R1",)])
            o("S", lambda e: e.activation(out=t4r, in_=v4, func=AF.Ln, bias=epsr[:]), [("R1",)], [("R2",)])
            o("S", lambda e: e.activation(out=r4, in_=t4r, func=AF.Exp, scale=-0.5, bias=zrow[:]),
              [("R2",)], [("R3",)])
            o("V", lambda e: e.tensor_tensor(out=q4, in0=m4, in1=r4, op=OP.mult),
              [("R0",), ("R3",)], [("R4",)])
            r4b = rowB("r4b"); q4b = rowB("q4b")
            o("V", lambda e: e.tensor_copy(r4b, r4), [("R3",)], [("RB0",)])
            o("V", lambda e: e.tensor_copy(q4b, q4), [("R4",)], [("RB1",)])
            o("T", lambda e: e.matmul(PS[0][:], ones_rb[:], r4b, start=True, stop=True),
              [("ones_rb",), ("RB0",)], [("PS", 0)], f"Ea4{tt}.{sb2}")
            o("S", lambda e: e.activation(out=a4sb[:], in_=PS[0][:], func=AF.Copy),
              [("PS", 0)], [("a4sb",)], f"Ea4c{tt}.{sb2}")
            o("T", lambda e: e.matmul(PS[1][:], ones_rb[:], q4b, start=True, stop=True),
              [("ones_rb",), ("RB1",)], [("PS", 1)], f"Eb4{tt}.{sb2}")
            o("S", lambda e: e.activation(out=b4sb[:], in_=PS[1][:], func=AF.Copy),
              [("PS", 1)], [("b4sb",)], f"Eb4c{tt}.{sb2}")
            for dc in range(0, DC, 2):
                dma("QS", yep[:].rearrange("p (a n) -> p a n", a=2),
                    y_d.rearrange("dc p n -> p dc n")[:, dc:dc + 2, tok2:tok2 + SUB],
                    [("y_d", 2 * tt + sb2)], [("yep",)], tag=f"Eyl{tt}.{sb2}.{dc}")
                o("P", lambda e, dc=dc, ssl=ssl: e.tensor_tensor(
                    out=outb[:].rearrange("p (a n) -> p a n", a=2),
                    in0=Yacc[:, dc:dc + 2, ssl], in1=rep2(a4sb, SUB), op=OP.mult),
                  [("Yacc", dc, sb2), ("Yacc", dc + 1, sb2), ("a4sb",)], [("outb",)],
                  f"Eo1{tt}.{sb2}.{dc}")
                o("P", lambda e, dc=dc: e.tensor_tensor(
                    out=outb[:].rearrange("p (a n) -> p a n", a=2),
                    in0=outb[:].rearrange("p (a n) -> p a n", a=2),
                    in1=rep2(b4sb, SUB), op=OP.subtract),
                  [("outb",), ("b4sb",)], [("outb",)], f"Eo2{tt}.{sb2}.{dc}")
                o("P", lambda e, dc=dc: e.tensor_tensor(
                    out=outb[:], in0=outb[:], in1=yep[:], op=OP.add),
                  [("outb",), ("yep",)], [("outb",)], f"Eo3{tt}.{sb2}.{dc}")
                dma("QS", outTr[:, dc:dc + 2, tok2:tok2 + SUB],
                    outb[:].rearrange("p (a n) -> p a n", a=2),
                    [("outb",)],
                    [("out_d", tt, sb2, dc), ("out_d", tt, sb2, dc + 1)], tag=f"Eos{tt}.{sb2}.{dc}")

    if "E" in run_phases:

        dma("QG", oh8_sb[:], oh8, [("d_in",)], [("oh8_sb",)])

        def e_loads(tt):
            tok = tt * TE
            tb = tt % 2
            for h in range(2):
                dma("QA", zt2e[tb][:, :, h * SUB:(h + 1) * SUB],
                    z_d.rearrange("dc p n -> p dc n")[:, :, tok + h * SUB:tok + (h + 1) * SUB],
                    [("z_d", 2 * tt + h)], [("zt", tb, h)], tag=f"Ez{tt}.{h}")

        def e_aload(tt):
            tok = tt * TE
            dma("QG", arow[:], a_d[:, tok:tok + TE], [("a_d", 2 * tt), ("a_d", 2 * tt + 1)],
                [("arow",)], tag=f"Ea{tt}")

        _egpos = [0]
        e_loads(0)
        e_aload(0)
        for tt in range(NTE):
            tok = tt * TE
            if tt + 1 < NTE:
                e_loads(tt + 1)
            units = [(ei, sb2) for ei in range(9) for sb2 in range(2)]
            e_abc_xt(tt, 0, *units[0])
            for u, (ei, sb2) in enumerate(units):
                if sb2 == 0:
                    _egpos[0] += 1
                wp = _egpos[0] % 2
                if sb2 == 0:
                    dma("QG", w13[wp][:, :, 0:D], wsrc(0, ei), [("d_in",)], [("w13", wp, 0)],
                        tag=f"Ew1.{tt}.{ei}")
                    dma("QG", w13[wp][:, :, D:2 * D], wsrc(1, ei), [("d_in",)], [("w13", wp, 1)],
                        tag=f"Ew3.{tt}.{ei}")
                    dma("QG", w2b[:], wsrc(2, ei), [("d_in",)], [("w2b",)],
                        tag=f"Ew2.{tt}.{ei}")
                ssl = slice(sb2 * SUB, (sb2 + 1) * SUB)
                zt = zt2e[tt % 2]
                if ei < 8:
                    xkeys = lambda kc: ("Xt", kc)
                else:
                    xkeys = lambda kc, sb2=sb2, tt=tt: ("zt", tt % 2, sb2)
                for hc in range(DC):
                    p1 = 0 if hc % 2 == 0 else 2
                    p3 = 1 if hc % 2 == 0 else 3
                    for kc in range(DC):
                        rh = Xt[:, kc, :] if ei < 8 else zt[:, kc, ssl]
                        o("T", lambda e, kc=kc, hc=hc, p1=p1, wp=wp, rh=rh: e.matmul(
                            PS[p1][:], w13[wp][:, kc, hc * 128:(hc + 1) * 128], rh,
                            start=(kc == 0), stop=(kc == DC - 1)),
                          [("w13", wp, 0), xkeys(kc)], [("PS", p1)], f"Ep1.{tt}.{ei}.{sb2}.{hc}.{kc}")
                    for kc in range(DC):
                        rh = Xt[:, kc, :] if ei < 8 else zt[:, kc, ssl]
                        o("T", lambda e, kc=kc, hc=hc, p3=p3, wp=wp, rh=rh: e.matmul(
                            PS[p3][:], w13[wp][:, kc, D + hc * 128:D + (hc + 1) * 128], rh,
                            start=(kc == 0), stop=(kc == DC - 1)),
                          [("w13", wp, 1), xkeys(kc)], [("PS", p3)], f"Ep3.{tt}.{ei}.{sb2}.{hc}.{kc}")
                    o("S", lambda e, p1=p1: e.activation(out=s1e[:], in_=PS[p1][:], func=AF.Silu, bias=zcol[:]),
                      [("PS", p1)], [("s1e",)], f"Esl{tt}.{ei}.{sb2}.{hc}")
                    o("V", lambda e, hc=hc, p3=p3: e.tensor_tensor(
                        out=Hs[:, hc, :], in0=s1e[:], in1=PS[p3][:], op=OP.mult),
                      [("s1e",), ("PS", p3)], [("Hs", hc)], f"Eh{tt}.{ei}.{sb2}.{hc}")
                if u == 0 and tt > 0:
                    e_epilogue(tt - 1)
                if u + 1 < len(units):
                    e_abc_xt(tt, u + 1, *units[u + 1])
                for dc in range(DC):
                    py = 4 + dc % 2
                    for kc in range(DC):
                        o("T", lambda e, kc=kc, dc=dc, py=py: e.matmul(
                            PS[py][:], w2b[:, kc, dc * 128:(dc + 1) * 128], Hs[:, kc, :],
                            start=(kc == 0), stop=(kc == DC - 1)),
                          [("w2b",), ("Hs", kc)], [("PS", py)], f"Ew2m.{tt}.{ei}.{sb2}.{dc}.{kc}")
                    ysl = Yacc[:, dc, ssl]
                    if ei == 0:
                        o("V", lambda e, ysl=ysl, py=py: e.tensor_copy(ysl, PS[py][:]),
                          [("PS", py)], [("Yacc", dc, sb2)], f"Eyc{tt}.{ei}.{sb2}.{dc}")
                    else:
                        o("V", lambda e, ysl=ysl, py=py: e.tensor_tensor(
                            out=ysl, in0=ysl, in1=PS[py][:], op=OP.add),
                          [("PS", py), ("Yacc", dc, sb2)], [("Yacc", dc, sb2)],
                          f"Eya{tt}.{ei}.{sb2}.{dc}")
            if u == len(units) - 1 and tt + 1 < NTE:
                e_aload(tt + 1)


    if "E" in run_phases:
        e_epilogue(NTE - 1)

    if debug_y and "E" not in run_phases:
        tmpy = Yacc[:, :, 0:TA]
        for t in range(NT):
            tok = t * TA
            dma("QA", tmpy, y_d.rearrange("dc p n -> p dc n")[:, :, tok:tok + TA],
                [("y_d", t)], [("tmpy",)])
            dma("QS", outTr[:, :, tok:tok + TA], tmpy, [("tmpy",)], [("out_d", t)])

    em.barrier("END")

    ops = em.finalize(nc, sems)
    totals = em.final_totals

    streams = {b: [] for b in set(BODY_OF.values())}
    bodies_all = sorted(streams.keys())
    for i, op_ in enumerate(ops):
        if op_["barrier"]:
            for b in bodies_all:
                streams[b].append(("bar", op_["totals"]))
            continue
        b = BODY_OF[op_["e"]]
        streams[b].append(("op", op_))

    blk = E(nc.Block())

    def emit_body(eng, body):
        for kind, item in streams[body]:
            if kind == "bar":
                if body in ("vector", "scalar", "gpsimd"):
                    eng.drain()
                for s, v in sorted(item.items()):
                    if v > 0:
                        eng.wait_ge(sems[s], v)
                continue
            op_ = item
            for s, v in op_["waits"]:
                eng.wait_ge(sems[s], v)
            if op_.get("need_drain"):
                eng.drain()
            inst = op_["fn"](eng)
            if op_["inc"]:
                n = 16 if op_["e"] in QUEUES else 1
                inst.then_inc(sems[op_["sem"]], n)

    @blk.tensor
    def _(eng):
        emit_body(eng, "tensor")

    @blk.vector
    def _(eng):
        emit_body(eng, "vector")

    @blk.scalar
    def _(eng):
        emit_body(eng, "scalar")

    @blk.gpsimd
    def _(eng):
        emit_body(eng, "gpsimd")

    @blk.sync
    def _(eng):
        emit_body(eng, "sync")

    ctxE.close()
    ctx.close()
    return nc


def kernel(L0_attn_x, L0_attn_norm_g, L0_attn_norm_b, L0_attn_W_h, L0_attn_b_h,
           L0_attn_W_qk, L0_attn_b_qk, L0_attn_qk_gamma, L0_attn_qk_beta,
           L0_attn_rel_emb, L0_attn_W_o, L0_attn_b_o, L0_ffn_gate_w,
           L0_ffn_w1, L0_ffn_w2, L0_ffn_w3, L0_ffn_w1s, L0_ffn_w2s, L0_ffn_w3s):
    f32 = np.float32
    x = np.asarray(L0_attn_x, dtype=f32)
    shared = dict(
        Wh=np.ascontiguousarray(L0_attn_W_h, dtype=f32),
        Wqk=np.ascontiguousarray(L0_attn_W_qk, dtype=f32),
        Wo=np.ascontiguousarray(L0_attn_W_o, dtype=f32),
        bh=np.ascontiguousarray(L0_attn_b_h, dtype=f32),
        bhp=np.ascontiguousarray(np.asarray(L0_attn_b_h, dtype=f32).reshape(32, 128).T),
        bqk=np.ascontiguousarray(L0_attn_b_qk, dtype=f32),
        ngv=np.ascontiguousarray(L0_attn_norm_g, dtype=f32),
        nbv=np.ascontiguousarray(L0_attn_norm_b, dtype=f32),
        nbp=np.ascontiguousarray(np.asarray(L0_attn_norm_b, dtype=f32).reshape(8, 128).T),
        bop=np.ascontiguousarray(np.asarray(L0_attn_b_o, dtype=f32).reshape(8, 128).T),
        bo=np.ascontiguousarray(L0_attn_b_o, dtype=f32),
        qgT=np.ascontiguousarray(np.asarray(L0_attn_qk_gamma, dtype=f32).T),
        qbT=np.ascontiguousarray(np.asarray(L0_attn_qk_beta, dtype=f32).T),
        qg3=np.ascontiguousarray(np.asarray(L0_attn_qk_gamma, dtype=f32)[3:4, :]),
        qb3=np.ascontiguousarray(np.asarray(L0_attn_qk_beta, dtype=f32)[3:4, :]),
        relE=np.ascontiguousarray(L0_attn_rel_emb, dtype=f32),
        gwT=np.ascontiguousarray(np.asarray(L0_ffn_gate_w, dtype=f32).T),
        w1=np.ascontiguousarray(L0_ffn_w1, dtype=f32),
        w2=np.ascontiguousarray(L0_ffn_w2, dtype=f32),
        w3=np.ascontiguousarray(L0_ffn_w3, dtype=f32),
        w1s=np.ascontiguousarray(np.asarray(L0_ffn_w1s, dtype=f32)[0]),
        w2s=np.ascontiguousarray(np.asarray(L0_ffn_w2s, dtype=f32)[0]),
        w3s=np.ascontiguousarray(np.asarray(L0_ffn_w3s, dtype=f32)[0]),
    )
    b1 = bucket1d_np()
    ohC = np.zeros((32, 512), dtype=f32)
    ohC[b1, np.arange(511)] = 1.0
    shared["ohC"] = ohC
    oh8 = np.zeros((8, 1024), dtype=f32)
    for e in range(8):
        oh8[e, e * 128:(e + 1) * 128] = 1.0
    shared["oh8"] = oh8
    shared["antiI"] = np.ascontiguousarray(np.eye(128, dtype=f32)[::-1])
    perm8 = np.zeros((8, 24), dtype=f32)
    for si, s in enumerate((4, 2, 1)):
        for m in range(8):
            perm8[(m + s) % 8, si * 8 + m] = 1.0
    shared["perm8"] = perm8

    nc = build_nc()
    in_maps = []
    for c in range(NCORES):
        m = dict(shared)
        m["xT"] = np.ascontiguousarray(x[c].T)
        in_maps.append(m)
    res = run_bass_kernel_spmd(nc, in_maps, core_ids=list(range(NCORES)))
    out = np.stack([np.asarray(r["outT"]).T for r in res.results])
    return np.ascontiguousarray(out)


if __name__ == "__main__":
    import time
    t0 = time.time()
    nc = build_nc()
    print(f"built ok in {time.time() - t0:.1f}s")


# version 29 (final)
# speedup vs baseline: 1.0319x; 1.0160x over previous
import math
import numpy as np
import ml_dtypes
import concourse.bass as bass
import concourse.mybir as mybir
from concourse.bass_utils import run_bass_kernel_spmd

F32 = mybir.dt.float32
BF16 = mybir.dt.bfloat16
AF = mybir.ActivationFunctionType
OP = mybir.AluOpType
AX = mybir.AxisListType

NCORES = 8
N = 4096
D = 1024
DC = 8
HID = 2048
HC = 16
QKD = 128
GSZ = 256
NG = N // GSZ
TA = 512
NT = N // TA
TE = 1024
NTE = N // TE
SUB = 512
EPS = 1e-5
NEXP = 8

PARANOID = False
QPOOL = {"QS": 10, "QA": 8, "QG": 6}
COMPUTE_ENGS = ("T", "V", "S", "P")
QUEUES = {"QS": "sync", "QA": "scalar", "QG": "gpsimd"}
BODY_OF = {"T": "tensor", "V": "vector", "S": "scalar", "P": "gpsimd",
           "QS": "sync", "QA": "scalar", "QG": "gpsimd"}


def bucket1d_np():
    d = np.arange(-255, 256)
    retv = (d < 0).astype(np.int32) * 16
    a = np.abs(d)
    lg = np.log(np.maximum(a, 1).astype(np.float32) / np.float32(8.0))
    vl = 8 + (lg / np.float32(math.log(16.0)) * np.float32(8.0)).astype(np.int32)
    vl = np.minimum(vl, 15)
    return retv + np.where(a < 8, a, vl)


class Emitter:

    def __init__(self):
        self.ops = []

    def op(self, eng, fn, reads=(), writes=(), tag="", serial=False):
        self.ops.append(dict(e=eng, fn=fn, r=tuple(reads), w=tuple(writes),
                             tag=tag, barrier=False, inc=False, serial=serial,
                             waits=None, dep=set(), drain_dep=0, myval=0))

    def barrier(self, tag="bar"):
        self.ops.append(dict(e=None, fn=None, r=(), w=(), tag=tag, barrier=True,
                             inc=False, serial=False, waits=None, dep=set(),
                             drain_dep=0, myval=0))

    def finalize(self, nc, sems):
        ops = self.ops
        last_w = {}
        readers = {}
        last_bar = -1
        last_op_of = {}
        for i, o in enumerate(ops):
            if o["barrier"]:
                for e, j in last_op_of.items():
                    if j > last_bar:
                        if ops[j]["e"] in COMPUTE_ENGS:
                            ops[j]["inc"] = True
                last_bar = i
                continue
            deps = set()
            for k in o["r"]:
                if k in last_w:
                    deps.add(last_w[k])
            for k in o["w"]:
                if k in last_w:
                    deps.add(last_w[k])
                for j in readers.get(k, ()):
                    deps.add(j)
            e = o["e"]
            for d in deps:
                if d <= last_bar:
                    continue
                pe = ops[d]["e"]
                if pe == e:
                    if e == "T":
                        continue
                    if e in QUEUES:
                        ops[d]["inc"] = True
                        o["dep"].add(d)
                    elif PARANOID:
                        ops[d]["inc"] = True
                        o["dep"].add(d)
                        o["drain_dep"] = max(o["drain_dep"], d + 1)
                    else:
                        o["drain_dep"] = max(o["drain_dep"], d + 1)
                elif pe in QUEUES and e in QUEUES and BODY_OF[pe] == BODY_OF[e]:
                    ops[d]["inc"] = True
                    o["dep"].add(d)
                else:
                    ops[d]["inc"] = True
                    o["dep"].add(d)
            for k in o["w"]:
                last_w[k] = i
                readers[k] = []
            for k in o["r"]:
                readers.setdefault(k, []).append(i)
            last_op_of[e] = i
        counters = {s: 0 for s in sems}
        semval = [0] * len(ops)
        qrot = {q: 0 for q in QUEUES}
        for i, o in enumerate(ops):
            if o["barrier"]:
                continue
            e = o["e"]
            if e in QUEUES:
                j = qrot[e] % QPOOL[e]
                qrot[e] += 1
                name = f"{e}.{j}"
                counters[name] += 16
                semval[i] = counters[name]
                o["inc"] = True
                o["myval"] = counters[name]
                o["sem"] = name
                o["guard"] = counters[name] - 16
            elif o["inc"]:
                counters[e] += 1
                semval[i] = counters[e]
                o["sem"] = e
        bodies = sorted(set(BODY_OF.values()))
        wm = {b: {s: 0 for s in sems} for b in bodies}
        totals_hist = []
        cnt2 = {s: 0 for s in sems}
        last_drain_pos = {b: 0 for b in bodies}
        for i, o in enumerate(ops):
            if o["barrier"]:
                o["totals"] = dict(cnt2)
                for b in bodies:
                    for s in sems:
                        wm[b][s] = max(wm[b][s], cnt2[s])
                continue
            e = o["e"]
            b = BODY_OF[e]
            waits = {}
            for d in o["dep"]:
                ps = ops[d].get("sem") or ops[d]["e"]
                v = semval[d]
                if v > wm[b].get(ps, 0):
                    waits[ps] = max(waits.get(ps, 0), v)
            if e in QUEUES and o.get("guard", 0) > 0:
                s = o["sem"]
                if o["guard"] > wm[b].get(s, 0):
                    waits[s] = max(waits.get(s, 0), o["guard"])
            for s, v in waits.items():
                wm[b][s] = v
            o["waits"] = sorted(waits.items())
            o["need_drain"] = (o["drain_dep"] > last_drain_pos[b])
            if o["need_drain"]:
                last_drain_pos[b] = i
            if o["inc"]:
                cnt2[o["sem"]] += 16 if o["e"] in QUEUES else 1
        self.final_totals = dict(cnt2)
        return ops


def build_nc(run_phases=("A", "B", "C", "E"), debug_y=False, debug_a=False):
    nc = bass.Bass("TRN2", target_bir_lowering=False)
    em = Emitter()

    xT = nc.dram_tensor("xT", [D, N], F32, kind="ExternalInput").ap()
    Wh = nc.dram_tensor("Wh", [D, 2 * HID], F32, kind="ExternalInput").ap()
    Wqk = nc.dram_tensor("Wqk", [D, QKD], F32, kind="ExternalInput").ap()
    Wo = nc.dram_tensor("Wo", [HID, D], F32, kind="ExternalInput").ap()
    bh = nc.dram_tensor("bh", [2 * HID], F32, kind="ExternalInput").ap()
    bhp = nc.dram_tensor("bhp", [128, 32], F32, kind="ExternalInput").ap()
    bqk = nc.dram_tensor("bqk", [QKD], F32, kind="ExternalInput").ap()
    ngv = nc.dram_tensor("ngv", [D], F32, kind="ExternalInput").ap()
    nbv = nc.dram_tensor("nbv", [D], F32, kind="ExternalInput").ap()
    bo = nc.dram_tensor("bo", [D], F32, kind="ExternalInput").ap()
    qgT = nc.dram_tensor("qgT", [128, 4], F32, kind="ExternalInput").ap()
    qbT = nc.dram_tensor("qbT", [128, 4], F32, kind="ExternalInput").ap()
    qg3 = nc.dram_tensor("qg3", [1, 128], F32, kind="ExternalInput").ap()
    qb3 = nc.dram_tensor("qb3", [1, 128], F32, kind="ExternalInput").ap()
    relE = nc.dram_tensor("relE", [32, 1], F32, kind="ExternalInput").ap()
    ohC = nc.dram_tensor("ohC", [32, 512], F32, kind="ExternalInput").ap()
    antiI = nc.dram_tensor("antiI", [128, 128], F32, kind="ExternalInput").ap()
    perm8 = nc.dram_tensor("perm8", [8, 24], F32, kind="ExternalInput").ap()
    nbp = nc.dram_tensor("nbp", [128, 8], F32, kind="ExternalInput").ap()
    bop = nc.dram_tensor("bop", [128, 8], F32, kind="ExternalInput").ap()
    gwT = nc.dram_tensor("gwT", [D, 8], F32, kind="ExternalInput").ap()
    w1 = nc.dram_tensor("w1", [NEXP, D, D], F32, kind="ExternalInput").ap()
    w2 = nc.dram_tensor("w2", [NEXP, D, D], F32, kind="ExternalInput").ap()
    w3 = nc.dram_tensor("w3", [NEXP, D, D], F32, kind="ExternalInput").ap()
    w1s = nc.dram_tensor("w1s", [D, D], F32, kind="ExternalInput").ap()
    w2s = nc.dram_tensor("w2s", [D, D], F32, kind="ExternalInput").ap()
    w3s = nc.dram_tensor("w3s", [D, D], F32, kind="ExternalInput").ap()

    oh8 = nc.dram_tensor("oh8", [8, 1024], F32, kind="ExternalInput").ap()
    outT = nc.dram_tensor("outT", [D, N], F32, kind="ExternalOutput").ap()
    dbg_a = nc.dram_tensor("dbg_a", [8, N], F32, kind="ExternalOutput").ap() if debug_a else None
    dbg_y = nc.dram_tensor("dbg_y", [DC, 128, N], F32, kind="ExternalOutput").ap() if debug_a else None

    vtm_d = nc.dram_tensor("vtm_d", [NT, 128, 4, HID], BF16, kind="Internal").ap()
    gate_d = nc.dram_tensor("gate_d", [NT, HC, 128, TA], BF16, kind="Internal").ap()
    og_d = nc.dram_tensor("og_d", [HC, 128, N], BF16, kind="Internal").ap()
    y_d = nc.dram_tensor("y_d", [DC, 128, N], F32, kind="Internal").ap()
    z_d = nc.dram_tensor("z_d", [DC, 128, N], BF16, kind="Internal").ap()
    a_d = nc.dram_tensor("a_d", [8, N], F32, kind="Internal").ap()
    grow_d = nc.dram_tensor("grow_d", [512], F32, kind="Internal").ap()

    xTr = xT.rearrange("(dc p) n -> p dc n", p=128)
    outTr = outT.rearrange("(dc p) n -> p dc n", p=128)

    from contextlib import ExitStack
    ctx = ExitStack()
    E = ctx.enter_context

    sems = {}
    for s in ("T", "V", "S", "P"):
        sems[s] = E(nc.semaphore("sem_" + s))
    for q, np_ in QPOOL.items():
        for j in range(np_):
            sems[f"{q}.{j}"] = E(nc.semaphore(f"sem_{q}_{j}"))

    PS = [E(nc.psum_tensor(f"PS{i}", [128, 512], F32)) for i in range(8)]
    arF = E(nc.sbuf_tensor("arF", [1, 5120], F32))
    arB = E(nc.sbuf_tensor("arB", [1, 4096], BF16))
    zcol = E(nc.sbuf_tensor("zcol", [128, 1], F32))
    epsr = E(nc.sbuf_tensor("epsr", [1, 1], F32))
    zrow = E(nc.sbuf_tensor("zrow", [1, 1], F32))
    ones_cf = E(nc.sbuf_tensor("ones_cf", [128, 1], F32))
    ones_cb = E(nc.sbuf_tensor("ones_cb", [128, 1], BF16))
    ones_rb = E(nc.sbuf_tensor("ones_rb", [1, 128], BF16))

    ctxABC = ExitStack()
    EABC = ctxABC.enter_context
    ng_row = EABC(nc.sbuf_tensor("ng_row", [1, D], BF16))
    bqk_rowb = EABC(nc.sbuf_tensor("bqk_rowb", [1, 128], BF16))
    bqk_col = EABC(nc.sbuf_tensor("bqk_col", [128, 1], F32))
    ones8f = EABC(nc.sbuf_tensor("ones8f", [1, 8], F32))
    o8col = EABC(nc.sbuf_tensor("o8col", [8, 1], F32))
    c4096 = EABC(nc.sbuf_tensor("c4096", [1, 128], F32))
    qg_pt = EABC(nc.sbuf_tensor("qg_pt", [128, 4], F32))
    qb_pt = EABC(nc.sbuf_tensor("qb_pt", [128, 4], F32))
    qgs = EABC(nc.sbuf_tensor("qgs", [128, 4], F32))
    qbs = EABC(nc.sbuf_tensor("qbs", [128, 4], F32))
    g3bc = EABC(nc.sbuf_tensor("g3bc", [128, 128], BF16))
    b3bc = EABC(nc.sbuf_tensor("b3bc", [128, 128], BF16))
    g3row = EABC(nc.sbuf_tensor("g3row", [1, 128], F32))
    b3row = EABC(nc.sbuf_tensor("b3row", [1, 128], F32))
    antiI_sb = EABC(nc.sbuf_tensor("antiI_sb", [128, 128], F32))
    H0 = EABC(nc.sbuf_tensor("H0", [128, 256], F32))
    H1 = EABC(nc.sbuf_tensor("H1", [128, 256], F32))
    gwT_sb = EABC(nc.sbuf_tensor("gwT_sb", [128, 8, 8], F32))
    srow = EABC(nc.sbuf_tensor("srow", [1, 8], F32))
    bhp_sb = EABC(nc.sbuf_tensor("bhp_sb", [128, 32], F32))
    perm8_sb = EABC(nc.sbuf_tensor("perm8_sb", [8, 24], F32))
    nbp_sb = EABC(nc.sbuf_tensor("nbp_sb", [128, 8], F32))
    bop_sb = EABC(nc.sbuf_tensor("bop_sb", [128, 8], F32))

    _ROW_SLOT = {
        "m1": 0, "vv": 1, "var": 2, "r1": 3, "v2": 4, "r2": 5, "Rr": 6, "Pp": 7, "tmp": 8,
        "m3": 0, "v3": 1, "t3r": 2, "r3": 3, "q3": 4, "mf": 5, "vf": 6, "tfr": 7,
        "rf": 8, "qf": 9, "nmf": 2, "sumr": 0, "rsum": 1, "mx1": 4, "mx2": 7,
        "m4": 0, "v4": 1, "t4r": 2, "r4": 3, "q4": 4,
    }
    def rowF(name, n=512):
        s = _ROW_SLOT[name]
        return arF[0:1, s * 512:s * 512 + n]
    _BROW_SLOT = {"r3b": 0, "q3b": 1, "rfb": 2, "qfb": 3, "r4b": 0, "q4b": 1,
                  "Rbs0": 4, "Pbs0": 5, "Rbs1": 6, "Pbs1": 7}
    def rowB(name, n=512):
        s = _BROW_SLOT[name]
        return arB[0:1, s * 512:s * 512 + n]

    o = em.op

    def dma(q, out, in_, reads, writes, tag="", serial=False, **kw):
        o(q, (lambda eng, out=out, in_=in_, kw=kw: eng.dma_start(out=out, in_=in_, **kw)),
          reads, writes, tag, serial=serial)

    o("V", lambda e: e.memset(zcol[:], 0.0), (), [("zcol",)])
    o("V", lambda e: e.memset(epsr[:], EPS), (), [("epsr",)])
    o("V", lambda e: e.memset(zrow[:], 0.0), (), [("zrow",)])
    o("V", lambda e: e.memset(ones_cf[:], 1.0), (), [("ones_cf",)])
    o("V", lambda e: e.memset(ones_cb[:], 1.0), (), [("ones_cb",)])
    o("V", lambda e: e.memset(ones_rb[:], 1.0), (), [("ones_rb",)])
    o("V", lambda e: e.memset(ones8f[:], 1.0), (), [("ones8f",)])
    o("V", lambda e: e.memset(o8col[:], 1.0), (), [("o8col",)])
    o("V", lambda e: e.memset(c4096[:], 1.0 / N), (), [("c4096",)])
    em.barrier("init")

    dma("QS", bhp_sb[:], bhp, [("d_in",)], [("bhp_sb",)])
    dma("QS", perm8_sb[:], perm8, [("d_in",)], [("perm8",)])
    dma("QS", nbp_sb[:], nbp, [("d_in",)], [("nbp_sb",)])
    dma("QS", bop_sb[:], bop, [("d_in",)], [("bop_sb",)])
    dma("QS", qg_pt[:], qgT, [("d_in",)], [("qg_pt",)])
    dma("QS", qb_pt[:], qbT, [("d_in",)], [("qb_pt",)])
    dma("QS", g3row[:], qg3, [("d_in",)], [("g3row",)])
    dma("QS", b3row[:], qb3, [("d_in",)], [("b3row",)])
    dma("QS", antiI_sb[:], antiI, [("d_in",)], [("antiI_sb",)])
    dma("QS", gwT_sb[:], gwT.rearrange("(kc p) e -> p kc e", p=128), [("d_in",)], [("gwT_sb",)])
    o("QS", lambda e: e.dma_start(out=bqk_col[:], in_=bqk.rearrange("(p u) -> p u", u=1),
                                  allow_slow_non_contiguous=True),
      [("d_in",)], [("bqk_col",)])
    dma("QG", ng_row[:], ngv.rearrange("(a n) -> a n", a=1), [("d_in",)], [("ng_row",)])
    dma("QG", bqk_rowb[:], bqk.rearrange("(a n) -> a n", a=1), [("d_in",)], [("bqk_rowb",)])

    for h, sc in ((0, 1.0 / GSZ), (1, 1.0), (2, 1.0)):
        o("V", lambda e, h=h, sc=sc: e.tensor_scalar_mul(qgs[:, h:h + 1], qg_pt[:, h:h + 1], sc),
          [("qg_pt",)], [("qgs",)])
        o("V", lambda e, h=h, sc=sc: e.tensor_scalar_mul(qbs[:, h:h + 1], qb_pt[:, h:h + 1], sc),
          [("qb_pt",)], [("qbs",)])
    o("T", lambda e: e.matmul(PS[0][:, 0:128], c4096[:], g3row[:], start=True, stop=True),
      [("c4096",), ("g3row",)], [("PS", 0)])
    o("V", lambda e: e.tensor_copy(g3bc[:], PS[0][:, 0:128]), [("PS", 0)], [("g3bc",)])
    o("T", lambda e: e.matmul(PS[1][:, 0:128], c4096[:], b3row[:], start=True, stop=True),
      [("c4096",), ("b3row",)], [("PS", 1)])
    o("V", lambda e: e.tensor_copy(b3bc[:], PS[1][:, 0:128]), [("PS", 1)], [("b3bc",)])
    for kc in range(8):
        o("T", lambda e, kc=kc: e.matmul(PS[3][0:1, 0:8], ones_cf[:], gwT_sb[:, kc, :],
                                         start=(kc == 0), stop=(kc == 7)),
          [("ones_cf",), ("gwT_sb",)], [("PS", 3)])
    o("V", lambda e: e.tensor_copy(srow[:], PS[3][0:1, 0:8]), [("PS", 3)], [("srow",)])

    sqfm = EABC(nc.sbuf_tensor("sqfm", [128, N], BF16))
    linkv = EABC(nc.sbuf_tensor("linkv", [128, HID], F32))
    linkvb = EABC(nc.sbuf_tensor("linkvb", [128, HID], BF16))
    r1b = EABC(nc.sbuf_tensor("r1b", [1, N], BF16))
    p1b = EABC(nc.sbuf_tensor("p1b", [1, N], BF16))

    ctxA = ExitStack()
    EA = ctxA.enter_context
    Wh_sb = EA(nc.sbuf_tensor("Wh_sb", [128, 8, 2 * HID], BF16))
    bh_row = EA(nc.sbuf_tensor("bh_row", [1, HID], BF16))
    ohC_sb = EA(nc.sbuf_tensor("ohC_sb", [32, 512], F32))
    rel_sb = EA(nc.sbuf_tensor("rel_sb", [32, 1], F32))
    grow_sb = EA(nc.sbuf_tensor("grow_sb", [1, 512], F32))
    Wqk_sb = EA(nc.sbuf_tensor("Wqk_sb", [128, 8, QKD], BF16))
    xt2 = [EA(nc.sbuf_tensor(f"xt2_{i}", [128, DC, TA], BF16)) for i in range(2)]
    normed2 = [EA(nc.sbuf_tensor(f"normed2_{i}", [128, DC, TA], BF16)) for i in range(2)]
    tnorm = [EA(nc.sbuf_tensor(f"tnorm{i}", [128, TA], F32)) for i in range(2)]
    sqtm = EA(nc.sbuf_tensor("sqtm", [128, 4, QKD], BF16))
    lktm = EA(nc.sbuf_tensor("lktm", [128, 4, QKD], BF16))
    lkt2 = EA(nc.sbuf_tensor("lkt2", [128, QKD], BF16))
    vtmb = [EA(nc.sbuf_tensor(f"vtmb{i}", [128, HID], BF16)) for i in range(2)]
    gateb = [EA(nc.sbuf_tensor(f"gateb{i}", [128, 2, TA], BF16)) for i in range(2)]

    dma("QG", bh_row[:], bh[0:HID].rearrange("(a n) -> a n", a=1), [("d_in",)], [("bh_row",)])
    dma("QS", ohC_sb[:], ohC, [("d_in",)], [("ohC_sb",)])
    o("QS", lambda e: e.dma_start(out=rel_sb[:], in_=relE, allow_slow_non_contiguous=True),
      [("d_in",)], [("rel_sb",)])
    o("T", lambda e: e.matmul(PS[2][0:1, :], rel_sb[:], ohC_sb[:], start=True, stop=True),
      [("rel_sb",), ("ohC_sb",)], [("PS", 2)])
    o("V", lambda e: e.tensor_scalar_mul(grow_sb[:], PS[2][0:1, :], math.sqrt(128.0)),
      [("PS", 2)], [("grow_sb",)])
    dma("QS", grow_d[0:512], grow_sb[:], [("grow_sb",)], [("grow_d",)])
    gsk0 = bass.AP(tensor=grow_d.tensor, offset=128, ap=[[1, 128], [1, 256]])
    gsk1 = bass.AP(tensor=grow_d.tensor, offset=0, ap=[[1, 128], [1, 256]])
    dma("QS", H0[:], gsk0, [("grow_d",)], [("H0",)])
    dma("QS", H1[:], gsk1, [("grow_d",)], [("H1",)])

    dma("QG", Wh_sb[:].rearrange("p kc (h m) -> p kc h m", m=2048),
        Wh.rearrange("(kc p) (h m) -> p kc h m", p=128, m=2048),
        [("d_in",)], [("Wh_sb",)])
    dma("QG", Wqk_sb[:], Wqk.rearrange("(kc p) m -> p kc m", p=128),
        [("d_in",)], [("Wqk_sb",)])

    def a_stats(t):
        par = t % 2
        xa = xt2[par]
        tok = t * TA
        dma("QG", xa[:], xTr[:, :, tok:tok + TA], [("d_in",)], [("xt", par)], tag=f"Ax{t}")
        xsq = normed2[par][:, 0, :]
        for dc in range(DC):
            o("S", lambda e, dc=dc, xa=xa, xsq=xsq: e.activation(out=xsq, in_=xa[:, dc, :], func=AF.Square, bias=zcol[:]),
              [("xt", par)], [("nsq", par)], f"A0sq{t}.{dc}")
            o("T", lambda e, dc=dc, xa=xa: e.matmul(PS[6][0:1, :], ones_cb[:], xa[:, dc, :],
                                                    start=(dc == 0), stop=(dc == DC - 1)),
              [("ones_cb",), ("xt", par)], [("PS", 6)], f"A0sx{t}.{dc}")
            o("T", lambda e, dc=dc, xsq=xsq: e.matmul(PS[7][0:1, :], ones_cb[:], xsq,
                                             start=(dc == 0), stop=(dc == DC - 1)),
              [("ones_cb",), ("nsq", par)], [("PS", 7)], f"A0sxx{t}.{dc}")

    def a_rows(t):
        par = t % 2
        tok = t * TA
        m1 = rowF("m1"); vv = rowF("vv"); var = rowF("var"); r1 = rowF("r1")
        v2 = rowF("v2"); r2 = rowF("r2"); Rr = rowF("Rr"); Pp = rowF("Pp")
        tmp = rowF("tmp")
        o("V", lambda e: e.tensor_scalar_mul(m1, PS[6][0:1, :], 1.0 / D), [("PS", 6)], [("R0",)])
        o("V", lambda e: e.tensor_scalar_mul(vv, PS[7][0:1, :], 1.0 / D), [("PS", 7)], [("R1",)])
        o("V", lambda e: e.tensor_tensor(out=tmp, in0=m1, in1=m1, op=OP.mult),
          [("R0",)], [("R8",)])
        o("V", lambda e: e.tensor_tensor(out=var, in0=vv, in1=tmp, op=OP.subtract),
          [("R1",), ("R8",)], [("R2",)])
        o("S", lambda e: e.activation(out=tmp, in_=var, func=AF.Ln, bias=epsr[:]),
          [("R2",)], [("R8",)])
        o("S", lambda e: e.activation(out=r1, in_=tmp, func=AF.Exp, scale=-0.5, bias=zrow[:]),
          [("R8",)], [("R3",)])
        o("V", lambda e: e.tensor_tensor(out=v2, in0=var, in1=r1, op=OP.mult),
          [("R2",), ("R3",)], [("R4",)])
        o("V", lambda e: e.tensor_tensor(out=v2, in0=v2, in1=r1, op=OP.mult),
          [("R4",), ("R3",)], [("R4",)])
        o("S", lambda e: e.activation(out=tmp, in_=v2, func=AF.Ln, bias=epsr[:]),
          [("R4",)], [("R8",)])
        o("S", lambda e: e.activation(out=r2, in_=tmp, func=AF.Exp, scale=-0.5, bias=zrow[:]),
          [("R8",)], [("R5",)])
        o("V", lambda e: e.tensor_tensor(out=Rr, in0=r1, in1=r2, op=OP.mult),
          [("R3",), ("R5",)], [("R6",)])
        o("V", lambda e: e.tensor_tensor(out=Pp, in0=m1, in1=Rr, op=OP.mult),
          [("R0",), ("R6",)], [("R7",)])
        Rbs = rowB(f"Rbs{par}"); Pbs = rowB(f"Pbs{par}")
        o("V", lambda e, Rbs=Rbs: e.tensor_copy(Rbs, Rr), [("R6",)], [("RBR", par)])
        o("V", lambda e, Pbs=Pbs: e.tensor_copy(Pbs, Pp), [("R7",)], [("RBP", par)])
        o("V", lambda e, tok=tok: e.tensor_copy(r1b[0:1, tok:tok + TA], r1), [("R3",)], [("r1b", t)])
        o("V", lambda e: e.tensor_tensor(out=tmp, in0=m1, in1=r1, op=OP.mult),
          [("R0",), ("R3",)], [("R8",)])
        o("V", lambda e, tok=tok: e.tensor_copy(p1b[0:1, tok:tok + TA], tmp), [("R8",)], [("p1b", t)])

    def a_normed(t):
        par = t % 2
        xa = xt2[par]
        normed = normed2[par]
        Rbs = rowB(f"Rbs{par}"); Pbs = rowB(f"Pbs{par}")
        for dc in range(DC):
            pa, pb = (0, 1) if dc % 2 == 0 else (2, 3)
            gsl = ng_row[0:1, dc * 128:(dc + 1) * 128]
            o("T", lambda e, pa=pa, gsl=gsl, Rbs=Rbs: e.matmul(
                PS[pa][:], gsl, Rbs, start=True, stop=True),
              [("ng_row",), ("RBR", par)], [("PS", pa)], f"Aba{t}.{dc}")
            o("T", lambda e, pb=pb, gsl=gsl, Pbs=Pbs: e.matmul(
                PS[pb][:], gsl, Pbs, start=True, stop=True),
              [("ng_row",), ("RBP", par)], [("PS", pb)], f"Abb1{t}.{dc}")
            sc = tnorm[dc % 2]
            o("V", lambda e, dc=dc, pa=pa, sc=sc, xa=xa: e.tensor_tensor(
                out=sc[:], in0=xa[:, dc, :], in1=PS[pa][:], op=OP.mult),
              [("xt", par), ("PS", pa)], [("tn", dc % 2)], f"An1{t}.{dc}")
            o("V", lambda e, dc=dc, pb=pb, sc=sc, normed=normed: e.scalar_tensor_tensor(
                out=normed[:, dc, :], in0=sc[:], scalar=nbp_sb[:, dc:dc + 1],
                in1=PS[pb][:], op0=OP.add, op1=OP.subtract),
              [("tn", dc % 2), ("PS", pb), ("nbp_sb",)],
              [("normed", par, dc), ("nsq", par)], f"An2{t}.{dc}")

    def a_proj(t):
        par = t % 2
        normed = normed2[par]
        tok = t * TA
        vg_ps = [0]
        def nxt_ps():
            vg_ps[0] = (vg_ps[0] + 1) % 4
            return 4 + vg_ps[0]
        for kc in range(DC):
            o("T", lambda e, kc=kc: e.matmul(PS[4][:], Wqk_sb[:, kc, :],
                                             normed[:, kc, :], start=(kc == 0), stop=(kc == DC - 1)),
              [("Wqk_sb",), ("normed", par, kc)], [("PS", 4)], f"Aqk{t}.{kc}")
        o("S", lambda e, tok=tok: e.activation(out=sqfm[:, tok:tok + TA], in_=PS[4][:],
                                               func=AF.Silu, bias=bqk_col[:]),
          [("PS", 4), ("bqk_col",)], [("sqfm", t)], f"Aqke{t}")
        for tc in range(4):
            tsl = slice(tc * 128, (tc + 1) * 128)
            for kc in range(DC):
                o("T", lambda e, kc=kc, tsl=tsl: e.matmul(
                    PS[5][:, 0:QKD], normed[:, kc, tsl], Wqk_sb[:, kc, :],
                    start=(kc == 0), stop=False),
                  [("normed", par, kc), ("Wqk_sb",)], [("PS", 5)], f"Aqt{t}.{tc}.{kc}")
            o("T", lambda e: e.matmul(PS[5][:, 0:QKD], ones_rb[:], bqk_rowb[:],
                                      start=False, stop=True),
              [("ones_rb",), ("bqk_rowb",)], [("PS", 5)], f"Aqtb{t}.{tc}")
            o("S", lambda e, tc=tc: e.activation(out=sqtm[:, tc, :], in_=PS[5][:, 0:QKD],
                                                 func=AF.Silu, bias=zcol[:]),
              [("PS", 5)], [("sqtm", tc)], f"Aqte{t}.{tc}")
            o("V", lambda e, tc=tc: e.tensor_tensor(out=lkt2[:], in0=sqtm[:, tc, :],
                                                    in1=g3bc[:], op=OP.mult),
              [("sqtm", tc), ("g3bc",)], [("lkt2",)], f"Alk1{t}.{tc}")
            o("V", lambda e, tc=tc: e.tensor_tensor(out=lktm[:, tc, :], in0=lkt2[:],
                                                    in1=b3bc[:], op=OP.add),
              [("lkt2",), ("b3bc",)], [("lktm", tc)], f"Alk2{t}.{tc}")
            par2 = (t * 4 + tc) % 2
            vb = vtmb[par2]
            for Ncc in range(4):
                ps = nxt_ps()
                for kc in range(DC):
                    o("T", lambda e, kc=kc, tsl=tsl, Ncc=Ncc, ps=ps: e.matmul(
                        PS[ps][:], normed[:, kc, tsl], Wh_sb[:, kc, Ncc * 512:(Ncc + 1) * 512],
                        start=(kc == 0), stop=False),
                      [("normed", par, kc), ("Wh_sb",)], [("PS", ps)], f"Av{t}.{tc}.{Ncc}.{kc}")
                o("T", lambda e, Ncc=Ncc, ps=ps: e.matmul(
                    PS[ps][:], ones_rb[:], bh_row[0:1, Ncc * 512:(Ncc + 1) * 512],
                    start=False, stop=True),
                  [("ones_rb",), ("bh_row",)], [("PS", ps)], f"Avb{t}.{tc}.{Ncc}")
                o("S", lambda e, Ncc=Ncc, ps=ps, vb=vb: e.activation(
                    out=vb[:, Ncc * 512:(Ncc + 1) * 512], in_=PS[ps][:], func=AF.Silu, bias=zcol[:]),
                  [("PS", ps)], [("vtmb", par2, Ncc)], f"Ave{t}.{tc}.{Ncc}")
            for Ncc in range(4):
                o("T", lambda e, tc=tc, Ncc=Ncc, vb=vb: e.matmul(
                    PS[Ncc][:], lktm[:, tc, :], vb[:, Ncc * 512:(Ncc + 1) * 512],
                    start=(tc == 0), stop=(tc == 3)),
                  [("lktm", tc), ("vtmb", par2, Ncc)], [("PS", Ncc)], f"Alkv{t}.{tc}.{Ncc}")
            dma("QS", vtm_d[t, :, tc, :], vb[:], [("vtmb", par2, 0), ("vtmb", par2, 1),
                ("vtmb", par2, 2), ("vtmb", par2, 3)], [("vtm_d", t, tc)], tag=f"Avs{t}.{tc}")
        for Ncc in range(4):
            if t == 0:
                o("V", lambda e, Ncc=Ncc: e.tensor_copy(linkv[:, Ncc * 512:(Ncc + 1) * 512], PS[Ncc][:]),
                  [("PS", Ncc)], [("linkv", Ncc)], f"Alkc{t}.{Ncc}")
            else:
                o("V", lambda e, Ncc=Ncc: e.tensor_tensor(
                    out=linkv[:, Ncc * 512:(Ncc + 1) * 512],
                    in0=linkv[:, Ncc * 512:(Ncc + 1) * 512], in1=PS[Ncc][:], op=OP.add),
                  [("PS", Ncc), ("linkv", Ncc)], [("linkv", Ncc)], f"Alka{t}.{Ncc}")
        def gates():
          for hc in range(HC):
            ps = nxt_ps()
            gb = gateb[(hc // 2) % 2]
            for kc in range(DC):
                o("T", lambda e, kc=kc, hc=hc, ps=ps: e.matmul(
                    PS[ps][:], Wh_sb[:, kc, HID + hc * 128:HID + (hc + 1) * 128],
                    normed[:, kc, :], start=(kc == 0), stop=(kc == DC - 1)),
                  [("Wh_sb",), ("normed", par, kc)], [("PS", ps)], f"Ag{t}.{hc}.{kc}")
            o("S", lambda e, hc=hc, ps=ps, gb=gb: e.activation(
                out=gb[:, hc % 2, :], in_=PS[ps][:], func=AF.Silu,
                bias=bhp_sb[:, 16 + hc:17 + hc]),
              [("PS", ps), ("bhp_sb",)], [("gateb", (hc // 2) % 2, hc % 2)], f"Age{t}.{hc}")
            if hc % 2 == 1:
                h2 = hc // 2
                gb2 = gateb[h2 % 2]
                dma("QS", gate_d[t].rearrange("hc p n -> p hc n")[:, h2 * 2:(h2 + 1) * 2, :],
                    gb2[:], [("gateb", h2 % 2, 0), ("gateb", h2 % 2, 1)],
                    [("gate_d", t, h2)], tag=f"Ags{t}.{h2}")
        return gates

    if "A" in run_phases:
        a_stats(0)
        a_rows(0)
        a_normed(0)
        for t in range(NT):
            if t + 1 < NT:
                a_stats(t + 1)
                a_rows(t + 1)
            gates = a_proj(t)
            if t + 1 < NT:
                a_normed(t + 1)
            gates()
        for Ncc in range(4):
            o("V", lambda e, Ncc=Ncc: e.tensor_copy(linkvb[:, Ncc * 512:(Ncc + 1) * 512],
                                                    linkv[:, Ncc * 512:(Ncc + 1) * 512]),
              [("linkv", Ncc)], [("linkvb",)], f"Alkb{Ncc}")

    em.barrier("A|B")
    ctxA.close()

    ctxB = ExitStack()
    EB = ctxB.enter_context
    vtmg = [EB(nc.sbuf_tensor(f"vtmg{i}", [128, 2, HID], BF16)) for i in range(2)]
    gateg = [EB(nc.sbuf_tensor(f"gateg{i}", [128, HC, 256], BF16)) for i in range(2)]
    hq = [EB(nc.sbuf_tensor(f"hq{i}", [128, 256], BF16)) for i in range(2)]
    hk = [EB(nc.sbuf_tensor(f"hk{i}", [128, 256], BF16)) for i in range(2)]
    hlq = [EB(nc.sbuf_tensor(f"hlq{i}", [128, 256], BF16)) for i in range(2)]
    rsc = [EB(nc.sbuf_tensor(f"rsc{i}", [128, 256], F32)) for i in range(2)]
    attnT = [EB(nc.sbuf_tensor(f"attnT{i}", [128, 256], BF16)) for i in range(4)]
    ogb = [EB(nc.sbuf_tensor(f"ogb{i}", [128, HC, 256], BF16)) for i in range(2)]

    def b_pre(g):
            t, half = g // 2, g % 2
            tok = g * GSZ
            par = g % 2
            vg = vtmg[par]; gg = gateg[par]
            dma("QA", vg[:], vtm_d[t, :, 2 * half:2 * half + 2, :],
                [("vtm_d", t, 2 * half), ("vtm_d", t, 2 * half + 1)], [("vtmg", par)], tag=f"Bv{g}")
            dma("QA", gg[:], gate_d[t].rearrange("hc p n -> p hc n")[:, :, half * 256:half * 256 + 256],
                [("gate_d", t, h8) for h8 in range(8)],
                [("gateg", par)], tag=f"Bg{g}")
            sq = sqfm[:, tok:tok + 256]
            o("V", lambda e, sq=sq, par=par: e.tensor_scalar(
                out=hq[par][:], in0=sq, scalar1=qgs[:, 0:1], scalar2=qbs[:, 0:1],
                op0=OP.mult, op1=OP.add),
              [("sqfm", t), ("qgs",), ("qbs",)], [("hq", par)], f"Bhq{g}")
            o("V", lambda e, sq=sq, par=par: e.tensor_scalar(
                out=hlq[par][:], in0=sq, scalar1=qgs[:, 1:2], scalar2=qbs[:, 1:2],
                op0=OP.mult, op1=OP.add),
              [("sqfm", t), ("qgs",), ("qbs",)], [("hlq", par)], f"Bhlq{g}")
            o("V", lambda e, sq=sq, par=par: e.tensor_scalar(
                out=hk[par][:], in0=sq, scalar1=qgs[:, 2:3], scalar2=qbs[:, 2:3],
                op0=OP.mult, op1=OP.add),
              [("sqfm", t), ("qgs",), ("qbs",)], [("hk", par)], f"Bhk{g}")
            for jc in range(2):
                ps = jc
                Hj = H0 if jc == 0 else H1
                o("T", lambda e, ps=ps, par=par, jc=jc: e.matmul(
                    PS[ps][:, 0:256], hk[par][:, jc * 128:(jc + 1) * 128], hq[par][:],
                    start=True, stop=False),
                  [("hk", par), ("hq", par)], [("PS", ps)], f"Bsim{g}.{jc}")
                o("T", lambda e, ps=ps, Hj=Hj: e.matmul(
                    PS[ps][:, 0:256], antiI_sb[:], Hj[:], start=False, stop=True),
                  [("antiI_sb",), ("H0",), ("H1",)], [("PS", ps)], f"Bbias{g}.{jc}")
                o("S", lambda e, ps=ps, jc=jc: e.activation(out=rsc[jc][:], in_=PS[ps][:, 0:256],
                                                            func=AF.Relu, bias=zcol[:]),
                  [("PS", ps)], [("rsc", jc)], f"Brelu{g}.{jc}")
                at = attnT[par * 2 + jc]
                o("V", lambda e, jc=jc, at=at: e.tensor_tensor(out=at[:], in0=rsc[jc][:],
                                                               in1=rsc[jc][:], op=OP.mult),
                  [("rsc", jc)], [("attnT", par * 2 + jc)], f"Bsq{g}.{jc}")
    def b_qolo(g):
            t, half = g // 2, g % 2
            tok = g * GSZ
            par = g % 2
            ob = ogb[par]
            for ec in range(HC):
                ps = 2 + ec % 4
                o("T", lambda e, ec=ec, ps=ps, par=par: e.matmul(
                    PS[ps][:, 0:256], vtmg[par][:, 0, ec * 128:(ec + 1) * 128], attnT[par * 2][:],
                    start=True, stop=False),
                  [("vtmg", par), ("attnT", par * 2)], [("PS", ps)], f"Bqo1{g}.{ec}")
                o("T", lambda e, ec=ec, ps=ps, par=par: e.matmul(
                    PS[ps][:, 0:256], vtmg[par][:, 1, ec * 128:(ec + 1) * 128], attnT[par * 2 + 1][:],
                    start=False, stop=False),
                  [("vtmg", par), ("attnT", par * 2 + 1)], [("PS", ps)], f"Bqo2{g}.{ec}")
                o("T", lambda e, ec=ec, ps=ps, par=par: e.matmul(
                    PS[ps][:, 0:256], linkvb[:, ec * 128:(ec + 1) * 128], hlq[par][:],
                    start=False, stop=True),
                  [("linkvb",), ("hlq", par)], [("PS", ps)], f"Blo{g}.{ec}")
                o("V", lambda e, ec=ec, ps=ps, par=par, ob=ob: e.tensor_tensor(
                    out=ob[:, ec, :], in0=gateg[par][:, ec, :], in1=PS[ps][:, 0:256], op=OP.mult),
                  [("gateg", par), ("PS", ps)], [("ogb", par, ec)], f"Bog{g}.{ec}")
            dma("QS", og_d.rearrange("ec p n -> p ec n")[:, :, tok:tok + 256], ob[:],
                [("ogb", par, ec2) for ec2 in range(HC)], [("og_d", g)], tag=f"Bogs{g}")

    if "B" in run_phases:
        b_pre(0)
        for g in range(NG):
            if g + 1 < NG:
                b_pre(g + 1)
            b_qolo(g)

    em.barrier("B|C")
    ctxB.close()

    ctxC = ExitStack()
    EC = ctxC.enter_context
    Wo_sb = EC(nc.sbuf_tensor("Wo_sb", [128, HC, D], BF16))
    ogt = EC(nc.sbuf_tensor("ogt", [128, HC, TA], BF16))
    xtc2 = [EC(nc.sbuf_tensor(f"xtc2_{i}", [128, DC, TA], BF16)) for i in range(2)]
    ao2 = [EC(nc.sbuf_tensor(f"ao2_{i}", [128, DC, TA], BF16)) for i in range(2)]
    yt = EC(nc.sbuf_tensor("yt", [128, DC, TA], F32))
    zt2 = [EC(nc.sbuf_tensor(f"zt2_{i}", [128, 2 * TA], BF16)) for i in range(2)]
    a1sb = EC(nc.sbuf_tensor("a1sb", [128, TA], BF16))
    pb1sb = EC(nc.sbuf_tensor("pb1sb", [128, TA], BF16))
    a3sb = EC(nc.sbuf_tensor("a3sb", [128, TA], BF16))
    b3sb = EC(nc.sbuf_tensor("b3sb", [128, TA], BF16))
    afsb = EC(nc.sbuf_tensor("afsb", [128, TA], BF16))
    zsc = EC(nc.sbuf_tensor("zsc", [128, 2 * TA], F32))
    bfsb = EC(nc.sbuf_tensor("bfsb", [128, TA], BF16))
    tc1b = EC(nc.sbuf_tensor("tc1b", [128, 2 * TA], F32))
    tc2b = EC(nc.sbuf_tensor("tc2b", [128, 2 * TA], F32))
    casb = EC(nc.sbuf_tensor("casb", [128, 2 * TA], BF16))
    cpl = EC(nc.sbuf_tensor("cpl", [128, 2 * TA], BF16))
    lgsb = EC(nc.sbuf_tensor("lgsb", [8, TA], F32))
    esb = EC(nc.sbuf_tensor("esb", [8, TA], F32))
    scsb = EC(nc.sbuf_tensor("scsb", [8, TA], F32))
    mk1 = EC(nc.sbuf_tensor("mk1", [8, TA], BF16))
    afm = EC(nc.sbuf_tensor("afm", [8, TA], F32))

    dma("QG", Wo_sb[:], Wo.rearrange("(ec p) m -> p ec m", p=128),
        [("d_in",)], [("Wo_sb",)])


    def rep2(tens, n=TA):
        return bass.AP(tensor=tens.ap().tensor, offset=0, ap=[[n, 128], [0, 2], [1, n]])

    def c_xload(t):
        par = t % 2
        tok = t * TA
        dma("QG", xtc2[par][:], xTr[:, :, tok:tok + TA], [("d_in",)], [("xtc", par)], tag=f"Cx{t}")

    def c_head(t):
        tok = t * TA
        dma("QS", ogt[:], og_d.rearrange("ec p n -> p ec n")[:, :, tok:tok + TA],
            [("og_d", 2 * t), ("og_d", 2 * t + 1)], [("ogt",)], tag=f"Cog{t}")
        o("T", lambda e, tok=tok: e.matmul(PS[2][:], ones_rb[:], r1b[0:1, tok:tok + TA],
                                           start=True, stop=True),
          [("ones_rb",), ("r1b", t)], [("PS", 2)], f"Ca1{t}")
        o("S", lambda e: e.activation(out=a1sb[:], in_=PS[2][:], func=AF.Copy),
          [("PS", 2)], [("a1sb",)], f"Ca1c{t}")
        o("T", lambda e, tok=tok: e.matmul(PS[2][:], ones_rb[:], p1b[0:1, tok:tok + TA],
                                           start=True, stop=True),
          [("ones_rb",), ("p1b", t)], [("PS", 2)], f"Cb1{t}")
        o("S", lambda e: e.activation(out=pb1sb[:], in_=PS[2][:], func=AF.Copy),
          [("PS", 2)], [("pb1sb",)], f"Cb1c{t}")

    def c_wo_parts(t):
        par = t % 2
        xc = xtc2[par]
        aot = ao2[par]
        ps_s, ps_q = (3, 4) if par == 0 else (6, 7)
        parts = []

        def wo_mms(d2):
            pwo = d2 % 2
            for ec in range(HC):
                o("T", lambda e, d2=d2, ec=ec, pwo=pwo: e.matmul(
                    PS[pwo][:], Wo_sb[:, ec, d2 * 128:(d2 + 1) * 128], ogt[:, ec, :],
                    start=(ec == 0), stop=(ec == HC - 1)),
                  [("Wo_sb",), ("ogt",)], [("PS", pwo)], f"Cwo{t}.{d2}.{ec}")

        def mk(dc):
            def th():
                wo_mms(dc + 1)
                o("V", lambda e, dc=dc, xc=xc: e.tensor_tensor(
                    out=tc1b[:].rearrange("p (a n) -> p a n", a=2), in0=xc[:, dc:dc + 2, :],
                    in1=rep2(a1sb), op=OP.mult),
                  [("xtc", par), ("a1sb",)], [("tc1b",)], f"Cu1{t}.{dc}")
                for d2 in (dc, dc + 1):
                    pwo = d2 % 2
                    o("V", lambda e, d2=d2, pwo=pwo: e.scalar_tensor_tensor(
                        out=tc2b[:, (d2 - dc) * TA:(d2 - dc + 1) * TA],
                        in0=tc1b[:, (d2 - dc) * TA:(d2 - dc + 1) * TA],
                        scalar=bop_sb[:, d2:d2 + 1], in1=PS[pwo][:],
                        op0=OP.add, op1=OP.add),
                      [("tc1b",), ("PS", pwo), ("bop_sb",)], [("tc2b",)], f"Cu2{t}.{d2}")
                o("V", lambda e, dc=dc, aot=aot: e.tensor_tensor(
                    out=aot[:, dc:dc + 2, :], in0=tc2b[:].rearrange("p (a n) -> p a n", a=2),
                    in1=rep2(pb1sb), op=OP.subtract),
                  [("tc2b",), ("pb1sb",)], [("ao", par, dc), ("ao", par, dc + 1)], f"Cao{t}.{dc}")
                o("S", lambda e, dc=dc, aot=aot: e.activation(
                    out=casb[:], in_=aot[:, dc:dc + 2, :].rearrange("p a n -> p (a n)"),
                    func=AF.Square, bias=zcol[:]),
                  [("ao", par, dc), ("ao", par, dc + 1)], [("casb",)], f"Csq{t}.{dc}")
                for d2 in (dc, dc + 1):
                    o("T", lambda e, d2=d2, dc=dc, ps_q=ps_q: e.matmul(
                        PS[ps_q][0:1, :], ones_cb[:], casb[:, (d2 - dc) * TA:(d2 - dc + 1) * TA],
                        start=(d2 == 0), stop=(d2 == DC - 1)),
                      [("ones_cb",), ("casb",)], [("PS", ps_q)], f"Css{t}.{d2}")
                    o("T", lambda e, d2=d2, ps_s=ps_s, aot=aot: e.matmul(
                        PS[ps_s][0:1, :], ones_cb[:], aot[:, d2, :],
                        start=(d2 == 0), stop=(d2 == DC - 1)),
                      [("ones_cb",), ("ao", par, d2)], [("PS", ps_s)], f"Csm{t}.{d2}")
            return th
        for dc in range(0, DC, 2):
            parts.append(lambda dc=dc: wo_mms(dc))
            parts.append(mk(dc))
        return parts

    def c_mid_parts(t):
        par = t % 2
        xc = xtc2[par]
        aot = ao2[par]
        ps_s, ps_q = (3, 4) if par == 0 else (6, 7)
        tok = t * TA
        parts = []

        def head():
            m3 = rowF("m3"); v3 = rowF("v3"); t3r = rowF("t3r"); r3 = rowF("r3"); q3 = rowF("q3")
            o("V", lambda e, ps_s=ps_s: e.tensor_scalar_mul(m3, PS[ps_s][0:1, :], 1.0 / D), [("PS", ps_s)], [("R0",)])
            o("V", lambda e, ps_q=ps_q: e.tensor_scalar_mul(v3, PS[ps_q][0:1, :], 1.0 / D), [("PS", ps_q)], [("R1",)])
            o("V", lambda e: e.tensor_tensor(out=t3r, in0=m3, in1=m3, op=OP.mult), [("R0",)], [("R2",)])
            o("V", lambda e: e.tensor_tensor(out=v3, in0=v3, in1=t3r, op=OP.subtract),
              [("R1",), ("R2",)], [("R1",)])
            o("S", lambda e: e.activation(out=t3r, in_=v3, func=AF.Ln, bias=epsr[:]), [("R1",)], [("R2",)])
            o("S", lambda e: e.activation(out=r3, in_=t3r, func=AF.Exp, scale=-0.5, bias=zrow[:]),
              [("R2",)], [("R3",)])
            o("V", lambda e: e.tensor_tensor(out=q3, in0=m3, in1=r3, op=OP.mult),
              [("R0",), ("R3",)], [("R4",)])
            r3b = rowB("r3b"); q3b = rowB("q3b")
            o("V", lambda e: e.tensor_copy(r3b, r3), [("R3",)], [("RB0",)])
            o("V", lambda e: e.tensor_copy(q3b, q3), [("R4",)], [("RB1",)])
            o("T", lambda e: e.matmul(PS[2][:], ones_rb[:], r3b, start=True, stop=True),
              [("ones_rb",), ("RB0",)], [("PS", 2)], f"Ca3{t}")
            o("S", lambda e: e.activation(out=a3sb[:], in_=PS[2][:], func=AF.Copy),
              [("PS", 2)], [("a3sb",)], f"Ca3c{t}")
            o("T", lambda e: e.matmul(PS[2][:], ones_rb[:], q3b, start=True, stop=True),
              [("ones_rb",), ("RB1",)], [("PS", 2)], f"Cb3{t}")
            o("S", lambda e: e.activation(out=b3sb[:], in_=PS[2][:], func=AF.Copy),
              [("PS", 2)], [("b3sb",)], f"Cb3c{t}")
        parts.append(head)

        def mkdc(dc):
            def th():
                o("V", lambda e, dc=dc, aot=aot: e.tensor_tensor(
                    out=tc1b[:].rearrange("p (a n) -> p a n", a=2), in0=aot[:, dc:dc + 2, :],
                    in1=rep2(a3sb), op=OP.mult),
                  [("ao", par, dc), ("ao", par, dc + 1), ("a3sb",)], [("tc1b",)], f"Cy1{t}.{dc}")
                o("V", lambda e, dc=dc: e.tensor_tensor(
                    out=tc2b[:].rearrange("p (a n) -> p a n", a=2),
                    in0=tc1b[:].rearrange("p (a n) -> p a n", a=2),
                    in1=rep2(b3sb), op=OP.subtract),
                  [("tc1b",), ("b3sb",)], [("tc2b",)], f"Cy2{t}.{dc}")
                o("V", lambda e, dc=dc, xc=xc: e.tensor_tensor(
                    out=yt[:, dc:dc + 2, :], in0=tc2b[:].rearrange("p (a n) -> p a n", a=2),
                    in1=xc[:, dc:dc + 2, :], op=OP.add),
                  [("tc2b",), ("xtc", par)], [("yt", dc), ("yt", dc + 1)], f"Cy3{t}.{dc}")
                o("S", lambda e, dc=dc: e.activation(
                    out=casb[:], in_=yt[:, dc:dc + 2, :].rearrange("p a n -> p (a n)"),
                    func=AF.Square, bias=zcol[:]),
                  [("yt", dc), ("yt", dc + 1)], [("casb",)], f"Cysq{t}.{dc}")
                o("V", lambda e, dc=dc: e.tensor_copy(
                    cpl[:], yt[:, dc:dc + 2, :].rearrange("p a n -> p (a n)")),
                  [("yt", dc), ("yt", dc + 1)], [("cpl",)], f"Cyc{t}.{dc}")
                for d2 in (dc, dc + 1):
                    o("T", lambda e, d2=d2, dc=dc, ps_q=ps_q: e.matmul(
                        PS[ps_q][0:1, :], ones_cb[:], casb[:, (d2 - dc) * TA:(d2 - dc + 1) * TA],
                        start=(d2 == 0), stop=(d2 == DC - 1)),
                      [("ones_cb",), ("casb",)], [("PS", ps_q)], f"Cyss{t}.{d2}")
                    o("T", lambda e, d2=d2, dc=dc, ps_s=ps_s: e.matmul(
                        PS[ps_s][0:1, :], ones_cb[:], cpl[:, (d2 - dc) * TA:(d2 - dc + 1) * TA],
                        start=(d2 == 0), stop=(d2 == DC - 1)),
                      [("ones_cb",), ("cpl",)], [("PS", ps_s)], f"Cysm{t}.{d2}")
                    o("T", lambda e, d2=d2: e.matmul(PS[5][0:8, :], gwT_sb[:, d2, :], yt[:, d2, :],
                                                     start=(d2 == 0), stop=False),
                      [("gwT_sb",), ("yt", d2)], [("PS", 5)], f"Clg{t}.{d2}")
            return th
        for dc in range(0, DC, 2):
            parts.append(mkdc(dc))

        def tail():
            dma("QS", y_d.rearrange("dc p n -> p dc n")[:, :, tok:tok + TA], yt[:],
                [("yt", dc2) for dc2 in range(DC)], [("y_d", t)], tag=f"Cys{t}")
            mf = rowF("mf"); vf = rowF("vf"); tfr = rowF("tfr"); rf = rowF("rf")
            qf = rowF("qf"); nmf = rowF("nmf")
            o("V", lambda e, ps_s=ps_s: e.tensor_scalar_mul(mf, PS[ps_s][0:1, :], 1.0 / D), [("PS", ps_s)], [("R5",)])
            o("V", lambda e, ps_q=ps_q: e.tensor_scalar_mul(vf, PS[ps_q][0:1, :], 1.0 / D), [("PS", ps_q)], [("R6",)])
            o("V", lambda e: e.tensor_tensor(out=tfr, in0=mf, in1=mf, op=OP.mult), [("R5",)], [("R7",)])
            o("V", lambda e: e.tensor_tensor(out=vf, in0=vf, in1=tfr, op=OP.subtract),
              [("R6",), ("R7",)], [("R6",)])
            o("S", lambda e: e.activation(out=tfr, in_=vf, func=AF.Ln, bias=epsr[:]), [("R6",)], [("R7",)])
            o("S", lambda e: e.activation(out=rf, in_=tfr, func=AF.Exp, scale=-0.5, bias=zrow[:]),
              [("R7",)], [("R8",)])
            o("V", lambda e: e.tensor_tensor(out=qf, in0=mf, in1=rf, op=OP.mult),
              [("R5",), ("R8",)], [("R9",)])
            o("V", lambda e: e.tensor_scalar_mul(nmf, mf, -1.0), [("R5",)], [("R2",)])
            rfb = rowB("rfb"); qfb = rowB("qfb")
            o("V", lambda e: e.tensor_copy(rfb, rf), [("R8",)], [("RB2",)])
            o("V", lambda e: e.tensor_copy(qfb, qf), [("R9",)], [("RB3",)])
            o("T", lambda e: e.matmul(PS[5][0:8, :], srow[:], nmf, start=False, stop=True),
              [("srow",), ("R2",)], [("PS", 5)], f"Clgf{t}")
        parts.append(tail)
        return parts

    def c_epi_parts(t):
        tok = t * TA
        parts = []

        def bcasts():
            o("T", lambda e: e.matmul(PS[2][:], ones_rb[:], rowB("rfb"), start=True, stop=True),
              [("ones_rb",), ("RB2",)], [("PS", 2)], f"Caf{t}")
            o("S", lambda e: e.activation(out=afsb[:], in_=PS[2][:], func=AF.Copy),
              [("PS", 2)], [("afsb",)], f"Cafc{t}")
            o("T", lambda e: e.matmul(PS[2][:], ones_rb[:], rowB("qfb"), start=True, stop=True),
              [("ones_rb",), ("RB3",)], [("PS", 2)], f"Cbf{t}")
            o("S", lambda e: e.activation(out=bfsb[:], in_=PS[2][:], func=AF.Copy),
              [("PS", 2)], [("bfsb",)], f"Cbfc{t}")
        parts.append(bcasts)

        def mkz(dc0):
            def th():
                zb = zt2[(dc0 // 2) % 2]
                o("P", lambda e, dc0=dc0: e.tensor_tensor(
                    out=zsc[:].rearrange("p (a n) -> p a n", a=2), in0=yt[:, dc0:dc0 + 2, :],
                    in1=rep2(afsb), op=OP.mult),
                  [("yt", dc0), ("yt", dc0 + 1), ("afsb",)], [("zsc",)], f"Cz1{t}.{dc0}")
                o("P", lambda e, dc0=dc0, zb=zb: e.tensor_tensor(
                    out=zb[:].rearrange("p (a n) -> p a n", a=2),
                    in0=zsc[:].rearrange("p (a n) -> p a n", a=2),
                    in1=rep2(bfsb), op=OP.subtract),
                  [("zsc",), ("bfsb",)], [("zt2", (dc0 // 2) % 2)], f"Cz2{t}.{dc0}")
                dma("QS", z_d.rearrange("dc p n -> p dc n")[:, dc0:dc0 + 2, tok:tok + TA], zb[:].rearrange("p (a n) -> p a n", a=2),
                    [("zt2", (dc0 // 2) % 2)], [("z_d", t, dc0), ("z_d", t, dc0 + 1)], tag=f"Czs{t}.{dc0}")
            return th
        for dc0 in range(0, DC, 2):
            parts.append(mkz(dc0))

        def router1():
            o("V", lambda e: e.tensor_copy(esb[:], PS[5][0:8, :]), [("PS", 5)], [("esb",)], f"Cl0{t}")
            o("T", lambda e: e.matmul(PS[2][0:8, :], ones8f[:], rowF("rf"), start=True, stop=True),
              [("ones8f",), ("R8",)], [("PS", 2)], f"Crb{t}")
            o("V", lambda e: e.tensor_tensor(out=lgsb[:], in0=esb[:], in1=PS[2][0:8, :], op=OP.mult),
              [("esb",), ("PS", 2)], [("lgsb",)], f"Clg2{t}")
            o("S", lambda e: e.activation(out=esb[:], in_=lgsb[:], func=AF.Exp, bias=zcol[0:8, :]),
              [("lgsb",)], [("esb",)], f"Cexp{t}")
            o("T", lambda e: e.matmul(PS[5][0:1, :], o8col[:], esb[:], start=True, stop=True),
              [("o8col",), ("esb",)], [("PS", 5)], f"Cse{t}")
            sumr = rowF("sumr"); rsum = rowF("rsum")
            o("V", lambda e: e.tensor_copy(sumr, PS[5][0:1, :]), [("PS", 5)], [("R0",)])
            o("V", lambda e: e.reciprocal(rsum, sumr), [("R0",)], [("R1",)])
            o("T", lambda e: e.matmul(PS[2][0:8, :], ones8f[:], rsum, start=True, stop=True),
              [("ones8f",), ("R1",)], [("PS", 2)], f"Crs{t}")
            o("V", lambda e: e.tensor_tensor(out=scsb[:], in0=esb[:], in1=PS[2][0:8, :], op=OP.mult),
              [("esb",), ("PS", 2)], [("scsb",)], f"Csc{t}")
        parts.append(router1)

        def tmax(tag, src, skey, sc_a, akey, sc_b, bkey):
            o("T", lambda e, src=src: e.matmul(PS[5][0:8, :], perm8_sb[:, 0:8], src,
                                               start=True, stop=True),
              [("perm8",), skey], [("PS", 5)], tag + "p4")
            o("V", lambda e, src=src, sc_a=sc_a: e.tensor_tensor(
                out=sc_a[:], in0=src, in1=PS[5][0:8, :], op=OP.max),
              [skey, ("PS", 5)], [akey], tag + "m4")
            o("T", lambda e, sc_a=sc_a: e.matmul(PS[2][0:8, :], perm8_sb[:, 8:16], sc_a[:],
                                                 start=True, stop=True),
              [("perm8",), akey], [("PS", 2)], tag + "p2")
            o("V", lambda e, sc_a=sc_a, sc_b=sc_b: e.tensor_tensor(
                out=sc_b[:], in0=sc_a[:], in1=PS[2][0:8, :], op=OP.max),
              [akey, ("PS", 2)], [bkey], tag + "m2")
            o("T", lambda e, sc_b=sc_b: e.matmul(PS[5][0:8, :], perm8_sb[:, 16:24], sc_b[:],
                                                 start=True, stop=True),
              [("perm8",), bkey], [("PS", 5)], tag + "p1")
            o("V", lambda e, sc_b=sc_b, sc_a=sc_a: e.tensor_tensor(
                out=sc_a[:], in0=sc_b[:], in1=PS[5][0:8, :], op=OP.max),
              [bkey, ("PS", 5)], [akey], tag + "m1")

        def rtr2():
            tmax(f"Ct1{t}", scsb[:], ("scsb",), esb, ("esb",), lgsb, ("lgsb",))
            o("V", lambda e: e.tensor_tensor(out=mk1[:], in0=scsb[:], in1=esb[:], op=OP.is_ge),
              [("scsb",), ("esb",)], [("mk1",)], f"Cmk1{t}")
            o("V", lambda e: e.scalar_tensor_tensor(out=lgsb[:], in0=mk1[:], scalar=-1e30,
                                                    in1=scsb[:], op0=OP.mult, op1=OP.add),
              [("mk1",), ("scsb",)], [("lgsb",)], f"Cmsk{t}")
        parts.append(rtr2)

        def rtr3():
            tmax(f"Ct2{t}", lgsb[:], ("lgsb",), esb, ("esb",), lgsb, ("lgsb",))
            o("V", lambda e: e.tensor_tensor(out=mk1[:], in0=scsb[:], in1=esb[:], op=OP.is_ge),
              [("scsb",), ("esb",)], [("mk1",)], f"Cmk2{t}")
            o("V", lambda e: e.tensor_tensor(out=afm[:], in0=mk1[:], in1=scsb[:], op=OP.mult),
              [("mk1",), ("scsb",)], [("afm",)], f"Ca{t}")
            dma("QS", a_d[:, tok:tok + TA], afm[:], [("afm",)], [("a_d", t)], tag=f"Cas{t}")
            if debug_a:
                dma("QS", dbg_a[:, tok:tok + TA], afm[:], [("afm",)], [("dbg_a", t)], tag=f"Cad{t}")
                dma("QS", dbg_y.rearrange("dc p n -> p dc n")[:, :, tok:tok + TA], yt[:],
                    [("yt", dc2) for dc2 in range(DC)], [("dbg_y", t)], tag=f"Cyd{t}")
        parts.append(rtr3)
        return parts

    if "C" in run_phases:
        c_xload(0)
        c_head(0)
        for p in c_wo_parts(0):
            p()
        c_xload(1)
        c_head(1)
        for t in range(NT):
            wparts = c_wo_parts(t + 1) if t + 1 < NT else []
            rest = (c_epi_parts(t - 1) if t >= 1 else []) + c_mid_parts(t)
            wi = 0
            for ri, r in enumerate(rest):
                r()
                while wi * len(rest) < (ri + 1) * len(wparts):
                    wparts[wi]()
                    wi += 1
            while wi < len(wparts):
                wparts[wi]()
                wi += 1
            if t + 2 < NT:
                c_xload(t + 2)
                c_head(t + 2)
        for p in c_epi_parts(NT - 1):
            p()

    em.barrier("C|E")
    ctxC.close()
    ctxABC.close()

    ctxE = ExitStack()
    EE = ctxE.enter_context
    oh8_sb = EE(nc.sbuf_tensor("oh8_sb", [8, 1024], BF16))
    w13 = [EE(nc.sbuf_tensor(f"w13_{i}", [128, 8, 2 * D], BF16)) for i in range(2)]
    w2b = EE(nc.sbuf_tensor("w2b", [128, 8, D], BF16))
    zt2e = [EE(nc.sbuf_tensor(f"zt2e{i}", [128, DC, TE], BF16)) for i in range(2)]
    Xt = EE(nc.sbuf_tensor("Xt", [128, DC, SUB], BF16))
    Hs = EE(nc.sbuf_tensor("Hs", [128, DC, SUB], BF16))
    Yacc = EE(nc.sbuf_tensor("Yacc", [128, DC, TE], F32))
    s1e = EE(nc.sbuf_tensor("s1e", [128, SUB], BF16))
    asb2 = EE(nc.sbuf_tensor("asb2", [128, TE], BF16))
    arow = EE(nc.sbuf_tensor("arow", [8, TE], BF16))
    yep = EE(nc.sbuf_tensor("yep", [128, 2 * SUB], F32))
    outb = EE(nc.sbuf_tensor("outb", [128, 2 * SUB], F32))
    sqe = EE(nc.sbuf_tensor("sqe", [128, SUB], BF16))
    ybb = EE(nc.sbuf_tensor("ybb", [128, SUB], BF16))
    a4sb = EE(nc.sbuf_tensor("a4sb", [128, SUB], BF16))
    b4sb = EE(nc.sbuf_tensor("b4sb", [128, SUB], BF16))

    def wsrc(mat, e):
        if e == 8:
            base = {0: w1s, 1: w3s, 2: w2s}[mat]
            return base.rearrange("(kc p) m -> p kc m", p=128)
        base = {0: w1, 1: w3, 2: w2}[mat]
        return base[e].rearrange("(kc p) m -> p kc m", p=128)

    def e_abc_xt(tt, u, ei, sb2):
        if ei == 8:
            return
        tb = tt % 2
        zt = zt2e[tb]
        pab = 6 + u % 2
        ssl = slice(sb2 * SUB, (sb2 + 1) * SUB)
        o("T", lambda e, ei=ei, sb2=sb2, pab=pab: e.matmul(
            PS[pab][:], oh8_sb[:, ei * 128:(ei + 1) * 128],
            arow[:, sb2 * SUB:(sb2 + 1) * SUB], start=True, stop=True),
          [("oh8_sb",), ("arow",)], [("PS", pab)], f"Eab{tt}.{ei}.{sb2}")
        o("V", lambda e, sb2=sb2, pab=pab: e.tensor_copy(
            asb2[:, sb2 * SUB:(sb2 + 1) * SUB], PS[pab][:]),
          [("PS", pab)], [("asb2", sb2)], f"Eae{tt}.{ei}.{sb2}")
        for dc in range(0, DC, 2):
            arep = bass.AP(tensor=asb2.ap().tensor, offset=sb2 * SUB,
                           ap=[[TE, 128], [0, 2], [1, SUB]])
            o("V", lambda e, dc=dc, ssl=ssl, sb2=sb2, arep=arep, zt=zt: e.tensor_tensor(
                out=Xt[:, dc:dc + 2, :], in0=zt[:, dc:dc + 2, ssl],
                in1=arep, op=OP.mult),
              [("zt", tb, sb2), ("asb2", sb2)], [("Xt", dc), ("Xt", dc + 1)], f"Ex{tt}.{ei}.{sb2}.{dc}")

    def e_epilogue(tt):
        tok = tt * TE
        for sb2 in range(2):
            ssl = slice(sb2 * SUB, (sb2 + 1) * SUB)
            tok2 = tok + sb2 * SUB
            for dc in range(DC):
                o("S", lambda e, dc=dc, ssl=ssl: e.activation(out=sqe[:], in_=Yacc[:, dc, ssl],
                                                              func=AF.Square, bias=zcol[:]),
                  [("Yacc", dc, sb2)], [("sqe",)], f"Efs{tt}.{sb2}.{dc}")
                o("T", lambda e, dc=dc: e.matmul(PS[0][0:1, :], ones_cb[:], sqe[:],
                                                 start=(dc == 0), stop=(dc == DC - 1)),
                  [("ones_cb",), ("sqe",)], [("PS", 0)], f"Efss{tt}.{sb2}.{dc}")
                o("S", lambda e, dc=dc, ssl=ssl: e.activation(out=ybb[:], in_=Yacc[:, dc, ssl],
                                                              func=AF.Copy),
                  [("Yacc", dc, sb2)], [("ybb",)], f"Efc{tt}.{sb2}.{dc}")
                o("T", lambda e, dc=dc: e.matmul(PS[1][0:1, :], ones_cb[:], ybb[:],
                                                 start=(dc == 0), stop=(dc == DC - 1)),
                  [("ones_cb",), ("ybb",)], [("PS", 1)], f"Efsm{tt}.{sb2}.{dc}")
            m4 = rowF("m4"); v4 = rowF("v4"); t4r = rowF("t4r"); r4 = rowF("r4"); q4 = rowF("q4")
            o("V", lambda e: e.tensor_scalar_mul(m4, PS[1][0:1, :], 1.0 / D), [("PS", 1)], [("R0",)])
            o("V", lambda e: e.tensor_scalar_mul(v4, PS[0][0:1, :], 1.0 / D), [("PS", 0)], [("R1",)])
            o("V", lambda e: e.tensor_tensor(out=t4r, in0=m4, in1=m4, op=OP.mult), [("R0",)], [("R2",)])
            o("V", lambda e: e.tensor_tensor(out=v4, in0=v4, in1=t4r, op=OP.subtract),
              [("R1",), ("R2",)], [("R1",)])
            o("S", lambda e: e.activation(out=t4r, in_=v4, func=AF.Ln, bias=epsr[:]), [("R1",)], [("R2",)])
            o("S", lambda e: e.activation(out=r4, in_=t4r, func=AF.Exp, scale=-0.5, bias=zrow[:]),
              [("R2",)], [("R3",)])
            o("V", lambda e: e.tensor_tensor(out=q4, in0=m4, in1=r4, op=OP.mult),
              [("R0",), ("R3",)], [("R4",)])
            r4b = rowB("r4b"); q4b = rowB("q4b")
            o("V", lambda e: e.tensor_copy(r4b, r4), [("R3",)], [("RB0",)])
            o("V", lambda e: e.tensor_copy(q4b, q4), [("R4",)], [("RB1",)])
            o("T", lambda e: e.matmul(PS[0][:], ones_rb[:], r4b, start=True, stop=True),
              [("ones_rb",), ("RB0",)], [("PS", 0)], f"Ea4{tt}.{sb2}")
            o("S", lambda e: e.activation(out=a4sb[:], in_=PS[0][:], func=AF.Copy),
              [("PS", 0)], [("a4sb",)], f"Ea4c{tt}.{sb2}")
            o("T", lambda e: e.matmul(PS[1][:], ones_rb[:], q4b, start=True, stop=True),
              [("ones_rb",), ("RB1",)], [("PS", 1)], f"Eb4{tt}.{sb2}")
            o("S", lambda e: e.activation(out=b4sb[:], in_=PS[1][:], func=AF.Copy),
              [("PS", 1)], [("b4sb",)], f"Eb4c{tt}.{sb2}")
            for dc in range(0, DC, 2):
                dma("QS", yep[:].rearrange("p (a n) -> p a n", a=2),
                    y_d.rearrange("dc p n -> p dc n")[:, dc:dc + 2, tok2:tok2 + SUB],
                    [("y_d", 2 * tt + sb2)], [("yep",)], tag=f"Eyl{tt}.{sb2}.{dc}")
                o("P", lambda e, dc=dc, ssl=ssl: e.tensor_tensor(
                    out=outb[:].rearrange("p (a n) -> p a n", a=2),
                    in0=Yacc[:, dc:dc + 2, ssl], in1=rep2(a4sb, SUB), op=OP.mult),
                  [("Yacc", dc, sb2), ("Yacc", dc + 1, sb2), ("a4sb",)], [("outb",)],
                  f"Eo1{tt}.{sb2}.{dc}")
                o("P", lambda e, dc=dc: e.tensor_tensor(
                    out=outb[:].rearrange("p (a n) -> p a n", a=2),
                    in0=outb[:].rearrange("p (a n) -> p a n", a=2),
                    in1=rep2(b4sb, SUB), op=OP.subtract),
                  [("outb",), ("b4sb",)], [("outb",)], f"Eo2{tt}.{sb2}.{dc}")
                o("P", lambda e, dc=dc: e.tensor_tensor(
                    out=outb[:], in0=outb[:], in1=yep[:], op=OP.add),
                  [("outb",), ("yep",)], [("outb",)], f"Eo3{tt}.{sb2}.{dc}")
                dma("QS", outTr[:, dc:dc + 2, tok2:tok2 + SUB],
                    outb[:].rearrange("p (a n) -> p a n", a=2),
                    [("outb",)],
                    [("out_d", tt, sb2, dc), ("out_d", tt, sb2, dc + 1)], tag=f"Eos{tt}.{sb2}.{dc}")

    if "E" in run_phases:

        dma("QG", oh8_sb[:], oh8, [("d_in",)], [("oh8_sb",)])

        def e_loads(tt):
            tok = tt * TE
            tb = tt % 2
            for h in range(2):
                dma("QA", zt2e[tb][:, :, h * SUB:(h + 1) * SUB],
                    z_d.rearrange("dc p n -> p dc n")[:, :, tok + h * SUB:tok + (h + 1) * SUB],
                    [("z_d", 2 * tt + h)], [("zt", tb, h)], tag=f"Ez{tt}.{h}")

        def e_aload(tt):
            tok = tt * TE
            dma("QG", arow[:], a_d[:, tok:tok + TE], [("a_d", 2 * tt), ("a_d", 2 * tt + 1)],
                [("arow",)], tag=f"Ea{tt}")

        _egpos = [0]
        e_loads(0)
        e_aload(0)
        for tt in range(NTE):
            tok = tt * TE
            if tt + 1 < NTE:
                e_loads(tt + 1)
            units = [(ei, sb2) for ei in range(9) for sb2 in range(2)]
            e_abc_xt(tt, 0, *units[0])
            for u, (ei, sb2) in enumerate(units):
                if sb2 == 0:
                    _egpos[0] += 1
                wp = _egpos[0] % 2
                if sb2 == 0:
                    dma("QG", w13[wp][:, :, 0:D], wsrc(0, ei), [("d_in",)], [("w13", wp, 0)],
                        tag=f"Ew1.{tt}.{ei}")
                    dma("QG", w13[wp][:, :, D:2 * D], wsrc(1, ei), [("d_in",)], [("w13", wp, 1)],
                        tag=f"Ew3.{tt}.{ei}")
                    dma("QG", w2b[:], wsrc(2, ei), [("d_in",)], [("w2b",)],
                        tag=f"Ew2.{tt}.{ei}")
                ssl = slice(sb2 * SUB, (sb2 + 1) * SUB)
                zt = zt2e[tt % 2]
                if ei < 8:
                    xkeys = lambda kc: ("Xt", kc)
                else:
                    xkeys = lambda kc, sb2=sb2, tt=tt: ("zt", tt % 2, sb2)
                for hc in range(DC):
                    p1 = 0 if hc % 2 == 0 else 2
                    p3 = 1 if hc % 2 == 0 else 3
                    for kc in range(DC):
                        rh = Xt[:, kc, :] if ei < 8 else zt[:, kc, ssl]
                        o("T", lambda e, kc=kc, hc=hc, p1=p1, wp=wp, rh=rh: e.matmul(
                            PS[p1][:], w13[wp][:, kc, hc * 128:(hc + 1) * 128], rh,
                            start=(kc == 0), stop=(kc == DC - 1)),
                          [("w13", wp, 0), xkeys(kc)], [("PS", p1)], f"Ep1.{tt}.{ei}.{sb2}.{hc}.{kc}")
                    for kc in range(DC):
                        rh = Xt[:, kc, :] if ei < 8 else zt[:, kc, ssl]
                        o("T", lambda e, kc=kc, hc=hc, p3=p3, wp=wp, rh=rh: e.matmul(
                            PS[p3][:], w13[wp][:, kc, D + hc * 128:D + (hc + 1) * 128], rh,
                            start=(kc == 0), stop=(kc == DC - 1)),
                          [("w13", wp, 1), xkeys(kc)], [("PS", p3)], f"Ep3.{tt}.{ei}.{sb2}.{hc}.{kc}")
                    o("S", lambda e, p1=p1: e.activation(out=s1e[:], in_=PS[p1][:], func=AF.Silu, bias=zcol[:]),
                      [("PS", p1)], [("s1e",)], f"Esl{tt}.{ei}.{sb2}.{hc}")
                    o("V", lambda e, hc=hc, p3=p3: e.tensor_tensor(
                        out=Hs[:, hc, :], in0=s1e[:], in1=PS[p3][:], op=OP.mult),
                      [("s1e",), ("PS", p3)], [("Hs", hc)], f"Eh{tt}.{ei}.{sb2}.{hc}")
                if u == 0 and tt > 0:
                    e_epilogue(tt - 1)
                if u + 1 < len(units):
                    e_abc_xt(tt, u + 1, *units[u + 1])
                for dc in range(DC):
                    py = 4 + dc % 2
                    for kc in range(DC):
                        o("T", lambda e, kc=kc, dc=dc, py=py: e.matmul(
                            PS[py][:], w2b[:, kc, dc * 128:(dc + 1) * 128], Hs[:, kc, :],
                            start=(kc == 0), stop=(kc == DC - 1)),
                          [("w2b",), ("Hs", kc)], [("PS", py)], f"Ew2m.{tt}.{ei}.{sb2}.{dc}.{kc}")
                    ysl = Yacc[:, dc, ssl]
                    if ei == 0:
                        o("V", lambda e, ysl=ysl, py=py: e.tensor_copy(ysl, PS[py][:]),
                          [("PS", py)], [("Yacc", dc, sb2)], f"Eyc{tt}.{ei}.{sb2}.{dc}")
                    else:
                        o("V", lambda e, ysl=ysl, py=py: e.tensor_tensor(
                            out=ysl, in0=ysl, in1=PS[py][:], op=OP.add),
                          [("PS", py), ("Yacc", dc, sb2)], [("Yacc", dc, sb2)],
                          f"Eya{tt}.{ei}.{sb2}.{dc}")
            if u == len(units) - 1 and tt + 1 < NTE:
                e_aload(tt + 1)


    if "E" in run_phases:
        e_epilogue(NTE - 1)

    if debug_y and "E" not in run_phases:
        tmpy = Yacc[:, :, 0:TA]
        for t in range(NT):
            tok = t * TA
            dma("QA", tmpy, y_d.rearrange("dc p n -> p dc n")[:, :, tok:tok + TA],
                [("y_d", t)], [("tmpy",)])
            dma("QS", outTr[:, :, tok:tok + TA], tmpy, [("tmpy",)], [("out_d", t)])

    em.barrier("END")

    ops = em.finalize(nc, sems)
    totals = em.final_totals

    streams = {b: [] for b in set(BODY_OF.values())}
    bodies_all = sorted(streams.keys())
    for i, op_ in enumerate(ops):
        if op_["barrier"]:
            for b in bodies_all:
                streams[b].append(("bar", op_["totals"]))
            continue
        b = BODY_OF[op_["e"]]
        streams[b].append(("op", op_))

    blk = E(nc.Block())

    def emit_body(eng, body):
        for kind, item in streams[body]:
            if kind == "bar":
                if body in ("vector", "scalar", "gpsimd"):
                    eng.drain()
                for s, v in sorted(item.items()):
                    if v > 0:
                        eng.wait_ge(sems[s], v)
                continue
            op_ = item
            for s, v in op_["waits"]:
                eng.wait_ge(sems[s], v)
            if op_.get("need_drain"):
                eng.drain()
            inst = op_["fn"](eng)
            if op_["inc"]:
                n = 16 if op_["e"] in QUEUES else 1
                inst.then_inc(sems[op_["sem"]], n)

    @blk.tensor
    def _(eng):
        emit_body(eng, "tensor")

    @blk.vector
    def _(eng):
        emit_body(eng, "vector")

    @blk.scalar
    def _(eng):
        emit_body(eng, "scalar")

    @blk.gpsimd
    def _(eng):
        emit_body(eng, "gpsimd")

    @blk.sync
    def _(eng):
        emit_body(eng, "sync")

    ctxE.close()
    ctx.close()
    return nc


def kernel(L0_attn_x, L0_attn_norm_g, L0_attn_norm_b, L0_attn_W_h, L0_attn_b_h,
           L0_attn_W_qk, L0_attn_b_qk, L0_attn_qk_gamma, L0_attn_qk_beta,
           L0_attn_rel_emb, L0_attn_W_o, L0_attn_b_o, L0_ffn_gate_w,
           L0_ffn_w1, L0_ffn_w2, L0_ffn_w3, L0_ffn_w1s, L0_ffn_w2s, L0_ffn_w3s):
    f32 = np.float32
    x = np.asarray(L0_attn_x, dtype=f32)
    shared = dict(
        Wh=np.ascontiguousarray(L0_attn_W_h, dtype=f32),
        Wqk=np.ascontiguousarray(L0_attn_W_qk, dtype=f32),
        Wo=np.ascontiguousarray(L0_attn_W_o, dtype=f32),
        bh=np.ascontiguousarray(L0_attn_b_h, dtype=f32),
        bhp=np.ascontiguousarray(np.asarray(L0_attn_b_h, dtype=f32).reshape(32, 128).T),
        bqk=np.ascontiguousarray(L0_attn_b_qk, dtype=f32),
        ngv=np.ascontiguousarray(L0_attn_norm_g, dtype=f32),
        nbv=np.ascontiguousarray(L0_attn_norm_b, dtype=f32),
        nbp=np.ascontiguousarray(np.asarray(L0_attn_norm_b, dtype=f32).reshape(8, 128).T),
        bop=np.ascontiguousarray(np.asarray(L0_attn_b_o, dtype=f32).reshape(8, 128).T),
        bo=np.ascontiguousarray(L0_attn_b_o, dtype=f32),
        qgT=np.ascontiguousarray(np.asarray(L0_attn_qk_gamma, dtype=f32).T),
        qbT=np.ascontiguousarray(np.asarray(L0_attn_qk_beta, dtype=f32).T),
        qg3=np.ascontiguousarray(np.asarray(L0_attn_qk_gamma, dtype=f32)[3:4, :]),
        qb3=np.ascontiguousarray(np.asarray(L0_attn_qk_beta, dtype=f32)[3:4, :]),
        relE=np.ascontiguousarray(L0_attn_rel_emb, dtype=f32),
        gwT=np.ascontiguousarray(np.asarray(L0_ffn_gate_w, dtype=f32).T),
        w1=np.ascontiguousarray(L0_ffn_w1, dtype=f32),
        w2=np.ascontiguousarray(L0_ffn_w2, dtype=f32),
        w3=np.ascontiguousarray(L0_ffn_w3, dtype=f32),
        w1s=np.ascontiguousarray(np.asarray(L0_ffn_w1s, dtype=f32)[0]),
        w2s=np.ascontiguousarray(np.asarray(L0_ffn_w2s, dtype=f32)[0]),
        w3s=np.ascontiguousarray(np.asarray(L0_ffn_w3s, dtype=f32)[0]),
    )
    b1 = bucket1d_np()
    ohC = np.zeros((32, 512), dtype=f32)
    ohC[b1, np.arange(511)] = 1.0
    shared["ohC"] = ohC
    oh8 = np.zeros((8, 1024), dtype=f32)
    for e in range(8):
        oh8[e, e * 128:(e + 1) * 128] = 1.0
    shared["oh8"] = oh8
    shared["antiI"] = np.ascontiguousarray(np.eye(128, dtype=f32)[::-1])
    perm8 = np.zeros((8, 24), dtype=f32)
    for si, s in enumerate((4, 2, 1)):
        for m in range(8):
            perm8[(m + s) % 8, si * 8 + m] = 1.0
    shared["perm8"] = perm8

    nc = build_nc()
    in_maps = []
    for c in range(NCORES):
        m = dict(shared)
        m["xT"] = np.ascontiguousarray(x[c].T)
        in_maps.append(m)
    res = run_bass_kernel_spmd(nc, in_maps, core_ids=list(range(NCORES)))
    out = np.stack([np.asarray(r["outT"]).T for r in res.results])
    return np.ascontiguousarray(out)


if __name__ == "__main__":
    import time
    t0 = time.time()
    nc = build_nc()
    print(f"built ok in {time.time() - t0:.1f}s")
